# version 01
# baseline (speedup 1.0000x reference)
import numpy as np
import ml_dtypes
import concourse.bass as bass
import concourse.mybir as mybir
from concourse.bass_utils import run_bass_kernel_spmd
from contextlib import ExitStack

F32 = mybir.dt.float32
BF16 = mybir.dt.bfloat16
U32 = mybir.dt.uint32
AF = mybir.ActivationFunctionType
ALU = mybir.AluOpType

T = 1024
HALO = 3
TF = T + HALO
D = 2048
KCH = 16
PROJ = 8512
DIN = 4096
NH = 64
HD = 64
CHUNK = 256
NCHUNK = 4
SBW = 512
EPS = 1e-5
HB = 16
KVH = 4
HDB = 128
ASCALE = float(1.0 / np.sqrt(128.0))

ENGS = ("t", "v", "a", "p", "y")


class Sem:
    def __init__(self, es, nc, name):
        self.h = es.enter_context(nc.semaphore(name))
        self.n = 0


class KB:
    def __init__(self, nc):
        self.nc = nc
        self.es = ExitStack()
        self.cl = {e: [] for e in ENGS}
        self._i = 0

    def sem(self, name=None):
        self._i += 1
        return Sem(self.es, self.nc, name or f"s{self._i}")

    def sb(self, name, shape, dt, stack=None):
        return (stack or self.es).enter_context(
            self.nc.sbuf_tensor(name, shape, dt))

    def emit(self, e, fn):
        self.cl[e].append(fn)

    def op(self, e, fn, inc=None, k=1):
        if inc is None:
            self.emit(e, fn)
        else:
            inc.n += k
            self.emit(e, lambda E, fn=fn, s=inc, kk=k: fn(E).then_inc(s.h, kk))

    def wait(self, e, sem, at=None):
        v = sem.n if at is None else at
        if v > 0:
            self.emit(e, lambda E, h=sem.h, v=v: E.wait_ge(h, v))

    def mile(self, e, sem, k=1):
        sem.n += k
        self.emit(e, lambda E, h=sem.h, k=k: E.drain().then_inc(h, k))

    def dma(self, e, out, in_, sem, **kw):
        sem.n += 16
        self.emit(e, lambda E, o=out, i=in_, h=sem.h, kw=kw:
                  E.dma_start(out=o, in_=i, **kw).then_inc(h, 16))


class Banks:
    def __init__(self, kb):
        self.kb = kb
        self.b = [kb.es.enter_context(
            kb.nc.psum_tensor(f"bank{i}", [128, 512], F32)) for i in range(8)]
        self.tag = [None] * 8

    def acq(self, i, weng="t"):
        if self.tag[i] is not None:
            s, v = self.tag[i]
            self.kb.wait(weng, s, v)
            self.tag[i] = None
        return self.b[i]

    def rel(self, i, sem, at=None):
        self.tag[i] = (sem, sem.n if at is None else at)


def build(build_upto=3, taps=()):
    nc = bass.Bass("TRN2", target_bir_lowering=False)
    kb = KB(nc)
    tap_req = set(taps)
    tap_names = []

    din = {}

    def dram_in(name, shape, dt=F32):
        din[name] = nc.dram_tensor(name, shape, dt, kind="ExternalInput").ap()
        return din[name]

    x_in = dram_in("x_fm", [D, TF])
    WinT = dram_in("WinT", [D, PROJ])
    dram_in("convw", [128, 34 * 4])
    dram_in("convb", [128, 34])
    dram_in("dtbias", [64, 1])
    dram_in("A_log", [64, 1])
    dram_in("D_bc", [128, 32])
    dram_in("normw", [128, 32])
    WoaT = dram_in("WoaT", [DIN, D])
    W1T = dram_in("W1T", [D, DIN])
    W2T = dram_in("W2T", [DIN, DIN])
    W3T = dram_in("W3T", [DIN, D])
    dram_in("b1c", [128, 32])
    dram_in("b2c", [128, 32])
    dram_in("b3c", [128, 16])
    WqT = dram_in("WqT", [D, D])
    WkvT = dram_in("WkvT", [D, 1024])
    WobT = dram_in("WobT", [D, D])
    dram_in("boc", [128, 16])
    dram_in("slopes_n", [128, HB])
    dram_in("flags", [1, 8], U32)
    dram_in("halfsc", [128, 1])

    out_fm = nc.dram_tensor("out_fm", [D, T], F32, kind="ExternalOutput").ap()
    ao_scr = nc.dram_tensor("ao_scr", [HB * HDB, T], BF16,
                            kind="Internal").ap()
    xbc_scr = nc.dram_tensor("xbc_scr", [34 * 128, TF], BF16,
                             kind="Internal").ap()
    silz_scr = nc.dram_tensor("silz_scr", [32 * 128, T], BF16,
                              kind="Internal").ap()
    h1_scr = nc.dram_tensor("h1_scr", [32 * 128, T], BF16,
                            kind="Internal").ap()
    sh_state = nc.dram_tensor("sh_state", [2 * 128, NH * HD], F32,
                              kind="Internal", addr_space="Shared").ap()
    sh_kv = nc.dram_tensor("sh_kv", [2 * 1024, T], BF16,
                           kind="Internal", addr_space="Shared").ap()
    sh_flag = nc.dram_tensor("sh_flag", [4, 16], U32,
                             kind="Internal", addr_space="Shared").ap()

    xbc_v = xbc_scr.rearrange("(m p) t -> p m t", p=128)
    silz_v = silz_scr.rearrange("(m p) t -> p m t", p=128)
    h1_v = h1_scr.rearrange("(m p) t -> p m t", p=128)
    ao_v = ao_scr.rearrange("(m p) t -> p m t", p=128)

    x_fm = kb.sb("x_fm_sb", [128, KCH, TF], F32)
    ident = kb.sb("ident", [128, 128], BF16)
    identf = kb.sb("identf", [128, 128], F32)
    ones_col = kb.sb("ones_col", [128, 1], BF16)
    ones_row = kb.sb("ones_row", [1, 128], BF16)
    convw_sb = kb.sb("convw_sb", [128, 34 * 4], F32)
    convb_sb = kb.sb("convb_sb", [128, 34], F32)
    dtbias_sb = kb.sb("dtbias_sb", [64, 1], F32)
    Acol = kb.sb("Acol", [64, 1], F32)
    Dbc_sb = kb.sb("Dbc_sb", [128, 32], F32)
    normw_sb = kb.sb("normw_sb", [128, 32], F32)
    b1_sb = kb.sb("b1_sb", [128, 32], F32)
    b2_sb = kb.sb("b2_sb", [128, 32], F32)
    b3_sb = kb.sb("b3_sb", [128, 16], F32)
    bo_sb = kb.sb("bo_sb", [128, 16], F32)
    slopes_sb = kb.sb("slopes_sb", [128, HB], F32)
    halfsc_sb = kb.sb("halfsc_sb", [128, 1], F32)
    flags_sb = kb.sb("flags_sb", [1, 8], U32)
    zeros64 = kb.sb("zeros64", [64, CHUNK], F32)
    eps_c = kb.sb("eps_c", [128, 1], F32)
    nc.const_aps.register(F32, EPS, eps_c[:, :])
    nonce_sb = kb.sb("nonce_sb", [1, 16], U32)
    flagrx = kb.sb("flagrx", [1, 16], U32)
    mean_bfr = kb.sb("mean_bfr", [1, SBW], BF16)
    rstd_row = kb.sb("rstd_row", [1, SBW], F32)
    rstd_bfr = kb.sb("rstd_bfr", [1, SBW], BF16)
    rstd_bc = kb.sb("rstd_bc", [128, SBW], F32)
    row_a = kb.sb("row_a", [1, SBW], F32)
    row_b = kb.sb("row_b", [1, SBW], F32)

    B = Banks(kb)

    sLD = kb.sem("sLD")
    sW = kb.sem("sW")
    sT = kb.sem("sT")
    sV = kb.sem("sV")
    sA = kb.sem("sA")
    sP = kb.sem("sP")
    sST = kb.sem("sST")
    sEX = kb.sem("sEX")
    sXC = kb.sem("sXC")
    sPL = kb.sem("sPL")
    sPL2 = kb.sem("sPL2")
    sPL3 = kb.sem("sPL3")

    for name, dst in [("convw", convw_sb), ("convb", convb_sb),
                      ("dtbias", dtbias_sb), ("A_log", Acol),
                      ("D_bc", Dbc_sb), ("normw", normw_sb),
                      ("b1c", b1_sb), ("b2c", b2_sb), ("b3c", b3_sb),
                      ("boc", bo_sb), ("slopes_n", slopes_sb),
                      ("halfsc", halfsc_sb), ("flags", flags_sb)]:
        kb.dma("y", dst.ap(), din[name], sLD)
    kb.dma("y", x_fm.ap(), x_in.rearrange("(k p) t -> p k t", p=128), sLD)

    kb.op("p", lambda E: E.memset(ident[:, :], 1.0))
    kb.op("p", lambda E: E.memset(ones_col[:, :], 1.0))
    kb.op("p", lambda E: E.memset(ones_row[:, :], 1.0))
    kb.op("p", lambda E: E.memset(zeros64[:, :], 0.0))
    kb.op("p", lambda E: E.memset(eps_c[:, :], EPS))
    kb.mile("p", sP)
    kb.op("p", lambda E: E.affine_select(ident[:, :], ident[:, :], [[1, 128]],
                                         ALU.is_equal, 0.0, base=0,
                                         channel_multiplier=-1))
    kb.mile("p", sP)
    kb.op("p", lambda E: E.tensor_copy(identf[:, :], ident[:, :]))
    kb.wait("p", sLD)
    kb.op("p", lambda E: E.tensor_copy(
        nonce_sb[:1, :], flags_sb[0:1, 1:2].broadcast_to([1, 16])))
    kb.mile("p", sP)
    kb.wait("a", sLD)
    kb.op("a", lambda E: E.activation(Acol[:, :], Acol[:, :], AF.Exp))
    kb.mile("a", sA)
    kb.wait("v", sA)
    kb.op("v", lambda E: E.tensor_scalar(Acol[:, :], Acol[:, :], -1.0, None,
                                         ALU.mult))
    kb.mile("v", sV)

    _st = (sLD.n, sP.n, sV.n, sA.n)
    for e in ENGS:
        kb.wait(e, sLD, _st[0])
        kb.wait(e, sP, _st[1])
        kb.wait(e, sV, _st[2])
        kb.wait(e, sA, _st[3])

    def tap(name, ap_factory, shape, dt, waits):
        if name not in tap_req:
            return
        th = nc.dram_tensor("tap_" + name, list(shape), dt,
                            kind="ExternalOutput").ap()
        tap_names.append("tap_" + name)
        for (sm, v) in waits:
            kb.wait("y", sm, v)
        kb.dma("y", th, ap_factory(), sST)

    war = {}

    def war_wait(e, buf):
        t = war.get(id(buf))
        if t is not None:
            kb.wait(e, t[0], t[1])

    def war_set(buf, sem, at=None):
        war[id(buf)] = (sem, sem.n if at is None else at)

    def emit_ln(c0, n, xcbuf, rbase=0):
        kb.wait("p", sV, sV.n)
        for k in range(KCH):
            kb.op("p", lambda E, k=k, n=n, c0=c0: E.tensor_copy(
                xcbuf[:, k, :n], x_fm[:, k, c0:c0 + n]))
        kb.mile("p", sP)
        ps_s = B.acq(3)
        kb.wait("t", sP)
        for k in range(KCH):
            kb.op("t", lambda E, k=k, ps=ps_s, n=n: E.matmul(
                ps[0:1, :n], ones_col[:, :], xcbuf[:, k, :n],
                start=(k == 0), stop=(k == KCH - 1)))
        kb.mile("t", sT)
        kb.wait("a", sT)
        for k in range(KCH):
            kb.op("a", lambda E, k=k, n=n: E.activation(
                xcbuf[:, k, :n], xcbuf[:, k, :n], AF.Square))
        kb.mile("a", sA)
        ps_q = B.acq(4)
        kb.wait("t", sA)
        for k in range(KCH):
            kb.op("t", lambda E, k=k, ps=ps_q, n=n: E.matmul(
                ps[0:1, :n], ones_col[:, :], xcbuf[:, k, :n],
                start=(k == 0), stop=(k == KCH - 1)))
        kb.mile("t", sT)
        r0 = rbase
        kb.wait("a", sT)
        kb.op("a", lambda E, ps=ps_s, n=n, r0=r0: E.activation(
            row_a[:, r0:r0 + n], ps[0:1, :n], AF.Copy, scale=1.0 / D))
        kb.op("a", lambda E, ps=ps_s, n=n, r0=r0: E.activation(
            mean_bfr[:, r0:r0 + n], ps[0:1, :n], AF.Copy, scale=1.0 / D))
        kb.op("a", lambda E, ps=ps_q, n=n, r0=r0: E.activation(
            row_b[:, r0:r0 + n], ps[0:1, :n], AF.Copy, scale=1.0 / D))
        kb.mile("a", sA)
        B.rel(3, sA)
        B.rel(4, sA)
        kb.wait("a", sA)
        kb.op("a", lambda E, n=n, r0=r0: E.activation(
            row_a[:, r0:r0 + n], row_a[:, r0:r0 + n], AF.Square))
        kb.mile("a", sA)
        kb.wait("v", sA)
        kb.op("v", lambda E, n=n, r0=r0: E.tensor_sub(
            row_b[:, r0:r0 + n], row_b[:, r0:r0 + n], row_a[:, r0:r0 + n]))
        kb.mile("v", sV)
        kb.wait("a", sV)
        kb.op("a", lambda E, n=n, r0=r0: E.activation(
            row_b[:, r0:r0 + n], row_b[:, r0:r0 + n], AF.Sqrt, bias=EPS))
        kb.mile("a", sA)
        kb.wait("v", sA)
        kb.op("v", lambda E, n=n, r0=r0: E.reciprocal(
            rstd_row[:, r0:r0 + n], row_b[:, r0:r0 + n]))
        kb.mile("v", sV)
        kb.wait("v", sV)
        kb.op("v", lambda E, n=n, r0=r0: E.tensor_copy(
            rstd_bfr[:, r0:r0 + n], rstd_row[:, r0:r0 + n]))
        kb.mile("v", sV)
        pb = B.acq(2)
        kb.wait("t", sV)
        kb.op("t", lambda E, pb=pb, n=n, r0=r0: E.matmul(
            pb[:, :n], ones_row[:1, :], rstd_bfr[:1, r0:r0 + n],
            start=True, stop=True))
        kb.mile("t", sT)
        kb.wait("v", sT)
        kb.op("v", lambda E, pb=pb, n=n, r0=r0: E.tensor_copy(
            rstd_bc[:, r0:r0 + n], pb[:, :n]))
        kb.mile("v", sV)
        B.rel(2, sV)

    def emit_xc(c0, n, xcbuf, rbase=0):
        pb = B.acq(2)
        kb.wait("t", sV)
        kb.op("t", lambda E, pb=pb, n=n, rbase=rbase: E.matmul(
            pb[:, :n], ones_row[:1, :], mean_bfr[:1, rbase:rbase + n],
            start=True, stop=True))
        kb.mile("t", sT)
        kb.wait("v", sT)
        for k in range(KCH):
            kb.op("v", lambda E, k=k, pb=pb, n=n, c0=c0: E.tensor_sub(
                xcbuf[:, k, :n], x_fm[:, k, c0:c0 + n], pb[:, :n]))
        kb.mile("v", sV)
        B.rel(2, sV)

    wfree = {}

    def wload(WT_ap, m0, mw, wbuf, par, krange=None):
        t = wfree.get(id(wbuf[par]))
        if t is not None:
            kb.wait("p", t[0], t[1])
        src = WT_ap.rearrange("(k p) m -> p k m", p=128)
        if krange is None:
            kcnt = src.shape[1]
            src = src[:, :, m0:m0 + mw]
        else:
            kcnt = krange[1] - krange[0]
            src = src[:, krange[0]:krange[1], m0:m0 + mw]
        kb.dma("p", wbuf[par][:, :kcnt, :mw], src, sW)

    def wdone(wbuf, par):
        wfree[id(wbuf[par])] = (sT, sT.n)

    def emit_attn_a():
        stk = ExitStack()
        state = kb.sb("state", [128, NH * HD], F32, stack=stk)
        state_bf = kb.sb("state_bf", [128, NH * HD], BF16, stack=stk)
        dt_pre = kb.sb("dt_pre", [64, T], F32, stack=stk)
        expT = kb.sb("expT", [128, NCHUNK, 2, 64], F32, stack=stk)
        endtT = kb.sb("endtT", [128, NCHUNK, 2, 64], F32, stack=stk)
        eLbc = kb.sb("eLbc", [128, NCHUNK, 64], F32, stack=stk)
        acsLbc = kb.sb("acsLbc", [128, NCHUNK, 64], F32, stack=stk)
        kb.op("p", lambda E: E.memset(state[:, :], 0.0))
        kb.op("p", lambda E: E.memset(state_bf[:, :], 0.0))
        kb.mile("p", sP)
        wbuf = [kb.sb(f"wa{i}", [128, KCH, 128], BF16, stack=stk)
                for i in range(2)]
        stk1 = ExitStack()
        xcbuf = kb.sb("xcbuf", [128, KCH, SBW], BF16, stack=stk1)
        strA = [kb.sb(f"strA{i}", [128, SBW], BF16, stack=stk1)
                for i in range(2)]
        zt1 = kb.sb("zt1", [128, SBW], F32, stack=stk1)
        xbchalo = kb.sb("xbchalo", [128, 34, HALO], BF16, stack=stk1)

        def evac_zpath(m, ps, bi, c0, n):
            t0 = c0 - HALO
            par = m % 2
            kb.wait("v", sT)
            war_wait("v", zt1)
            kb.op("v", lambda E, ps=ps, n=n: E.tensor_mul(
                zt1[:, :n], ps[:128, :n], rstd_bc[:, :n]))
            kb.mile("v", sV)
            B.rel(bi, sV)
            kb.wait("a", sV)
            war_set(zt1, sA, sA.n + 1)
            war_wait("a", strA[par])
            kb.op("a", lambda E, par=par, n=n: E.activation(
                strA[par][:, :n], zt1[:, :n], AF.Silu))
            kb.mile("a", sA)
            kb.wait("y", sA)
            kb.dma("y", silz_v[:, m, t0:t0 + n], strA[par][:, :n], sST)
            war_set(strA[par], sST)

        def evac_xbc(m, ps, bi, c0, n):
            par = m % 2
            kb.wait("v", sT)
            war_wait("v", strA[par])
            kb.op("v", lambda E, ps=ps, n=n, par=par: E.tensor_mul(
                strA[par][:, :n], ps[:128, :n], rstd_bc[:, :n]))
            kb.mile("v", sV)
            B.rel(bi, sV)
            kb.wait("y", sV)
            kb.dma("y", xbc_v[:, m - 32, c0:c0 + n], strA[par][:, :n], sST)
            war_set(strA[par], sST)

        def evac_xbc_halo(m, ps, bi):
            kb.wait("v", sT)
            kb.op("v", lambda E, ps=ps, m=m: E.tensor_mul(
                xbchalo[:, m - 32, :], ps[:128, :HALO], rstd_bc[:, :HALO]))
            kb.mile("v", sV)
            B.rel(bi, sV)

        def evac_dt(m, ps, bi, c0, n):
            t0 = c0 - HALO
            kb.wait("v", sT)
            kb.op("v", lambda E, ps=ps, n=n, t0=t0: E.tensor_mul(
                dt_pre[:, t0:t0 + n], ps[0:64, :n], rstd_bc[0:64, :n]))
            kb.mile("v", sV)
            B.rel(bi, sV)

        def passA(c0, n, mlist, outfn):
            kb.wait("t", sV)
            w0 = sW.n
            par = 0
            wload(WinT, mlist[0] * 128, 64 if mlist[0] == 66 else 128,
                  wbuf, par)
            for mi, m in enumerate(mlist):
                if mi + 1 < len(mlist):
                    nm = mlist[mi + 1]
                    wload(WinT, nm * 128, 64 if nm == 66 else 128,
                          wbuf, 1 - par)
                mw = 64 if m == 66 else 128
                bi = mi % 2
                ps = B.acq(bi)
                kb.wait("t", sW, w0 + (mi + 1) * 16)
                for k in range(KCH):
                    kb.op("t", lambda E, k=k, ps=ps, par=par, mw=mw, n=n:
                          E.matmul(ps[:mw, :n], wbuf[par][:, k, :mw],
                                   xcbuf[:, k, :n],
                                   start=(k == 0), stop=(k == KCH - 1)))
                kb.mile("t", sT)
                wdone(wbuf, par)
                outfn(m, ps, bi)
                par = 1 - par

        emit_ln(0, HALO, xcbuf)
        emit_xc(0, HALO, xcbuf)
        passA(0, HALO, list(range(32, 66)),
              lambda m, ps, bi: evac_xbc_halo(m, ps, bi))
        kb.wait("y", sV)
        kb.dma("y", xbc_v[:, :, 0:HALO], xbchalo.ap(), sST)

        for sbi in range(2):
            c0 = HALO + sbi * SBW
            emit_ln(c0, SBW, xcbuf)
            emit_xc(c0, SBW, xcbuf)
            passA(c0, SBW, list(range(0, 67)),
                  lambda m, ps, bi, c0=c0: (
                      evac_zpath(m, ps, bi, c0, SBW) if m < 32 else
                      (evac_dt(m, ps, bi, c0, SBW) if m == 66 else
                       evac_xbc(m, ps, bi, c0, SBW))))

        tap("dtpre", lambda: dt_pre.ap(), [64, T], F32, [(sV, sV.n)])
        tap("xbcscr", lambda: xbc_scr, [34 * 128, TF], BF16,
            [(sST, sST.n)])
        tap("silzscr", lambda: silz_scr, [32 * 128, T], BF16,
            [(sST, sST.n)])
        for _e in ("t", "v", "a", "p"):
            kb.mile(_e, sXC)
        for _e in ("t", "v", "a", "p", "y"):
            kb.wait(_e, sXC)
            kb.wait(_e, sST)
        stk1.close()
        stk2 = ExitStack()
        y32w = kb.sb("y32w", [128, 32, CHUNK], BF16, stack=stk2)
        co = kb.sb("co", [128, 34, CHUNK], BF16, stack=stk2)
        xbcin = [kb.sb(f"xbcin{i}", [128, 10, CHUNK + HALO], BF16, stack=stk2)
                 for i in range(2)]
        zstr = [kb.sb(f"zstr{i}", [128, CHUNK], BF16, stack=stk2)
                for i in range(2)]
        cvA = kb.sb("cvA", [128, 5, CHUNK], F32, stack=stk2)
        cvB = kb.sb("cvB", [128, 5, CHUNK], F32, stack=stk2)
        Ssb = kb.sb("Ssb", [128, 2, CHUNK], BF16, stack=stk2)
        SmaskT = kb.sb("SmaskT", [128, 2, CHUNK], BF16, stack=stk2)
        btok = kb.sb("btok", [128, 2, 128], BF16, stack=stk2)
        xT = kb.sb("xT", [128, 2, 128], BF16, stack=stk2)
        xd = kb.sb("xd", [128, 2, 128], BF16, stack=stk2)
        u_bf = kb.sb("u_bf", [128, 2, 128], BF16, stack=stk2)
        dt_c = kb.sb("dt_c", [64, CHUNK], F32, stack=stk2)
        dta = kb.sb("dta", [64, CHUNK], F32, stack=stk2)
        dtb = kb.sb("dtb", [64, CHUNK], F32, stack=stk2)
        dtv = kb.sb("dtv", [64, CHUNK], F32, stack=stk2)
        acs = kb.sb("acs", [64, CHUNK], F32, stack=stk2)
        acsT = kb.sb("acsT", [128, 2, 64], F32, stack=stk2)
        dtvT = kb.sb("dtvT", [128, 2, 64], F32, stack=stk2)
        negT = kb.sb("negT", [128, 2, 64], F32, stack=stk2)
        eLrow = kb.sb("eLrow", [1, 64], F32, stack=stk2)
        arow = kb.sb("arow", [1, CHUNK], F32, stack=stk2)
        arow_hi = kb.sb("arow_hi", [1, CHUNK], F32, stack=stk2)
        Larg = kb.sb("Larg", [128, CHUNK], F32, stack=stk2)
        SL = kb.sb("SL", [128, 2, CHUNK], BF16, stack=stk2)
        uw = kb.sb("uw", [128, 2, 128], BF16, stack=stk2)
        xdY = kb.sb("xdY", [128, 2, 128], BF16, stack=stk2)
        onesf_row = kb.sb("onesf_row", [1, 128], F32, stack=stk2)
        xD = kb.sb("xD", [128, CHUNK], F32, stack=stk2)
        yt = kb.sb("yt", [128, CHUNK], F32, stack=stk2)
        y2s = kb.sb("y2s", [128, CHUNK], BF16, stack=stk2)
        stw = kb.sb("stw", [128, 128], F32, stack=stk2)
        rst2_bfr = kb.sb("rst2_bfr", [1, CHUNK], BF16, stack=stk2)
        rst2_bc = kb.sb("rst2_bc", [128, CHUNK], F32, stack=stk2)
        kb.wait("p", sV, sV.n)
        kb.wait("p", sA, sA.n)

        def rx_state(E):
            with E.register("mh") as mh, E.register("fv") as fv, \
                 E.register("nr") as nr, E.register("pc") as pc, \
                 E.register("nv") as nv:
                E.reg_load(mh, flags_sb[0:1, 0:1])
                E.reg_load(nv, flags_sb[0:1, 1:2])
                with E.If_eq(mh, 1):
                    E.reg_mov(pc, 0)

                    def poll():
                        E.dma_start(out=flagrx[:1, :],
                                    in_=sh_flag[0:1, :]).then_inc(sPL.h, 16)
                        E.reg_add(pc, pc, 16)
                        E.wait_ge(sPL.h, pc)
                        E.reg_load(fv, flagrx[0:1, 0:1])
                        E.reg_sub(nr, fv, nv)

                    poll()
                    with E.While(nr):
                        poll()
                    E.dma_start(out=state[:, :],
                                in_=sh_state[0:128, :]).then_inc(sPL.h, 16)
                    E.reg_add(pc, pc, 16)
                    E.wait_ge(sPL.h, pc)
            E.tensor_copy(state_bf[:, :], state[:, :])
            return E.drain()

        kb.op("p", rx_state, inc=sEX)
        for e in ("t", "v"):
            kb.wait(e, sEX)

        wbufL = wbuf
        for c in range(NCHUNK):
            tok0 = c * CHUNK
            kb.wait("a", sT, sT.n)
            kb.wait("a", sV, sV.n)
            kb.wait("v", sT, sT.n)
            kb.wait("p", sT, sT.n)
            kb.wait("p", sV, sV.n)
            kb.wait("p", sA, sA.n)
            groups = [(0, 10), (10, 10), (20, 10), (30, 4)]

            def issue_group(gi):
                g0, gn = groups[gi]
                par = gi % 2
                war_wait("y", xbcin[par])
                kb.dma("y", xbcin[par][:, :gn, :],
                       xbc_v[:, g0:g0 + gn, tok0:tok0 + CHUNK + HALO], sLD)
                return (g0, gn, par, sLD.n)

            gq = [issue_group(0), issue_group(1)]
            for gi in range(len(groups)):
                (g0, gn, par, ldv) = gq[gi]
                if gi + 2 < len(groups):
                    pass
                kb.wait("p", sLD, ldv)
                for sub in range(0, gn, 5):
                    sn = min(5, gn - sub)
                    kb.wait("p", sA, sA.n)

                    def wbc(col):
                        return convw_sb[:, col:col + 1].broadcast_to(
                            [128, CHUNK])

                    for j in range(sn):
                        kb.op("p", lambda E, par=par, sub=sub, j=j, g0=g0:
                              E.tensor_mul(
                                  cvA[:, j, :],
                                  xbcin[par][:, sub + j, 0:CHUNK],
                                  wbc((g0 + sub + j) * 4)))
                    kb.mile("p", sP)
                    for i in range(1, 4):
                        for j in range(sn):
                            kb.op("p", lambda E, par=par, sub=sub, j=j, g0=g0,
                                  i=i: E.tensor_mul(
                                      cvB[:, j, :],
                                      xbcin[par][:, sub + j, i:i + CHUNK],
                                      wbc((g0 + sub + j) * 4 + i)))
                        kb.mile("p", sP)
                        for j in range(sn):
                            kb.op("p", lambda E, j=j: E.tensor_add(
                                cvA[:, j, :], cvA[:, j, :], cvB[:, j, :]))
                        kb.mile("p", sP)
                    kb.wait("a", sP)
                    for j in range(sn):
                        kb.op("a", lambda E, g0=g0, sub=sub, j=j: E.activation(
                            co[:, g0 + sub + j, :], cvA[:, j, :], AF.Silu,
                            bias=convb_sb[:, g0 + sub + j:g0 + sub + j + 1]))
                    kb.mile("a", sA)
                war_set(xbcin[par], sP)
                if gi + 2 < len(groups):
                    gq.append(issue_group(gi + 2))
            co_ready = sA.n

            kb.wait("a", sV, sV.n)
            kb.op("a", lambda E, tok0=tok0: E.activation(
                dta[:, :], dt_pre[0:64, tok0:tok0 + CHUNK], AF.Abs,
                bias=dtbias_sb[:, 0:1]))
            kb.op("a", lambda E, tok0=tok0: E.activation(
                dt_c[:, :], dt_pre[0:64, tok0:tok0 + CHUNK], AF.Identity,
                bias=dtbias_sb[:, 0:1]))
            kb.mile("a", sA)
            kb.wait("a", sA)
            kb.op("a", lambda E: E.activation(dtb[:, :], dta[:, :], AF.Exp,
                                              scale=-1.0))
            kb.mile("a", sA)
            kb.wait("a", sA)
            kb.op("a", lambda E: E.activation(dtb[:, :], dtb[:, :], AF.Ln,
                                              bias=1.0))
            kb.mile("a", sA)
            kb.wait("v", sA)
            kb.op("v", lambda E: E.tensor_scalar(dta[:, :], dt_c[:, :], 0.0,
                                                 None, ALU.max))
            kb.mile("v", sV)
            kb.wait("v", sV)
            kb.op("v", lambda E: E.tensor_add(dtv[:, :], dta[:, :], dtb[:, :]))
            kb.mile("v", sV)
            kb.wait("v", sV)
            kb.op("v", lambda E: E.tensor_scalar(dta[:, :], dtv[:, :],
                                                 Acol[:, 0:1], None, ALU.mult))
            kb.mile("v", sV)
            kb.wait("v", sV)
            kb.op("v", lambda E: E.tensor_tensor_scan(
                acs[:, :], dta[:, :], zeros64[:, :], 0.0, ALU.add, ALU.add))
            kb.mile("v", sV)
            pm = B.acq(5)
            kb.wait("t", sV)
            for lt in range(2):
                kb.op("t", lambda E, lt=lt, pm=pm: E.transpose(
                    pm[:, lt * 64:lt * 64 + 64],
                    acs[0:64, lt * 128:lt * 128 + 128], identf[0:64, 0:64]))
                kb.op("t", lambda E, lt=lt, pm=pm: E.transpose(
                    pm[:, 128 + lt * 64:128 + lt * 64 + 64],
                    dtv[0:64, lt * 128:lt * 128 + 128], identf[0:64, 0:64]))
            kb.mile("t", sT)
            kb.wait("v", sT)
            for lt in range(2):
                kb.op("v", lambda E, lt=lt, pm=pm: E.tensor_copy(
                    acsT[:, lt, :], pm[:, lt * 64:lt * 64 + 64]))
                kb.op("v", lambda E, lt=lt, pm=pm: E.tensor_copy(
                    dtvT[:, lt, :], pm[:, 128 + lt * 64:128 + lt * 64 + 64]))
            kb.mile("v", sV)
            B.rel(5, sV)
            kb.wait("a", sV)
            for lt in range(2):
                kb.op("a", lambda E, lt=lt, c=c: E.activation(
                    expT[:, c, lt, :], acsT[:, lt, :], AF.Exp))
            kb.mile("a", sA)
            pm = B.acq(5)
            kb.wait("t", sV)
            kb.op("t", lambda E, pm=pm: E.transpose(
                pm[0:1, 0:64], acs[0:64, 255:256], identf[0:64, 0:64]))
            kb.mile("t", sT)
            kb.wait("a", sT)
            kb.op("a", lambda E, pm=pm: E.activation(
                eLrow[:1, :], pm[0:1, 0:64], AF.Copy))
            kb.mile("a", sA)
            kb.wait("p", sA)
            kb.op("p", lambda E: E.memset(onesf_row[:1, :], 1.0))
            kb.mile("p", sP)
            kb.wait("t", sA)
            kb.wait("t", sP)
            kb.op("t", lambda E, pm=pm: E.matmul(
                pm[:, 64:128], onesf_row[:1, :], eLrow[:1, :],
                start=True, stop=True))
            kb.mile("t", sT)
            kb.wait("v", sT)
            kb.op("v", lambda E, pm=pm, c=c: E.tensor_copy(
                acsLbc[:, c, :], pm[:, 64:128]))
            kb.mile("v", sV)
            B.rel(5, sV)
            kb.wait("a", sV)
            kb.op("a", lambda E, c=c: E.activation(
                eLbc[:, c, :], acsLbc[:, c, :], AF.Exp))
            kb.mile("a", sA)
            kb.wait("v", sA)
            for lt in range(2):
                kb.op("v", lambda E, lt=lt, c=c: E.tensor_sub(
                    negT[:, lt, :], acsLbc[:, c, :], acsT[:, lt, :]))
            kb.mile("v", sV)
            kb.wait("a", sV)
            for lt in range(2):
                kb.op("a", lambda E, lt=lt: E.activation(
                    negT[:, lt, :], negT[:, lt, :], AF.Exp))
            kb.mile("a", sA)
            kb.wait("v", sA)
            for lt in range(2):
                kb.op("v", lambda E, lt=lt, c=c: E.tensor_mul(
                    endtT[:, c, lt, :], negT[:, lt, :], dtvT[:, lt, :]))
            kb.mile("v", sV)

            ps_s = B.acq(5)
            kb.wait("t", sA, co_ready)
            for lt in range(2):
                kb.op("t", lambda E, lt=lt, ps=ps_s: E.matmul(
                    ps[:, lt * 256:lt * 256 + 256],
                    co[:, 33, lt * 128:lt * 128 + 128], co[:, 32, :],
                    start=True, stop=True))
            kb.mile("t", sT)
            kb.wait("a", sT)
            for lt in range(2):
                kb.op("a", lambda E, lt=lt, ps=ps_s: E.activation(
                    Ssb[:, lt, :], ps[:, lt * 256:lt * 256 + 256], AF.Copy))
            kb.mile("a", sA)
            B.rel(5, sA)
            ps_t = B.acq(5)
            pt_bf = ps_t.ap().bitcast(BF16)
            kb.wait("t", sA)
            for st in range(2):
                for lt in range(2):
                    kb.op("t", lambda E, st=st, lt=lt, pt=pt_bf: E.transpose(
                        pt[:, st * 512 + lt * 128:st * 512 + lt * 128 + 128],
                        Ssb[:, lt, st * 128:st * 128 + 128], ident[:, :]))
            kb.mile("t", sT)
            kb.wait("v", sT)
            kb.wait("v", sP, sP.n)
            for st in range(2):
                kb.op("v", lambda E, st=st, pt=pt_bf: E.tensor_copy(
                    SmaskT[:, st, :], pt[:, st * 512:st * 512 + 256]))
            kb.mile("v", sV)
            B.rel(5, sV)
            kb.wait("p", sV)
            for st in range(2):
                kb.op("p", lambda E, st=st: E.affine_select(
                    SmaskT[:, st, :], SmaskT[:, st, :], [[1, CHUNK]],
                    ALU.is_ge, 0.0, base=-st * 128, channel_multiplier=-1))
            kb.mile("p", sP)
            ps_b = B.acq(5)
            pb_bf = ps_b.ap().bitcast(BF16)
            kb.wait("t", sP)
            for lt in range(2):
                kb.op("t", lambda E, lt=lt, pb=pb_bf: E.transpose(
                    pb[:, lt * 128:lt * 128 + 128],
                    co[:, 32, lt * 128:lt * 128 + 128], ident[:, :]))
            kb.mile("t", sT)
            kb.wait("v", sT)
            for lt in range(2):
                kb.op("v", lambda E, lt=lt, pb=pb_bf: E.tensor_copy(
                    btok[:, lt, :], pb[:, lt * 128:lt * 128 + 128]))
            kb.mile("v", sV)
            B.rel(5, sV)

            tap(f"co{c}", lambda: co.ap(), [128, 34, CHUNK], BF16,
                [(sA, co_ready)])
            tap(f"acs{c}", lambda: acs.ap(), [64, CHUNK], F32, [(sV, sV.n)])

            kb.dma("y", zstr[0][:, :], silz_v[:, 0, tok0:tok0 + CHUNK], sLD)
            zq = [sLD.n]

            for hp in range(32):
                h0 = 2 * hp
                if hp + 1 < 32:
                    par = (hp + 1) % 2
                    war_wait("y", zstr[par])
                    kb.dma("y", zstr[par][:, :],
                           silz_v[:, hp + 1, tok0:tok0 + CHUNK], sLD)
                    zq.append(sLD.n)
                ps_x = B.acq(6)
                px_bf = ps_x.ap().bitcast(BF16)
                kb.wait("t", sA, co_ready)
                for lt in range(2):
                    kb.op("t", lambda E, lt=lt, hp=hp, px=px_bf: E.transpose(
                        px[:, lt * 128:lt * 128 + 128],
                        co[:, hp, lt * 128:lt * 128 + 128], ident[:, :]))
                kb.mile("t", sT)
                kb.wait("v", sT)
                kb.wait("v", sP, sP.n)
                for lt in range(2):
                    kb.op("v", lambda E, lt=lt, px=px_bf: E.tensor_copy(
                        xT[:, lt, :], px[:, lt * 128:lt * 128 + 128]))
                kb.mile("v", sV)
                B.rel(6, sV)
                kb.wait("p", sV)
                kb.wait("p", sT, sT.n)
                for lt in range(2):
                    kb.op("p", lambda E, lt=lt, hp=hp, c=c, h0=h0:
                          E.tensor_tensor(
                              xdY[:, lt, :].rearrange("p (h q) -> p h q", h=2),
                              xT[:, lt, :].rearrange("p (h q) -> p h q", h=2),
                              dtvT[:, lt, h0:h0 + 2].unsqueeze(2)
                              .broadcast_to([128, 2, 64]),
                              ALU.mult))
                    kb.op("p", lambda E, lt=lt, hp=hp, c=c, h0=h0:
                          E.tensor_tensor(
                              xd[:, lt, :].rearrange("p (h q) -> p h q", h=2),
                              xT[:, lt, :].rearrange("p (h q) -> p h q", h=2),
                              endtT[:, c, lt, h0:h0 + 2].unsqueeze(2)
                              .broadcast_to([128, 2, 64]),
                              ALU.mult))
                kb.mile("p", sP)
                ps_y = B.acq(7)
                for hh in range(2):
                    h = h0 + hh
                    pm5 = B.acq(5)
                    kb.wait("t", sV, sV.n)
                    for lt in range(2):
                        kb.op("t", lambda E, lt=lt, h=h, pm=pm5: E.transpose(
                            pm[0:1, lt * 128:lt * 128 + 128],
                            acsT[:, lt, h:h + 1], identf[:, :]))
                    kb.mile("t", sT)
                    kb.wait("a", sT)
                    kb.op("a", lambda E, pm=pm5: E.activation(
                        arow[:1, :], pm[0:1, 0:CHUNK], AF.Copy))
                    kb.mile("a", sA)
                    kb.wait("t", sA)
                    kb.op("t", lambda E, pm=pm5: E.matmul(
                        pm[:, 256:512], onesf_row[:1, :], arow[:1, :],
                        start=True, stop=True))
                    kb.mile("t", sT)
                    kb.wait("v", sT)
                    war_wait("v", Larg)
                    for st in range(2):
                        kb.op("v", lambda E, st=st, h=h, pm=pm5:
                              E.tensor_scalar(
                                  Larg[:, :], pm[:, 256:512],
                                  acsT[:, st, h:h + 1], 0.0,
                                  ALU.subtract, ALU.min))
                        kb.mile("v", sV)
                        kb.wait("a", sV)
                        kb.op("a", lambda E, st=st: E.activation(
                            SL[:, st, :], Larg[:, :], AF.Exp))
                        kb.mile("a", sA)
                        kb.wait("v", sA)
                    war_set(Larg, sA, sA.n)
                    B.rel(5, sV)
                    kb.wait("p", sA)
                    kb.wait("p", sT, sT.n)
                    for st in range(2):
                        kb.op("p", lambda E, st=st: E.tensor_mul(
                            SL[:, st, :], SL[:, st, :], SmaskT[:, st, :]))
                    kb.mile("p", sP)
                    kb.wait("t", sP)
                    for lt in range(2):
                        for st in range(2):
                            kb.op("t", lambda E, lt=lt, st=st, hh=hh,
                                  ps=ps_y: E.matmul(
                                      ps[:, lt * 128 + hh * 64:
                                         lt * 128 + hh * 64 + 64],
                                      SL[:, st, lt * 128:lt * 128 + 128],
                                      xdY[:, st, hh * 64:hh * 64 + 64],
                                      start=(st == 0), stop=(st == 1)))
                    kb.mile("t", sT)
                kb.wait("t", sV, sV.n)
                for lt in range(2):
                    kb.op("t", lambda E, lt=lt, hp=hp, ps=ps_y: E.matmul(
                        ps[:, 256 + lt * 128:256 + lt * 128 + 128],
                        co[:, 33, lt * 128:lt * 128 + 128],
                        state_bf[:, hp * 128:hp * 128 + 128],
                        start=True, stop=True))
                kb.mile("t", sT)
                kb.wait("v", sT)
                for lt in range(2):
                    for hh in range(2):
                        kb.op("v", lambda E, lt=lt, hh=hh, c=c, h0=h0,
                              ps=ps_y: E.tensor_scalar(
                                  uw[:, lt, hh * 64:hh * 64 + 64],
                                  ps[:, 256 + lt * 128 + hh * 64:
                                     256 + lt * 128 + hh * 64 + 64],
                                  expT[:, c, lt, h0 + hh:h0 + hh + 1],
                                  None, ALU.mult))
                kb.mile("v", sV)
                kb.wait("v", sV)
                for lt in range(2):
                    kb.op("v", lambda E, lt=lt, ps=ps_y: E.tensor_add(
                        u_bf[:, lt, :], ps[:, lt * 128:lt * 128 + 128],
                        uw[:, lt, :]))
                kb.mile("v", sV)
                B.rel(7, sV)
                ps_c = B.acq(6)
                kb.wait("t", sV)
                for lt in range(2):
                    kb.op("t", lambda E, lt=lt, ps=ps_c: E.matmul(
                        ps[:, 0:128], btok[:, lt, :], xd[:, lt, :],
                        start=(lt == 0), stop=(lt == 1)))
                kb.mile("t", sT)
                kb.wait("v", sT)
                for hh in range(2):
                    kb.op("v", lambda E, hh=hh, hp=hp, c=c, h0=h0:
                          E.tensor_scalar(
                              stw[:, hh * 64:hh * 64 + 64],
                              state[:, hp * 128 + hh * 64:
                                    hp * 128 + hh * 64 + 64],
                              eLbc[:, c, h0 + hh:h0 + hh + 1], None,
                              ALU.mult))
                kb.mile("v", sV)
                kb.wait("v", sV)
                kb.op("v", lambda E, hp=hp, ps=ps_c: E.tensor_add(
                    state[:, hp * 128:hp * 128 + 128], stw[:, :],
                    ps[:, 0:128]))
                kb.mile("v", sV)
                B.rel(6, sV)
                kb.wait("v", sV)
                kb.op("v", lambda E, hp=hp: E.tensor_copy(
                    state_bf[:, hp * 128:hp * 128 + 128],
                    state[:, hp * 128:hp * 128 + 128]))
                kb.mile("v", sV)
                ps_f = B.acq(6)
                pf_bf = ps_f.ap().bitcast(BF16)
                kb.wait("t", sV)
                for lt in range(2):
                    kb.op("t", lambda E, lt=lt, pf=pf_bf: E.transpose(
                        pf[:, lt * 128:lt * 128 + 128], u_bf[:, lt, :],
                        ident[:, :]))
                kb.mile("t", sT)
                kb.wait("v", sT)
                kb.wait("v", sA, sA.n)
                kb.wait("v", sLD, zq[hp])
                kb.op("v", lambda E, hp=hp: E.tensor_scalar(
                    xD[:, :], co[:, hp, :], Dbc_sb[:, hp:hp + 1], None,
                    ALU.mult))
                kb.mile("v", sV)
                kb.wait("v", sV)
                kb.op("v", lambda E, pf=pf_bf: E.tensor_add(
                    yt[:, :], pf[:, 0:256], xD[:, :]))
                kb.mile("v", sV)
                B.rel(6, sV)
                kb.wait("v", sV)
                kb.op("v", lambda E, hp=hp: E.tensor_mul(
                    yt[:, :], yt[:, :], zstr[hp % 2][:, :]))
                kb.mile("v", sV)
                war_set(zstr[hp % 2], sV)
                kb.wait("a", sV)
                kb.op("a", lambda E: E.activation(y2s[:, :], yt[:, :],
                                                  AF.Square))
                kb.mile("a", sA)
                ps_v = B.acq(2) if hp == 0 else B.b[2]
                kb.wait("t", sA)
                kb.op("t", lambda E, hp=hp, ps=ps_v: E.matmul(
                    ps[0:1, 0:CHUNK], ones_col[:, :], y2s[:, :],
                    start=(hp == 0), stop=(hp == 31)), inc=sT)
                kb.wait("v", sV)
                kb.op("v", lambda E, hp=hp: E.tensor_scalar(
                    y32w[:, hp, :], yt[:, :], normw_sb[:, hp:hp + 1],
                    None, ALU.mult))
                kb.mile("v", sV)

            tap(f"y32w{c}", lambda: y32w.ap(), [128, 32, CHUNK], BF16,
                [(sV, sV.n)])
            tap(f"state{c}", lambda: state.ap(), [128, NH * HD], F32,
                [(sV, sV.n)])

            kb.wait("a", sT)
            kb.op("a", lambda E: E.activation(
                row_a[:1, 0:CHUNK], B.b[2][0:1, 0:CHUNK], AF.Copy,
                scale=1.0 / DIN))
            kb.mile("a", sA)
            B.rel(2, sA)
            kb.wait("a", sA)
            kb.op("a", lambda E: E.activation(row_a[:1, 0:CHUNK],
                                              row_a[:1, 0:CHUNK], AF.Sqrt,
                                              bias=EPS))
            kb.mile("a", sA)
            kb.wait("v", sA)
            kb.op("v", lambda E: E.reciprocal(row_b[:1, 0:CHUNK],
                                              row_a[:1, 0:CHUNK]))
            kb.mile("v", sV)
            kb.wait("v", sV)
            kb.op("v", lambda E: E.tensor_copy(rst2_bfr[:1, :],
                                               row_b[:1, 0:CHUNK]))
            kb.mile("v", sV)
            pm2 = B.acq(2)
            kb.wait("t", sV)
            kb.op("t", lambda E, pm=pm2: E.matmul(
                pm[:, 0:CHUNK], ones_row[:1, :], rst2_bfr[:1, :],
                start=True, stop=True))
            kb.mile("t", sT)
            kb.wait("v", sT)
            kb.op("v", lambda E, pm=pm2: E.tensor_copy(rst2_bc[:, :],
                                                       pm[:, 0:CHUNK]))
            kb.mile("v", sV)
            B.rel(2, sV)

            col0 = c * CHUNK + HALO
            w0 = sW.n
            wload(WoaT, 0, 128, wbufL, 0, krange=(0, 16))
            for mi in range(16):
                wload(WoaT, mi * 128, 128, wbufL, 1, krange=(16, 32))
                bi = mi % 2
                ps = B.acq(bi)
                kb.wait("t", sW, w0 + (2 * mi + 2) * 16)
                for k in range(32):
                    par = 0 if k < 16 else 1
                    kb.op("t", lambda E, k=k, ps=ps, par=par: E.matmul(
                        ps[:, 0:CHUNK], wbufL[par][:, k % 16, :],
                        y32w[:, k, :], start=(k == 0), stop=(k == 31)))
                kb.mile("t", sT)
                wdone(wbufL, 0)
                wdone(wbufL, 1)
                if mi + 1 < 16:
                    wload(WoaT, (mi + 1) * 128, 128, wbufL, 0, krange=(0, 16))
                kb.wait("v", sT)
                war_wait("v", yt)
                kb.op("v", lambda E, ps=ps: E.tensor_mul(
                    yt[:, :], ps[:, 0:CHUNK], rst2_bc[:, :]))
                kb.mile("v", sV)
                B.rel(bi, sV)
                war_set(yt, sV, sV.n + 1)
                kb.wait("v", sV)
                kb.op("v", lambda E, mi=mi, col0=col0: E.tensor_add(
                    x_fm[:, mi, col0:col0 + CHUNK],
                    x_fm[:, mi, col0:col0 + CHUNK], yt[:, :]))
                kb.mile("v", sV)

        kb.wait("p", sV, sV.n)

        def tx_state(E):
            with E.register("txmh") as mh, E.register("txpc") as pc:
                E.reg_load(mh, flags_sb[0:1, 0:1])
                E.reg_mov(pc, 0)
                with E.If_eq(mh, 0):
                    E.dma_start(out=sh_state[0:128, :],
                                in_=state[:, :]).then_inc(sPL2.h, 16)
                    E.reg_add(pc, pc, 16)
                    E.wait_ge(sPL2.h, pc)
                    E.dma_start(out=sh_flag[0:1, :],
                                in_=nonce_sb[:1, :]).then_inc(sPL2.h, 16)
                with E.Else():
                    E.dma_start(out=sh_state[128:256, :],
                                in_=state[:, :]).then_inc(sPL2.h, 16)
                    E.reg_add(pc, pc, 16)
                    E.wait_ge(sPL2.h, pc)
                    E.dma_start(out=sh_flag[1:2, :],
                                in_=nonce_sb[:1, :]).then_inc(sPL2.h, 16)
            return E.drain()

        kb.op("p", tx_state, inc=sEX)
        stk2.close()
        stk.close()

    emit_attn_a()

    def emit_ffn(tag):
        stkf = ExitStack()
        wb = [kb.sb(f"wf{tag}{i}", [128, KCH, 128], BF16, stack=stkf)
              for i in range(2)]
        xcf = kb.sb(f"xcf{tag}", [128, KCH, SBW], BF16, stack=stkf)
        h1sb = kb.sb(f"h1{tag}", [128, 32, SBW], BF16, stack=stkf)
        h2sb = kb.sb(f"h2{tag}", [128, 32, SBW], BF16, stack=stkf)
        t1 = kb.sb(f"t1{tag}", [128, SBW], F32, stack=stkf)

        for sbi in range(2):
            c0 = HALO + sbi * SBW
            emit_ln(c0, SBW, xcf)
            emit_xc(c0, SBW, xcf)
            kb.wait("t", sV)
            w0 = sW.n
            par = 0
            wload(W1T, 0, 128, wb, 0)
            for mi in range(32):
                if mi + 1 < 32:
                    wload(W1T, (mi + 1) * 128, 128, wb, 1 - par)
                bi = mi % 2
                ps = B.acq(bi)
                kb.wait("t", sW, w0 + (mi + 1) * 16)
                for k in range(KCH):
                    kb.op("t", lambda E, k=k, ps=ps, par=par: E.matmul(
                        ps[:, :], wb[par][:, k, :], xcf[:, k, :],
                        start=(k == 0), stop=(k == KCH - 1)))
                kb.mile("t", sT)
                wdone(wb, par)
                kb.wait("v", sT)
                war_wait("v", t1)
                kb.op("v", lambda E, ps=ps: E.tensor_mul(
                    t1[:, :], ps[:, :], rstd_bc[:, :SBW]))
                kb.mile("v", sV)
                B.rel(bi, sV)
                kb.wait("a", sV)
                war_set(t1, sA, sA.n + 1)
                kb.op("a", lambda E, mi=mi: E.activation(
                    h1sb[:, mi, :], t1[:, :], AF.Relu,
                    bias=b1_sb[:, mi:mi + 1]))
                kb.mile("a", sA)
                par = 1 - par
            kb.wait("t", sA)
            w0 = sW.n
            wload(W2T, 0, 128, wb, 0, krange=(0, 16))
            for mi in range(32):
                wload(W2T, mi * 128, 128, wb, 1, krange=(16, 32))
                bi = mi % 2
                ps = B.acq(bi)
                kb.wait("t", sW, w0 + (2 * mi + 2) * 16)
                for k in range(32):
                    par = 0 if k < 16 else 1
                    kb.op("t", lambda E, k=k, ps=ps, par=par: E.matmul(
                        ps[:, :], wb[par][:, k % 16, :], h1sb[:, k, :],
                        start=(k == 0), stop=(k == 31)))
                kb.mile("t", sT)
                wdone(wb, 0)
                wdone(wb, 1)
                if mi + 1 < 32:
                    wload(W2T, (mi + 1) * 128, 128, wb, 0, krange=(0, 16))
                kb.wait("a", sT)
                kb.op("a", lambda E, mi=mi, ps=ps: E.activation(
                    h2sb[:, mi, :], ps[:, :], AF.Relu,
                    bias=b2_sb[:, mi:mi + 1]))
                kb.mile("a", sA)
                B.rel(bi, sA)
            kb.wait("t", sA)
            w0 = sW.n
            wload(W3T, 0, 128, wb, 0, krange=(0, 16))
            for mi in range(16):
                wload(W3T, mi * 128, 128, wb, 1, krange=(16, 32))
                bi = mi % 2
                ps = B.acq(bi)
                kb.wait("t", sW, w0 + (2 * mi + 2) * 16)
                for k in range(32):
                    par = 0 if k < 16 else 1
                    kb.op("t", lambda E, k=k, ps=ps, par=par: E.matmul(
                        ps[:, :], wb[par][:, k % 16, :], h2sb[:, k, :],
                        start=(k == 0), stop=(k == 31)))
                kb.mile("t", sT)
                wdone(wb, 0)
                wdone(wb, 1)
                if mi + 1 < 16:
                    wload(W3T, (mi + 1) * 128, 128, wb, 0, krange=(0, 16))
                kb.wait("a", sT)
                war_wait("a", t1)
                kb.op("a", lambda E, mi=mi, ps=ps: E.activation(
                    t1[:, :], ps[:, :], AF.Identity,
                    bias=b3_sb[:, mi:mi + 1]))
                kb.mile("a", sA)
                B.rel(bi, sA)
                kb.wait("v", sA)
                war_set(t1, sV, sV.n + 1)
                kb.op("v", lambda E, mi=mi, c0=c0: E.tensor_add(
                    x_fm[:, mi, c0:c0 + SBW], x_fm[:, mi, c0:c0 + SBW],
                    t1[:, :]))
                kb.mile("v", sV)
        stkf.close()

    if build_upto >= 1:
        emit_ffn("A")

    def emit_attn_b():
        stkb = ExitStack()
        wb = [kb.sb(f"wb{i}", [128, KCH, 128], BF16, stack=stkb)
              for i in range(2)]
        k_full = kb.sb("k_full", [128, KVH, 2 * T], BF16, stack=stkb)
        vtok = kb.sb("vtok", [128, KVH, 16, 128], BF16, stack=stkb)
        q_bf = kb.sb("q_bf", [128, HB, T], BF16, stack=stkb)
        qsc_bc = kb.sb("qsc_bc", [128, SBW], F32, stack=stkb)
        qrow = kb.sb("qrow", [1, SBW], F32, stack=stkb)
        qrow_bf = kb.sb("qrow_bf", [1, SBW], BF16, stack=stkb)
        negmax = kb.sb("negmax", [128, 1], F32, stack=stkb)
        rowsum = kb.sb("rowsum", [128, 1], F32, stack=stkb)
        rsum_i = kb.sb("rsum_i", [128, 1], F32, stack=stkb)
        stkv = ExitStack()
        v_fm = kb.sb("v_fm", [128, KVH, 2 * T], BF16, stack=stkv)
        xcb = kb.sb("xcb", [128, KCH, SBW], BF16, stack=stkv)
        astr = [kb.sb(f"astr{i}", [128, SBW], BF16, stack=stkv)
                for i in range(2)]

        myt0 = None

        for sbi in range(2):
            c0 = HALO + sbi * SBW
            t0 = sbi * SBW
            emit_ln(c0, SBW, xcb)
            emit_xc(c0, SBW, xcb)
            ps_cw = B.acq(3)
            for k in range(KCH):
                par = k % 2
                kb.wait("a", sV)
                war_wait("a", astr[par])
                kb.op("a", lambda E, k=k, par=par: E.activation(
                    astr[par][:, :], xcb[:, k, :], AF.Abs))
                kb.mile("a", sA)
                kb.wait("t", sA)
                kb.op("t", lambda E, k=k, par=par, ps=ps_cw: E.matmul(
                    ps[0:1, :], ones_col[:, :], astr[par][:, :],
                    start=(k == 0), stop=(k == KCH - 1)))
                kb.mile("t", sT)
                war_set(astr[par], sT)
            kb.wait("a", sT)
            kb.op("a", lambda E, ps=ps_cw: E.activation(
                qrow[:1, :], ps[0:1, :], AF.Copy))
            kb.mile("a", sA)
            B.rel(3, sA)
            kb.wait("v", sA)
            kb.op("v", lambda E: E.tensor_mul(qrow[:1, :], qrow[:1, :],
                                              rstd_row[:1, 0:SBW]))
            kb.mile("v", sV)
            kb.wait("a", sV)
            kb.op("a", lambda E: E.activation(qrow[:1, :], qrow[:1, :],
                                              AF.Sigmoid))
            kb.mile("a", sA)
            kb.wait("v", sA)
            kb.op("v", lambda E: E.tensor_scalar(
                qrow[:1, :], qrow[:1, :], float(ASCALE), None, ALU.mult))
            kb.mile("v", sV)
            kb.wait("v", sV)
            kb.op("v", lambda E: E.tensor_mul(qrow[:1, :], qrow[:1, :],
                                              rstd_row[:1, 0:SBW]))
            kb.mile("v", sV)
            kb.wait("v", sV)
            kb.op("v", lambda E: E.tensor_copy(qrow_bf[:1, :], qrow[:1, :]))
            kb.mile("v", sV)
            pb = B.acq(3)
            kb.wait("t", sV)
            kb.op("t", lambda E, pb=pb: E.matmul(
                pb[:, :], ones_row[:1, :], qrow_bf[:1, :],
                start=True, stop=True))
            kb.mile("t", sT)
            kb.wait("v", sT)
            kb.op("v", lambda E, pb=pb: E.tensor_copy(qsc_bc[:, :], pb[:, :]))
            kb.mile("v", sV)
            B.rel(3, sV)

            kb.wait("t", sV)
            w0 = sW.n
            par = 0
            wload(WkvT, 0, 128, wb, 0)
            for mi in range(8):
                if mi + 1 < 8:
                    wload(WkvT, (mi + 1) * 128, 128, wb, 1 - par)
                bi = mi % 2
                ps = B.acq(bi)
                kb.wait("t", sW, w0 + (mi + 1) * 16)
                for k in range(KCH):
                    kb.op("t", lambda E, k=k, ps=ps, par=par: E.matmul(
                        ps[:, :], wb[par][:, k, :], xcb[:, k, :],
                        start=(k == 0), stop=(k == KCH - 1)))
                kb.mile("t", sT)
                wdone(wb, par)
                kb.wait("v", sT)

                def evkv(E, mi=mi, ps=ps, t0=t0, rn=f"kvh{sbi}_{mi}"):
                    with E.register(rn) as mh:
                        E.reg_load(mh, flags_sb[0:1, 0:1])
                        dst = k_full if mi < 4 else v_fm
                        mm = mi % 4
                        with E.If_eq(mh, 0):
                            E.tensor_tensor(dst[:, mm, t0:t0 + SBW],
                                            ps[:, :], rstd_bc[:, 0:SBW],
                                            ALU.mult)
                        with E.Else():
                            E.tensor_tensor(dst[:, mm, T + t0:T + t0 + SBW],
                                            ps[:, :], rstd_bc[:, 0:SBW],
                                            ALU.mult)
                    return E.drain()
                kb.op("v", evkv, inc=sV)
                B.rel(bi, sV)
                par = 1 - par

            kb.wait("t", sV)
            w0 = sW.n
            par = 0
            wload(WqT, 0, 128, wb, 0)
            for mi in range(16):
                if mi + 1 < 16:
                    wload(WqT, (mi + 1) * 128, 128, wb, 1 - par)
                bi = mi % 2
                ps = B.acq(bi)
                kb.wait("t", sW, w0 + (mi + 1) * 16)
                for k in range(KCH):
                    kb.op("t", lambda E, k=k, ps=ps, par=par: E.matmul(
                        ps[:, :], wb[par][:, k, :], xcb[:, k, :],
                        start=(k == 0), stop=(k == KCH - 1)))
                kb.mile("t", sT)
                wdone(wb, par)
                kb.wait("v", sT)
                kb.op("v", lambda E, mi=mi, ps=ps, t0=t0: E.tensor_mul(
                    q_bf[:, mi, t0:t0 + SBW], ps[:, :], qsc_bc[:, :]))
                kb.mile("v", sV)
                B.rel(bi, sV)
                par = 1 - par

        shkv_v = sh_kv.rearrange("(o m p) t -> p o m t", p=128, o=2)
        kb.wait("p", sV, sV.n)

        def kv_xch(E):
            with E.register("xmh") as mh, E.register("xfv") as fv, \
                 E.register("xnr") as nr, E.register("xpc") as pc, \
                 E.register("xnv") as nv:
                E.reg_load(mh, flags_sb[0:1, 0:1])
                E.reg_load(nv, flags_sb[0:1, 1:2])
                E.reg_mov(pc, 0)
                with E.If_eq(mh, 0):
                    E.dma_start(out=shkv_v[:, 0, 0:4, :],
                                in_=k_full[:, :, 0:T]).then_inc(sPL3.h, 16)
                    E.dma_start(out=shkv_v[:, 0, 4:8, :],
                                in_=v_fm[:, :, 0:T]).then_inc(sPL3.h, 16)
                    E.reg_add(pc, pc, 32)
                    E.wait_ge(sPL3.h, pc)
                    E.dma_start(out=sh_flag[2:3, :],
                                in_=nonce_sb[:1, :]).then_inc(sPL3.h, 16)
                    E.reg_add(pc, pc, 16)

                    def poll0():
                        E.dma_start(out=flagrx[:1, :],
                                    in_=sh_flag[3:4, :]).then_inc(sPL3.h, 16)
                        E.reg_add(pc, pc, 16)
                        E.wait_ge(sPL3.h, pc)
                        E.reg_load(fv, flagrx[0:1, 0:1])
                        E.reg_sub(nr, fv, nv)
                    poll0()
                    with E.While(nr):
                        poll0()
                    E.dma_start(out=k_full[:, :, T:2 * T],
                                in_=shkv_v[:, 1, 0:4, :]).then_inc(sPL3.h, 16)
                    E.dma_start(out=v_fm[:, :, T:2 * T],
                                in_=shkv_v[:, 1, 4:8, :]).then_inc(sPL3.h, 16)
                    E.reg_add(pc, pc, 32)
                    E.wait_ge(sPL3.h, pc)
                with E.Else():
                    E.dma_start(out=shkv_v[:, 1, 0:4, :],
                                in_=k_full[:, :, T:2 * T]).then_inc(sPL3.h, 16)
                    E.dma_start(out=shkv_v[:, 1, 4:8, :],
                                in_=v_fm[:, :, T:2 * T]).then_inc(sPL3.h, 16)
                    E.reg_add(pc, pc, 32)
                    E.wait_ge(sPL3.h, pc)
                    E.dma_start(out=sh_flag[3:4, :],
                                in_=nonce_sb[:1, :]).then_inc(sPL3.h, 16)
                    E.reg_add(pc, pc, 16)

                    def poll1():
                        E.dma_start(out=flagrx[:1, :],
                                    in_=sh_flag[2:3, :]).then_inc(sPL3.h, 16)
                        E.reg_add(pc, pc, 16)
                        E.wait_ge(sPL3.h, pc)
                        E.reg_load(fv, flagrx[0:1, 0:1])
                        E.reg_sub(nr, fv, nv)
                    poll1()
                    with E.While(nr):
                        poll1()
                    E.dma_start(out=k_full[:, :, 0:T],
                                in_=shkv_v[:, 0, 0:4, :]).then_inc(sPL3.h, 16)
                    E.dma_start(out=v_fm[:, :, 0:T],
                                in_=shkv_v[:, 0, 4:8, :]).then_inc(sPL3.h, 16)
                    E.reg_add(pc, pc, 32)
                    E.wait_ge(sPL3.h, pc)
            return E.drain()

        kb.op("p", kv_xch, inc=sEX)
        for e in ("t", "v", "a"):
            kb.wait(e, sEX)

        for kv in range(KVH):
            for g in range(4):
                ps_t = B.acq(4 + (g % 2))
                pt_bf = ps_t.ap().bitcast(BF16)
                for jt4 in range(4):
                    jt = g * 4 + jt4
                    kb.op("t", lambda E, kv=kv, jt=jt, jt4=jt4, pt=pt_bf:
                          E.transpose(pt[:, jt4 * 128:jt4 * 128 + 128],
                                      v_fm[:, kv, jt * 128:jt * 128 + 128],
                                      ident[:, :]))
                kb.mile("t", sT)
                kb.wait("v", sT)
                for jt4 in range(4):
                    jt = g * 4 + jt4
                    kb.op("v", lambda E, kv=kv, jt=jt, jt4=jt4, pt=pt_bf:
                          E.tensor_copy(vtok[:, kv, jt, :],
                                        pt[:, jt4 * 128:jt4 * 128 + 128]))
                kb.mile("v", sV)
                B.rel(4 + (g % 2), sV)
        stkv.close()

        stkh = ExitStack()
        P_bf = kb.sb("P_bf", [128, 2, 2 * T], BF16, stack=stkh)
        PT_bf = kb.sb("PT_bf", [128, 16, 256], BF16, stack=stkh)
        S_sb = kb.sb("S_sb", [128, 2 * T], F32, stack=stkh)
        dbase = kb.sb("dbase", [128, 2, 2 * T], F32, stack=stkh)
        band01 = kb.sb("band01", [128, 2 * T], BF16, stack=stkh)
        aostr = kb.sb("aostr", [128, 256], BF16, stack=stkh)

        for itp in range(4):
            for sl in range(2):
                it = 2 * itp + sl

                def mk_dbase(E, it=it, sl=sl, rn=f"dmh{it}"):
                    with E.register(rn) as mh:
                        E.reg_load(mh, flags_sb[0:1, 0:1])
                        with E.If_eq(mh, 0):
                            E.iota(dbase[:, sl, :], [[1, 2 * T]],
                                   base=-(it * 128), channel_multiplier=-1,
                                   allow_small_or_imprecise_dtypes=True)
                        with E.Else():
                            E.iota(dbase[:, sl, :], [[1, 2 * T]],
                                   base=-(T + it * 128),
                                   channel_multiplier=-1,
                                   allow_small_or_imprecise_dtypes=True)
                        E.memset(band01[:, :], 1.0)
                    return E.drain()
                kb.wait("p", sA, sA.n)
                kb.wait("p", sV, sV.n)
                kb.op("p", mk_dbase, inc=sP)
                kb.wait("a", sP)
                kb.op("a", lambda E, sl=sl: E.activation(
                    dbase[:, sl, :], dbase[:, sl, :], AF.Abs))
                kb.mile("a", sA)

                def mk_band(E, it=it, rn=f"bmh{it}"):
                    with E.register(rn) as mh:
                        E.reg_load(mh, flags_sb[0:1, 0:1])
                        for half in range(2):
                            base_i = half * T + it * 128
                            with (E.If_eq(mh, 0) if half == 0 else E.Else()):
                                E.affine_select(band01[:, :], band01[:, :],
                                                [[1, 2 * T]], ALU.is_ge, 0.0,
                                                base=-base_i + 256,
                                                channel_multiplier=-1)
                                E.drain()
                                E.affine_select(band01[:, :], band01[:, :],
                                                [[-1, 2 * T]], ALU.is_ge, 0.0,
                                                base=base_i + 256,
                                                channel_multiplier=1)
                    return E.drain()
                kb.op("p", mk_band, inc=sP)
                kb.wait("v", sP)
                kb.wait("v", sA)
                kb.op("v", lambda E, sl=sl: E.scalar_tensor_tensor(
                    dbase[:, sl, :], band01[:, :], 4.0e9, dbase[:, sl, :],
                    ALU.mult, ALU.add))
                kb.mile("v", sV)
                kb.wait("v", sV)
                kb.op("v", lambda E, sl=sl: E.memset(dbase[:, sl, 0:64],
                                                     4.0e9))
                kb.mile("v", sV)
            dbase_ready = sV.n

            for h in range(HB):
                kv = h // 4
                for sl in range(2):
                    it = 2 * itp + sl
                    kb.wait("v", sA, sA.n)
                    for js in range(4):
                        bi = js % 2
                        ps = B.acq(bi)
                        kb.wait("t", sEX)
                        kb.wait("t", sV, dbase_ready)
                        kb.op("t", lambda E, h=h, it=it, js=js, kv=kv, ps=ps:
                              E.matmul(ps[:, :],
                                       q_bf[:, h, it * 128:it * 128 + 128],
                                       k_full[:, kv, js * 512:js * 512 + 512],
                                       start=True, stop=True))
                        kb.mile("t", sT)
                        kb.wait("v", sT)
                        kb.op("v", lambda E, h=h, js=js, sl=sl, ps=ps:
                              E.scalar_tensor_tensor(
                                  S_sb[:, js * 512:js * 512 + 512],
                                  dbase[:, sl, js * 512:js * 512 + 512],
                                  slopes_sb[:, h:h + 1], ps[:, :],
                                  ALU.mult, ALU.add))
                        kb.mile("v", sV)
                        B.rel(bi, sV)
                    kb.wait("v", sV)
                    kb.op("v", lambda E: E.reduce_max(
                        negmax[:, :], S_sb[:, :], axis=mybir.AxisListType.X,
                        negate=True))
                    kb.mile("v", sV)
                    kb.wait("a", sV)
                    kb.op("a", lambda E, sl=sl: E.activation(
                        P_bf[:, sl, :], S_sb[:, :], AF.Exp,
                        bias=negmax[:, 0:1], accum_out=rowsum[:, 0:1]))
                    kb.mile("a", sA)
                    kb.wait("v", sA)
                    if it == 0:
                        def fix_glob(E, sl=sl, rn=f"gmh{h}"):
                            with E.register(rn) as mh:
                                E.reg_load(mh, flags_sb[0:1, 0:1])
                                with E.If_eq(mh, 0):
                                    E.memset(P_bf[0:64, sl, :], 1.0)
                                    E.memset(rowsum[0:64, 0:1], 2048.0)
                            return E.drain()
                        kb.op("v", fix_glob, inc=sV)
                        kb.wait("v", sV)
                    kb.op("v", lambda E: E.reciprocal(rsum_i[:, :],
                                                      rowsum[:, :]))
                    kb.mile("v", sV)
                    kb.wait("v", sV)
                    kb.op("v", lambda E, sl=sl: E.tensor_scalar(
                        P_bf[:, sl, :], P_bf[:, sl, :], rsum_i[:, 0:1], None,
                        ALU.mult))
                    kb.mile("v", sV)

                for jt in range(16):
                    ps_t = B.acq(4 + (jt % 2))
                    pt_bf = ps_t.ap().bitcast(BF16)
                    kb.wait("t", sV, sV.n)
                    for i4 in range(2):
                        kb.op("t", lambda E, jt=jt, i4=i4, pt=pt_bf:
                              E.transpose(
                                  pt[:, i4 * 128:i4 * 128 + 128],
                                  P_bf[:, i4, jt * 128:jt * 128 + 128],
                                  ident[:, :]))
                    kb.mile("t", sT)
                    kb.wait("v", sT)
                    kb.op("v", lambda E, jt=jt, pt=pt_bf: E.tensor_copy(
                        PT_bf[:, jt, :], pt[:, 0:256]))
                    kb.mile("v", sV)
                    B.rel(4 + (jt % 2), sV)
                ps_av = B.acq(6)
                kb.wait("t", sV)
                for jt in range(16):
                    kb.op("t", lambda E, jt=jt, kv=kv, ps=ps_av: E.matmul(
                        ps[:, 0:256], vtok[:, kv, jt, :], PT_bf[:, jt, :],
                        start=(jt == 0), stop=(jt == 15)))
                kb.mile("t", sT)
                kb.wait("v", sT)
                war_wait("v", aostr)
                kb.op("v", lambda E, ps=ps_av: E.tensor_copy(
                    aostr[:, :], ps[:, 0:256]))
                kb.mile("v", sV)
                B.rel(6, sV)
                kb.wait("y", sV)
                kb.dma("y", ao_v[:, h, itp * 256:itp * 256 + 256],
                       aostr[:, :], sST)
                war_set(aostr, sST)
        stkh.close()

        tap("aoscr", lambda: ao_scr, [HB * HDB, T], BF16, [(sST, sST.n)])
        tap("kfull", lambda: k_full.ap(), [128, KVH, 2 * T], BF16,
            [(sEX, sEX.n)])
        tap("qbf", lambda: q_bf.ap(), [128, HB, T], BF16, [(sV, sV.n)])
        stko = ExitStack()
        aosb = kb.sb("aosb", [128, HB, T], BF16, stack=stko)
        t1b = kb.sb("t1b", [128, SBW], F32, stack=stko)
        kb.wait("y", sST)
        kb.dma("y", aosb.ap(), ao_v, sLD)
        for sbi in range(2):
            t0 = sbi * SBW
            c0 = HALO + t0
            w0 = sW.n
            wload(WobT, 0, 128, wb, 0)
            for mi in range(16):
                if mi + 1 < 16:
                    wload(WobT, (mi + 1) * 128, 128, wb, 1 - (mi % 2))
                bi = mi % 2
                ps = B.acq(bi)
                kb.wait("t", sW, w0 + (mi + 1) * 16)
                kb.wait("t", sLD)
                for k in range(HB):
                    kb.op("t", lambda E, k=k, ps=ps, par=mi % 2, t0=t0:
                          E.matmul(ps[:, :], wb[par][:, k, :],
                                   aosb[:, k, t0:t0 + SBW],
                                   start=(k == 0), stop=(k == HB - 1)))
                kb.mile("t", sT)
                wdone(wb, mi % 2)
                kb.wait("a", sT)
                war_wait("a", t1b)
                kb.op("a", lambda E, mi=mi, ps=ps: E.activation(
                    t1b[:, :], ps[:, :], AF.Identity,
                    bias=bo_sb[:, mi:mi + 1]))
                kb.mile("a", sA)
                B.rel(bi, sA)
                kb.wait("v", sA)
                war_set(t1b, sV, sV.n + 1)
                kb.op("v", lambda E, mi=mi, c0=c0: E.tensor_add(
                    x_fm[:, mi, c0:c0 + SBW], x_fm[:, mi, c0:c0 + SBW],
                    t1b[:, :]))
                kb.mile("v", sV)
            w0 = sW.n
        stko.close()
        stkb.close()

    if build_upto >= 2:
        emit_attn_b()
    if build_upto >= 3:
        emit_ffn("B")


    for e in ("v", "a", "p", "t"):
        kb.mile(e, sXC)
    kb.wait("y", sXC)
    kb.dma("y", out_fm.rearrange("(k p) t -> p k t", p=128),
           x_fm.ap()[:, :, HALO:], sST)
    kb.wait("y", sST)

    with nc.Block() as block:
        @block.tensor
        def _(E):
            for f in kb.cl["t"]:
                f(E)

        @block.vector
        def _(E):
            for f in kb.cl["v"]:
                f(E)

        @block.scalar
        def _(E):
            for f in kb.cl["a"]:
                f(E)

        @block.gpsimd
        def _(E):
            for f in kb.cl["p"]:
                f(E)

        @block.sync
        def _(E):
            for f in kb.cl["y"]:
                f(E)

    kb.es.close()
    return nc, tap_names



def _prep_inputs(inputs):
    x = np.ascontiguousarray(np.asarray(inputs["L0_attn_a_hidden_states"],
                                        np.float32))
    f32 = np.float32

    def c(a):
        return np.ascontiguousarray(np.asarray(a, f32))

    Dp = np.asarray(inputs["L0_attn_a_D"], f32)
    shared = {
        "WinT": c(np.asarray(inputs["L0_attn_a_in_proj_w"]).T),
        "convw": c(np.asarray(inputs["L0_attn_a_conv_w"])
                   .reshape(34, 128, 4).transpose(1, 0, 2).reshape(128, 136)),
        "convb": c(np.asarray(inputs["L0_attn_a_conv_b"]).reshape(34, 128).T),
        "dtbias": c(np.asarray(inputs["L0_attn_a_dt_bias"]).reshape(64, 1)),
        "A_log": c(np.asarray(inputs["L0_attn_a_A_log"]).reshape(64, 1)),
        "D_bc": c(Dp.repeat(64).reshape(32, 128).T),
        "normw": c(np.asarray(inputs["L0_attn_a_norm_w"]).reshape(32, 128).T),
        "WoaT": c(np.asarray(inputs["L0_attn_a_out_proj_w"]).T),
        "W1T": c(np.asarray(inputs["L0_ffn_W1"]).T),
        "W2T": c(np.asarray(inputs["L0_ffn_W2"]).T),
        "W3T": c(np.asarray(inputs["L0_ffn_W3"]).T),
        "b1c": c(np.asarray(inputs["L0_ffn_b1"]).reshape(32, 128).T),
        "b2c": c(np.asarray(inputs["L0_ffn_b2"]).reshape(32, 128).T),
        "b3c": c(np.asarray(inputs["L0_ffn_b3"]).reshape(16, 128).T),
        "WqT": c(inputs["L0_attn_b_Wq"]),
        "WkvT": c(inputs["L0_attn_b_Wkv"]),
        "WobT": c(inputs["L0_attn_b_Wo"]),
        "boc": c(np.asarray(inputs["L0_attn_b_bo"]).reshape(16, 128).T),
    }
    cp = 2.0 ** np.floor(np.log2(HB))
    base = 2.0 ** (-(2.0 ** (-(np.log2(cp) - 3.0))))
    slopes = (base ** np.arange(1, HB + 1, dtype=np.float64)).astype(f32)
    shared["slopes_n"] = c(np.broadcast_to(-slopes[None, :], (128, HB)))

    rng = np.random.default_rng()
    nonce = np.uint32(rng.integers(1, 2**32 - 1, dtype=np.uint64))

    in_maps = []
    for cid in range(8):
        b, h = cid // 2, cid % 2
        xi = x[b]
        t0 = h * T
        halo = np.zeros((HALO, D), f32) if h == 0 else xi[t0 - HALO:t0]
        xslab = np.concatenate([halo, xi[t0:t0 + T]], axis=0)
        m = dict(shared)
        m["x_fm"] = c(xslab.T)
        m["flags"] = np.array([[h, nonce, 0, 0, 0, 0, 0, 0]], np.uint32)
        m["halfsc"] = np.full((128, 1), float(h), f32)
        in_maps.append(m)
    return in_maps


def kernel(**inputs):
    in_maps = _prep_inputs(inputs)
    nc, _ = build()
    res = run_bass_kernel_spmd(nc, in_maps, core_ids=list(range(8)))
    out = np.empty((4, 2048, 2048), np.float32)
    for cid in range(8):
        b, h = cid // 2, cid % 2
        out[b, h * T:(h + 1) * T, :] = res.results[cid]["out_fm"].T
    return out


# version 02
# speedup vs baseline: 1.0414x; 1.0414x over previous
import numpy as np
import ml_dtypes
import concourse.bass as bass
import concourse.mybir as mybir
from concourse.bass_utils import run_bass_kernel_spmd
from contextlib import ExitStack

F32 = mybir.dt.float32
BF16 = mybir.dt.bfloat16
U32 = mybir.dt.uint32
AF = mybir.ActivationFunctionType
ALU = mybir.AluOpType

T = 1024
HALO = 3
TF = T + HALO
D = 2048
KCH = 16
PROJ = 8512
DIN = 4096
NH = 64
HD = 64
CHUNK = 256
NCHUNK = 4
SBW = 512
EPS = 1e-5
HB = 16
KVH = 4
HDB = 128
ASCALE = float(1.0 / np.sqrt(128.0))

ENGS = ("t", "v", "a", "p", "y")


class Sem:
    def __init__(self, es, nc, name):
        self.h = es.enter_context(nc.semaphore(name))
        self.n = 0


class KB:
    def __init__(self, nc):
        self.nc = nc
        self.es = ExitStack()
        self.cl = {e: [] for e in ENGS}
        self._i = 0

    def sem(self, name=None):
        self._i += 1
        return Sem(self.es, self.nc, name or f"s{self._i}")

    def sb(self, name, shape, dt, stack=None):
        return (stack or self.es).enter_context(
            self.nc.sbuf_tensor(name, shape, dt))

    def emit(self, e, fn):
        self.cl[e].append(fn)

    def op(self, e, fn, inc=None, k=1):
        if inc is None:
            d = self.esem.get(e)
            if d is None:
                self.emit(e, fn)
            else:
                d.n += 1
                self.emit(e, lambda E, fn=fn, s=d: fn(E).then_inc(s.h, 1))
        else:
            inc.n += k
            self.emit(e, lambda E, fn=fn, s=inc, kk=k: fn(E).then_inc(s.h, kk))

    def wait(self, e, sem, at=None):
        v = sem.n if at is None else at
        if v > 0:
            self.emit(e, lambda E, h=sem.h, v=v: E.wait_ge(h, v))

    def mile(self, e, sem, k=1):
        if self.esem.get(e) is sem:
            return
        sem.n += k
        self.emit(e, lambda E, h=sem.h, k=k: E.drain().then_inc(h, k))

    def dma(self, e, out, in_, sem, **kw):
        sem.n += 16
        self.emit(e, lambda E, o=out, i=in_, h=sem.h, kw=kw:
                  E.dma_start(out=o, in_=i, **kw).then_inc(h, 16))


class Banks:
    def __init__(self, kb):
        self.kb = kb
        self.b = [kb.es.enter_context(
            kb.nc.psum_tensor(f"bank{i}", [128, 512], F32)) for i in range(8)]
        self.tag = [None] * 8

    def acq(self, i, weng="t"):
        if self.tag[i] is not None:
            s, v = self.tag[i]
            self.kb.wait(weng, s, v)
            self.tag[i] = None
        return self.b[i]

    def rel(self, i, sem, at=None):
        self.tag[i] = (sem, sem.n if at is None else at)


def build(build_upto=3, taps=()):
    nc = bass.Bass("TRN2", target_bir_lowering=False)
    kb = KB(nc)
    tap_req = set(taps)
    tap_names = []

    din = {}

    def dram_in(name, shape, dt=F32):
        din[name] = nc.dram_tensor(name, shape, dt, kind="ExternalInput").ap()
        return din[name]

    x_in = dram_in("x_fm", [D, TF])
    WinT = dram_in("WinT", [D, PROJ])
    dram_in("convw", [128, 34 * 4])
    dram_in("convb", [128, 34])
    dram_in("dtbias", [64, 1])
    dram_in("A_log", [64, 1])
    dram_in("D_bc", [128, 32])
    dram_in("normw", [128, 32])
    WoaT = dram_in("WoaT", [DIN, D])
    W1T = dram_in("W1T", [D, DIN])
    W2T = dram_in("W2T", [DIN, DIN])
    W3T = dram_in("W3T", [DIN, D])
    dram_in("b1c", [128, 32])
    dram_in("b2c", [128, 32])
    dram_in("b3c", [128, 16])
    WqT = dram_in("WqT", [D, D])
    WkvT = dram_in("WkvT", [D, 1024])
    WobT = dram_in("WobT", [D, D])
    dram_in("boc", [128, 16])
    dram_in("slopes_n", [128, HB])
    dram_in("flags", [1, 8], U32)
    dram_in("halfsc", [128, 1])

    out_fm = nc.dram_tensor("out_fm", [D, T], F32, kind="ExternalOutput").ap()
    ao_scr = nc.dram_tensor("ao_scr", [HB * HDB, T], BF16,
                            kind="Internal").ap()
    xbc_scr = nc.dram_tensor("xbc_scr", [34 * 128, TF], BF16,
                             kind="Internal").ap()
    silz_scr = nc.dram_tensor("silz_scr", [32 * 128, T], BF16,
                              kind="Internal").ap()
    h1_scr = nc.dram_tensor("h1_scr", [32 * 128, T], BF16,
                            kind="Internal").ap()
    sh_state = nc.dram_tensor("sh_state", [2 * 128, NH * HD], F32,
                              kind="Internal", addr_space="Shared").ap()
    sh_kv = nc.dram_tensor("sh_kv", [2 * 1024, T], BF16,
                           kind="Internal", addr_space="Shared").ap()
    sh_flag = nc.dram_tensor("sh_flag", [4, 16], U32,
                             kind="Internal", addr_space="Shared").ap()

    xbc_v = xbc_scr.rearrange("(m p) t -> p m t", p=128)
    silz_v = silz_scr.rearrange("(m p) t -> p m t", p=128)
    h1_v = h1_scr.rearrange("(m p) t -> p m t", p=128)
    ao_v = ao_scr.rearrange("(m p) t -> p m t", p=128)

    x_fm = kb.sb("x_fm_sb", [128, KCH, TF], F32)
    ident = kb.sb("ident", [128, 128], BF16)
    identf = kb.sb("identf", [128, 128], F32)
    ones_col = kb.sb("ones_col", [128, 1], BF16)
    ones_row = kb.sb("ones_row", [1, 128], BF16)
    convw_sb = kb.sb("convw_sb", [128, 34 * 4], F32)
    convb_sb = kb.sb("convb_sb", [128, 34], F32)
    dtbias_sb = kb.sb("dtbias_sb", [64, 1], F32)
    Acol = kb.sb("Acol", [64, 1], F32)
    Dbc_sb = kb.sb("Dbc_sb", [128, 32], F32)
    normw_sb = kb.sb("normw_sb", [128, 32], F32)
    b1_sb = kb.sb("b1_sb", [128, 32], F32)
    b2_sb = kb.sb("b2_sb", [128, 32], F32)
    b3_sb = kb.sb("b3_sb", [128, 16], F32)
    bo_sb = kb.sb("bo_sb", [128, 16], F32)
    slopes_sb = kb.sb("slopes_sb", [128, HB], F32)
    halfsc_sb = kb.sb("halfsc_sb", [128, 1], F32)
    flags_sb = kb.sb("flags_sb", [1, 8], U32)
    zeros64 = kb.sb("zeros64", [64, CHUNK], F32)
    eps_c = kb.sb("eps_c", [128, 1], F32)
    nc.const_aps.register(F32, EPS, eps_c[:, :])
    nonce_sb = kb.sb("nonce_sb", [1, 16], U32)
    flagrx = kb.sb("flagrx", [1, 16], U32)
    mean_bfr = kb.sb("mean_bfr", [1, SBW], BF16)
    rstd_row = kb.sb("rstd_row", [1, SBW], F32)
    rstd_bfr = kb.sb("rstd_bfr", [1, SBW], BF16)
    rstd_bc = kb.sb("rstd_bc", [128, SBW], F32)
    row_a = kb.sb("row_a", [1, SBW], F32)
    row_b = kb.sb("row_b", [1, SBW], F32)

    B = Banks(kb)

    sLD = kb.sem("sLD")
    sW = kb.sem("sW")
    sT = kb.sem("sT")
    sV = kb.sem("sV")
    sA = kb.sem("sA")
    sP = kb.sem("sP")
    sST = kb.sem("sST")
    sEX = kb.sem("sEX")
    sXC = kb.sem("sXC")
    kb.esem = {"t": sT, "v": sV, "a": sA, "p": sP}
    sPL = kb.sem("sPL")
    sPL2 = kb.sem("sPL2")
    sPL3 = kb.sem("sPL3")

    for name, dst in [("convw", convw_sb), ("convb", convb_sb),
                      ("dtbias", dtbias_sb), ("A_log", Acol),
                      ("D_bc", Dbc_sb), ("normw", normw_sb),
                      ("b1c", b1_sb), ("b2c", b2_sb), ("b3c", b3_sb),
                      ("boc", bo_sb), ("slopes_n", slopes_sb),
                      ("halfsc", halfsc_sb), ("flags", flags_sb)]:
        kb.dma("y", dst.ap(), din[name], sLD)
    kb.dma("y", x_fm.ap(), x_in.rearrange("(k p) t -> p k t", p=128), sLD)

    kb.op("p", lambda E: E.memset(ident[:, :], 1.0))
    kb.op("p", lambda E: E.memset(ones_col[:, :], 1.0))
    kb.op("p", lambda E: E.memset(ones_row[:, :], 1.0))
    kb.op("p", lambda E: E.memset(zeros64[:, :], 0.0))
    kb.op("p", lambda E: E.memset(eps_c[:, :], EPS))
    kb.mile("p", sP)
    kb.op("p", lambda E: E.affine_select(ident[:, :], ident[:, :], [[1, 128]],
                                         ALU.is_equal, 0.0, base=0,
                                         channel_multiplier=-1))
    kb.mile("p", sP)
    kb.op("p", lambda E: E.tensor_copy(identf[:, :], ident[:, :]))
    kb.wait("p", sLD)
    kb.op("p", lambda E: E.tensor_copy(
        nonce_sb[:1, :], flags_sb[0:1, 1:2].broadcast_to([1, 16])))
    kb.mile("p", sP)
    kb.wait("a", sLD)
    kb.op("a", lambda E: E.activation(Acol[:, :], Acol[:, :], AF.Exp))
    kb.mile("a", sA)
    kb.wait("v", sA)
    kb.op("v", lambda E: E.tensor_scalar(Acol[:, :], Acol[:, :], -1.0, None,
                                         ALU.mult))
    kb.mile("v", sV)

    _st = (sLD.n, sP.n, sV.n, sA.n)
    for e in ENGS:
        kb.wait(e, sLD, _st[0])
        kb.wait(e, sP, _st[1])
        kb.wait(e, sV, _st[2])
        kb.wait(e, sA, _st[3])

    def tap(name, ap_factory, shape, dt, waits):
        if name not in tap_req:
            return
        th = nc.dram_tensor("tap_" + name, list(shape), dt,
                            kind="ExternalOutput").ap()
        tap_names.append("tap_" + name)
        for (sm, v) in waits:
            kb.wait("y", sm, v)
        kb.dma("y", th, ap_factory(), sST)

    war = {}

    def war_wait(e, buf):
        t = war.get(id(buf))
        if t is not None:
            kb.wait(e, t[0], t[1])

    def war_set(buf, sem, at=None):
        war[id(buf)] = (sem, sem.n if at is None else at)

    def emit_ln(c0, n, xcbuf, rbase=0):
        kb.wait("p", sV, sV.n)
        for k in range(KCH):
            kb.op("p", lambda E, k=k, n=n, c0=c0: E.tensor_copy(
                xcbuf[:, k, :n], x_fm[:, k, c0:c0 + n]))
        kb.mile("p", sP)
        ps_s = B.acq(3)
        kb.wait("t", sP)
        for k in range(KCH):
            kb.op("t", lambda E, k=k, ps=ps_s, n=n: E.matmul(
                ps[0:1, :n], ones_col[:, :], xcbuf[:, k, :n],
                start=(k == 0), stop=(k == KCH - 1)))
        kb.mile("t", sT)
        kb.wait("a", sT)
        for k in range(KCH):
            kb.op("a", lambda E, k=k, n=n: E.activation(
                xcbuf[:, k, :n], xcbuf[:, k, :n], AF.Square))
        kb.mile("a", sA)
        ps_q = B.acq(4)
        kb.wait("t", sA)
        for k in range(KCH):
            kb.op("t", lambda E, k=k, ps=ps_q, n=n: E.matmul(
                ps[0:1, :n], ones_col[:, :], xcbuf[:, k, :n],
                start=(k == 0), stop=(k == KCH - 1)))
        kb.mile("t", sT)
        r0 = rbase
        kb.wait("a", sT)
        kb.op("a", lambda E, ps=ps_s, n=n, r0=r0: E.activation(
            row_a[:, r0:r0 + n], ps[0:1, :n], AF.Copy, scale=1.0 / D))
        kb.op("a", lambda E, ps=ps_s, n=n, r0=r0: E.activation(
            mean_bfr[:, r0:r0 + n], ps[0:1, :n], AF.Copy, scale=1.0 / D))
        kb.op("a", lambda E, ps=ps_q, n=n, r0=r0: E.activation(
            row_b[:, r0:r0 + n], ps[0:1, :n], AF.Copy, scale=1.0 / D))
        kb.mile("a", sA)
        B.rel(3, sA)
        B.rel(4, sA)
        kb.wait("a", sA)
        kb.op("a", lambda E, n=n, r0=r0: E.activation(
            row_a[:, r0:r0 + n], row_a[:, r0:r0 + n], AF.Square))
        kb.mile("a", sA)
        kb.wait("v", sA)
        kb.op("v", lambda E, n=n, r0=r0: E.tensor_sub(
            row_b[:, r0:r0 + n], row_b[:, r0:r0 + n], row_a[:, r0:r0 + n]))
        kb.mile("v", sV)
        kb.wait("a", sV)
        kb.op("a", lambda E, n=n, r0=r0: E.activation(
            row_b[:, r0:r0 + n], row_b[:, r0:r0 + n], AF.Sqrt, bias=EPS))
        kb.mile("a", sA)
        kb.wait("v", sA)
        kb.op("v", lambda E, n=n, r0=r0: E.reciprocal(
            rstd_row[:, r0:r0 + n], row_b[:, r0:r0 + n]))
        kb.mile("v", sV)
        kb.wait("v", sV)
        kb.op("v", lambda E, n=n, r0=r0: E.tensor_copy(
            rstd_bfr[:, r0:r0 + n], rstd_row[:, r0:r0 + n]))
        kb.mile("v", sV)
        pb = B.acq(2)
        kb.wait("t", sV)
        kb.op("t", lambda E, pb=pb, n=n, r0=r0: E.matmul(
            pb[:, :n], ones_row[:1, :], rstd_bfr[:1, r0:r0 + n],
            start=True, stop=True))
        kb.mile("t", sT)
        kb.wait("v", sT)
        kb.op("v", lambda E, pb=pb, n=n, r0=r0: E.tensor_copy(
            rstd_bc[:, r0:r0 + n], pb[:, :n]))
        kb.mile("v", sV)
        B.rel(2, sV)

    def emit_xc(c0, n, xcbuf, rbase=0):
        pb = B.acq(2)
        kb.wait("t", sV)
        kb.op("t", lambda E, pb=pb, n=n, rbase=rbase: E.matmul(
            pb[:, :n], ones_row[:1, :], mean_bfr[:1, rbase:rbase + n],
            start=True, stop=True))
        kb.mile("t", sT)
        kb.wait("v", sT)
        for k in range(KCH):
            kb.op("v", lambda E, k=k, pb=pb, n=n, c0=c0: E.tensor_sub(
                xcbuf[:, k, :n], x_fm[:, k, c0:c0 + n], pb[:, :n]))
        kb.mile("v", sV)
        B.rel(2, sV)

    wfree = {}

    def wload(WT_ap, m0, mw, wbuf, par, krange=None):
        t = wfree.get(id(wbuf[par]))
        if t is not None:
            kb.wait("p", t[0], t[1])
        src = WT_ap.rearrange("(k p) m -> p k m", p=128)
        if krange is None:
            kcnt = src.shape[1]
            src = src[:, :, m0:m0 + mw]
        else:
            kcnt = krange[1] - krange[0]
            src = src[:, krange[0]:krange[1], m0:m0 + mw]
        kb.dma("p", wbuf[par][:, :kcnt, :mw], src, sW)

    def wdone(wbuf, par):
        wfree[id(wbuf[par])] = (sT, sT.n)

    def emit_attn_a():
        stk = ExitStack()
        state = kb.sb("state", [128, NH * HD], F32, stack=stk)
        state_bf = kb.sb("state_bf", [128, NH * HD], BF16, stack=stk)
        dt_pre = kb.sb("dt_pre", [64, T], F32, stack=stk)
        expT = kb.sb("expT", [128, NCHUNK, 2, 64], F32, stack=stk)
        endtT = kb.sb("endtT", [128, NCHUNK, 2, 64], F32, stack=stk)
        eLbc = kb.sb("eLbc", [128, NCHUNK, 64], F32, stack=stk)
        acsLbc = kb.sb("acsLbc", [128, NCHUNK, 64], F32, stack=stk)
        kb.op("p", lambda E: E.memset(state[:, :], 0.0))
        kb.op("p", lambda E: E.memset(state_bf[:, :], 0.0))
        kb.mile("p", sP)
        wbuf = [kb.sb(f"wa{i}", [128, KCH, 128], BF16, stack=stk)
                for i in range(2)]
        stk1 = ExitStack()
        xcbuf = kb.sb("xcbuf", [128, KCH, SBW], BF16, stack=stk1)
        strA = [kb.sb(f"strA{i}", [128, SBW], BF16, stack=stk1)
                for i in range(2)]
        zt1 = kb.sb("zt1", [128, SBW], F32, stack=stk1)
        xbchalo = kb.sb("xbchalo", [128, 34, HALO], BF16, stack=stk1)

        def evac_zpath(m, ps, bi, c0, n):
            t0 = c0 - HALO
            par = m % 2
            kb.wait("v", sT)
            war_wait("v", zt1)
            kb.op("v", lambda E, ps=ps, n=n: E.tensor_mul(
                zt1[:, :n], ps[:128, :n], rstd_bc[:, :n]))
            kb.mile("v", sV)
            B.rel(bi, sV)
            kb.wait("a", sV)
            war_set(zt1, sA, sA.n + 1)
            war_wait("a", strA[par])
            kb.op("a", lambda E, par=par, n=n: E.activation(
                strA[par][:, :n], zt1[:, :n], AF.Silu))
            kb.mile("a", sA)
            kb.wait("y", sA)
            kb.dma("y", silz_v[:, m, t0:t0 + n], strA[par][:, :n], sST)
            war_set(strA[par], sST)

        def evac_xbc(m, ps, bi, c0, n):
            par = m % 2
            kb.wait("v", sT)
            war_wait("v", strA[par])
            kb.op("v", lambda E, ps=ps, n=n, par=par: E.tensor_mul(
                strA[par][:, :n], ps[:128, :n], rstd_bc[:, :n]))
            kb.mile("v", sV)
            B.rel(bi, sV)
            kb.wait("y", sV)
            kb.dma("y", xbc_v[:, m - 32, c0:c0 + n], strA[par][:, :n], sST)
            war_set(strA[par], sST)

        def evac_xbc_halo(m, ps, bi):
            kb.wait("v", sT)
            kb.op("v", lambda E, ps=ps, m=m: E.tensor_mul(
                xbchalo[:, m - 32, :], ps[:128, :HALO], rstd_bc[:, :HALO]))
            kb.mile("v", sV)
            B.rel(bi, sV)

        def evac_dt(m, ps, bi, c0, n):
            t0 = c0 - HALO
            kb.wait("v", sT)
            kb.op("v", lambda E, ps=ps, n=n, t0=t0: E.tensor_mul(
                dt_pre[:, t0:t0 + n], ps[0:64, :n], rstd_bc[0:64, :n]))
            kb.mile("v", sV)
            B.rel(bi, sV)

        def passA(c0, n, mlist, outfn):
            kb.wait("t", sV)
            w0 = sW.n
            par = 0
            wload(WinT, mlist[0] * 128, 64 if mlist[0] == 66 else 128,
                  wbuf, par)
            for mi, m in enumerate(mlist):
                if mi + 1 < len(mlist):
                    nm = mlist[mi + 1]
                    wload(WinT, nm * 128, 64 if nm == 66 else 128,
                          wbuf, 1 - par)
                mw = 64 if m == 66 else 128
                bi = mi % 2
                ps = B.acq(bi)
                kb.wait("t", sW, w0 + (mi + 1) * 16)
                for k in range(KCH):
                    kb.op("t", lambda E, k=k, ps=ps, par=par, mw=mw, n=n:
                          E.matmul(ps[:mw, :n], wbuf[par][:, k, :mw],
                                   xcbuf[:, k, :n],
                                   start=(k == 0), stop=(k == KCH - 1)))
                kb.mile("t", sT)
                wdone(wbuf, par)
                outfn(m, ps, bi)
                par = 1 - par

        emit_ln(0, HALO, xcbuf)
        emit_xc(0, HALO, xcbuf)
        passA(0, HALO, list(range(32, 66)),
              lambda m, ps, bi: evac_xbc_halo(m, ps, bi))
        kb.wait("y", sV)
        kb.dma("y", xbc_v[:, :, 0:HALO], xbchalo.ap(), sST)

        for sbi in range(2):
            c0 = HALO + sbi * SBW
            emit_ln(c0, SBW, xcbuf)
            emit_xc(c0, SBW, xcbuf)
            passA(c0, SBW, list(range(0, 67)),
                  lambda m, ps, bi, c0=c0: (
                      evac_zpath(m, ps, bi, c0, SBW) if m < 32 else
                      (evac_dt(m, ps, bi, c0, SBW) if m == 66 else
                       evac_xbc(m, ps, bi, c0, SBW))))

        tap("dtpre", lambda: dt_pre.ap(), [64, T], F32, [(sV, sV.n)])
        tap("xbcscr", lambda: xbc_scr, [34 * 128, TF], BF16,
            [(sST, sST.n)])
        tap("silzscr", lambda: silz_scr, [32 * 128, T], BF16,
            [(sST, sST.n)])
        for _e in ("t", "v", "a", "p"):
            kb.mile(_e, sXC)
        for _e in ("t", "v", "a", "p", "y"):
            kb.wait(_e, sXC)
            kb.wait(_e, sST)
        stk1.close()
        stk2 = ExitStack()
        y32w = kb.sb("y32w", [128, 32, CHUNK], BF16, stack=stk2)
        co = kb.sb("co", [128, 34, CHUNK], BF16, stack=stk2)
        xbcin = [kb.sb(f"xbcin{i}", [128, 10, CHUNK + HALO], BF16, stack=stk2)
                 for i in range(2)]
        zstr = [kb.sb(f"zstr{i}", [128, CHUNK], BF16, stack=stk2)
                for i in range(2)]
        cvA = kb.sb("cvA", [128, 5, CHUNK], F32, stack=stk2)
        cvB = kb.sb("cvB", [128, 5, CHUNK], F32, stack=stk2)
        Ssb = kb.sb("Ssb", [128, 2, CHUNK], BF16, stack=stk2)
        SmaskT = kb.sb("SmaskT", [128, 2, CHUNK], BF16, stack=stk2)
        btok = kb.sb("btok", [128, 2, 128], BF16, stack=stk2)
        xT = kb.sb("xT", [128, 2, 128], BF16, stack=stk2)
        xd = kb.sb("xd", [128, 2, 128], BF16, stack=stk2)
        u_bf = kb.sb("u_bf", [128, 2, 128], BF16, stack=stk2)
        dt_c = kb.sb("dt_c", [64, CHUNK], F32, stack=stk2)
        dta = kb.sb("dta", [64, CHUNK], F32, stack=stk2)
        dtb = kb.sb("dtb", [64, CHUNK], F32, stack=stk2)
        dtv = kb.sb("dtv", [64, CHUNK], F32, stack=stk2)
        acs = kb.sb("acs", [64, CHUNK], F32, stack=stk2)
        acsT = kb.sb("acsT", [128, 2, 64], F32, stack=stk2)
        dtvT = kb.sb("dtvT", [128, 2, 64], F32, stack=stk2)
        negT = kb.sb("negT", [128, 2, 64], F32, stack=stk2)
        eLrow = kb.sb("eLrow", [1, 64], F32, stack=stk2)
        arow = kb.sb("arow", [1, CHUNK], F32, stack=stk2)
        arow_hi = kb.sb("arow_hi", [1, CHUNK], F32, stack=stk2)
        Larg = kb.sb("Larg", [128, CHUNK], F32, stack=stk2)
        SL = kb.sb("SL", [128, 2, CHUNK], BF16, stack=stk2)
        uw = kb.sb("uw", [128, 2, 128], BF16, stack=stk2)
        xdY = kb.sb("xdY", [128, 2, 128], BF16, stack=stk2)
        onesf_row = kb.sb("onesf_row", [1, 128], F32, stack=stk2)
        xD = kb.sb("xD", [128, CHUNK], F32, stack=stk2)
        yt = kb.sb("yt", [128, CHUNK], F32, stack=stk2)
        y2s = kb.sb("y2s", [128, CHUNK], BF16, stack=stk2)
        stw = kb.sb("stw", [128, 128], F32, stack=stk2)
        rst2_bfr = kb.sb("rst2_bfr", [1, CHUNK], BF16, stack=stk2)
        rst2_bc = kb.sb("rst2_bc", [128, CHUNK], F32, stack=stk2)
        kb.wait("p", sV, sV.n)
        kb.wait("p", sA, sA.n)

        def rx_state(E):
            with E.register("mh") as mh, E.register("fv") as fv, \
                 E.register("nr") as nr, E.register("pc") as pc, \
                 E.register("nv") as nv:
                E.reg_load(mh, flags_sb[0:1, 0:1])
                E.reg_load(nv, flags_sb[0:1, 1:2])
                with E.If_eq(mh, 1):
                    E.reg_mov(pc, 0)

                    def poll():
                        E.dma_start(out=flagrx[:1, :],
                                    in_=sh_flag[0:1, :]).then_inc(sPL.h, 16)
                        E.reg_add(pc, pc, 16)
                        E.wait_ge(sPL.h, pc)
                        E.reg_load(fv, flagrx[0:1, 0:1])
                        E.reg_sub(nr, fv, nv)

                    poll()
                    with E.While(nr):
                        poll()
                    E.dma_start(out=state[:, :],
                                in_=sh_state[0:128, :]).then_inc(sPL.h, 16)
                    E.reg_add(pc, pc, 16)
                    E.wait_ge(sPL.h, pc)
            E.tensor_copy(state_bf[:, :], state[:, :])
            return E.drain()

        kb.op("p", rx_state, inc=sEX)
        for e in ("t", "v"):
            kb.wait(e, sEX)

        wbufL = wbuf
        for c in range(NCHUNK):
            tok0 = c * CHUNK
            kb.wait("a", sT, sT.n)
            kb.wait("a", sV, sV.n)
            kb.wait("v", sT, sT.n)
            kb.wait("p", sT, sT.n)
            kb.wait("p", sV, sV.n)
            kb.wait("p", sA, sA.n)
            groups = [(0, 10), (10, 10), (20, 10), (30, 4)]

            def issue_group(gi):
                g0, gn = groups[gi]
                par = gi % 2
                war_wait("y", xbcin[par])
                kb.dma("y", xbcin[par][:, :gn, :],
                       xbc_v[:, g0:g0 + gn, tok0:tok0 + CHUNK + HALO], sLD)
                return (g0, gn, par, sLD.n)

            gq = [issue_group(0), issue_group(1)]
            for gi in range(len(groups)):
                (g0, gn, par, ldv) = gq[gi]
                if gi + 2 < len(groups):
                    pass
                kb.wait("p", sLD, ldv)
                for sub in range(0, gn, 5):
                    sn = min(5, gn - sub)
                    kb.wait("p", sA, sA.n)

                    def wbc(col):
                        return convw_sb[:, col:col + 1].broadcast_to(
                            [128, CHUNK])

                    for j in range(sn):
                        kb.op("p", lambda E, par=par, sub=sub, j=j, g0=g0:
                              E.tensor_mul(
                                  cvA[:, j, :],
                                  xbcin[par][:, sub + j, 0:CHUNK],
                                  wbc((g0 + sub + j) * 4)))
                    kb.mile("p", sP)
                    for i in range(1, 4):
                        for j in range(sn):
                            kb.op("p", lambda E, par=par, sub=sub, j=j, g0=g0,
                                  i=i: E.tensor_mul(
                                      cvB[:, j, :],
                                      xbcin[par][:, sub + j, i:i + CHUNK],
                                      wbc((g0 + sub + j) * 4 + i)))
                        kb.mile("p", sP)
                        for j in range(sn):
                            kb.op("p", lambda E, j=j: E.tensor_add(
                                cvA[:, j, :], cvA[:, j, :], cvB[:, j, :]))
                        kb.mile("p", sP)
                    kb.wait("a", sP)
                    for j in range(sn):
                        kb.op("a", lambda E, g0=g0, sub=sub, j=j: E.activation(
                            co[:, g0 + sub + j, :], cvA[:, j, :], AF.Silu,
                            bias=convb_sb[:, g0 + sub + j:g0 + sub + j + 1]))
                    kb.mile("a", sA)
                war_set(xbcin[par], sP)
                if gi + 2 < len(groups):
                    gq.append(issue_group(gi + 2))
            co_ready = sA.n

            kb.wait("a", sV, sV.n)
            kb.op("a", lambda E, tok0=tok0: E.activation(
                dta[:, :], dt_pre[0:64, tok0:tok0 + CHUNK], AF.Abs,
                bias=dtbias_sb[:, 0:1]))
            kb.op("a", lambda E, tok0=tok0: E.activation(
                dt_c[:, :], dt_pre[0:64, tok0:tok0 + CHUNK], AF.Identity,
                bias=dtbias_sb[:, 0:1]))
            kb.mile("a", sA)
            kb.wait("a", sA)
            kb.op("a", lambda E: E.activation(dtb[:, :], dta[:, :], AF.Exp,
                                              scale=-1.0))
            kb.mile("a", sA)
            kb.wait("a", sA)
            kb.op("a", lambda E: E.activation(dtb[:, :], dtb[:, :], AF.Ln,
                                              bias=1.0))
            kb.mile("a", sA)
            kb.wait("v", sA)
            kb.op("v", lambda E: E.tensor_scalar(dta[:, :], dt_c[:, :], 0.0,
                                                 None, ALU.max))
            kb.mile("v", sV)
            kb.wait("v", sV)
            kb.op("v", lambda E: E.tensor_add(dtv[:, :], dta[:, :], dtb[:, :]))
            kb.mile("v", sV)
            kb.wait("v", sV)
            kb.op("v", lambda E: E.tensor_scalar(dta[:, :], dtv[:, :],
                                                 Acol[:, 0:1], None, ALU.mult))
            kb.mile("v", sV)
            kb.wait("v", sV)
            kb.op("v", lambda E: E.tensor_tensor_scan(
                acs[:, :], dta[:, :], zeros64[:, :], 0.0, ALU.add, ALU.add))
            kb.mile("v", sV)
            pm = B.acq(5)
            kb.wait("t", sV)
            for lt in range(2):
                kb.op("t", lambda E, lt=lt, pm=pm: E.transpose(
                    pm[:, lt * 64:lt * 64 + 64],
                    acs[0:64, lt * 128:lt * 128 + 128], identf[0:64, 0:64]))
                kb.op("t", lambda E, lt=lt, pm=pm: E.transpose(
                    pm[:, 128 + lt * 64:128 + lt * 64 + 64],
                    dtv[0:64, lt * 128:lt * 128 + 128], identf[0:64, 0:64]))
            kb.mile("t", sT)
            kb.wait("v", sT)
            for lt in range(2):
                kb.op("v", lambda E, lt=lt, pm=pm: E.tensor_copy(
                    acsT[:, lt, :], pm[:, lt * 64:lt * 64 + 64]))
                kb.op("v", lambda E, lt=lt, pm=pm: E.tensor_copy(
                    dtvT[:, lt, :], pm[:, 128 + lt * 64:128 + lt * 64 + 64]))
            kb.mile("v", sV)
            B.rel(5, sV)
            kb.wait("a", sV)
            for lt in range(2):
                kb.op("a", lambda E, lt=lt, c=c: E.activation(
                    expT[:, c, lt, :], acsT[:, lt, :], AF.Exp))
            kb.mile("a", sA)
            pm = B.acq(5)
            kb.wait("t", sV)
            kb.op("t", lambda E, pm=pm: E.transpose(
                pm[0:1, 0:64], acs[0:64, 255:256], identf[0:64, 0:64]))
            kb.mile("t", sT)
            kb.wait("a", sT)
            kb.op("a", lambda E, pm=pm: E.activation(
                eLrow[:1, :], pm[0:1, 0:64], AF.Copy))
            kb.mile("a", sA)
            kb.wait("p", sA)
            kb.op("p", lambda E: E.memset(onesf_row[:1, :], 1.0))
            kb.mile("p", sP)
            kb.wait("t", sA)
            kb.wait("t", sP)
            kb.op("t", lambda E, pm=pm: E.matmul(
                pm[:, 64:128], onesf_row[:1, :], eLrow[:1, :],
                start=True, stop=True))
            kb.mile("t", sT)
            kb.wait("v", sT)
            kb.op("v", lambda E, pm=pm, c=c: E.tensor_copy(
                acsLbc[:, c, :], pm[:, 64:128]))
            kb.mile("v", sV)
            B.rel(5, sV)
            kb.wait("a", sV)
            kb.op("a", lambda E, c=c: E.activation(
                eLbc[:, c, :], acsLbc[:, c, :], AF.Exp))
            kb.mile("a", sA)
            kb.wait("v", sA)
            for lt in range(2):
                kb.op("v", lambda E, lt=lt, c=c: E.tensor_sub(
                    negT[:, lt, :], acsLbc[:, c, :], acsT[:, lt, :]))
            kb.mile("v", sV)
            kb.wait("a", sV)
            for lt in range(2):
                kb.op("a", lambda E, lt=lt: E.activation(
                    negT[:, lt, :], negT[:, lt, :], AF.Exp))
            kb.mile("a", sA)
            kb.wait("v", sA)
            for lt in range(2):
                kb.op("v", lambda E, lt=lt, c=c: E.tensor_mul(
                    endtT[:, c, lt, :], negT[:, lt, :], dtvT[:, lt, :]))
            kb.mile("v", sV)

            ps_s = B.acq(5)
            kb.wait("t", sA, co_ready)
            for lt in range(2):
                kb.op("t", lambda E, lt=lt, ps=ps_s: E.matmul(
                    ps[:, lt * 256:lt * 256 + 256],
                    co[:, 33, lt * 128:lt * 128 + 128], co[:, 32, :],
                    start=True, stop=True))
            kb.mile("t", sT)
            kb.wait("a", sT)
            for lt in range(2):
                kb.op("a", lambda E, lt=lt, ps=ps_s: E.activation(
                    Ssb[:, lt, :], ps[:, lt * 256:lt * 256 + 256], AF.Copy))
            kb.mile("a", sA)
            B.rel(5, sA)
            ps_t = B.acq(5)
            pt_bf = ps_t.ap().bitcast(BF16)
            kb.wait("t", sA)
            for st in range(2):
                for lt in range(2):
                    kb.op("t", lambda E, st=st, lt=lt, pt=pt_bf: E.transpose(
                        pt[:, st * 512 + lt * 128:st * 512 + lt * 128 + 128],
                        Ssb[:, lt, st * 128:st * 128 + 128], ident[:, :]))
            kb.mile("t", sT)
            kb.wait("v", sT)
            kb.wait("v", sP, sP.n)
            for st in range(2):
                kb.op("v", lambda E, st=st, pt=pt_bf: E.tensor_copy(
                    SmaskT[:, st, :], pt[:, st * 512:st * 512 + 256]))
            kb.mile("v", sV)
            B.rel(5, sV)
            kb.wait("p", sV)
            for st in range(2):
                kb.op("p", lambda E, st=st: E.affine_select(
                    SmaskT[:, st, :], SmaskT[:, st, :], [[1, CHUNK]],
                    ALU.is_ge, 0.0, base=-st * 128, channel_multiplier=-1))
            kb.mile("p", sP)
            ps_b = B.acq(5)
            pb_bf = ps_b.ap().bitcast(BF16)
            kb.wait("t", sP)
            for lt in range(2):
                kb.op("t", lambda E, lt=lt, pb=pb_bf: E.transpose(
                    pb[:, lt * 128:lt * 128 + 128],
                    co[:, 32, lt * 128:lt * 128 + 128], ident[:, :]))
            kb.mile("t", sT)
            kb.wait("v", sT)
            for lt in range(2):
                kb.op("v", lambda E, lt=lt, pb=pb_bf: E.tensor_copy(
                    btok[:, lt, :], pb[:, lt * 128:lt * 128 + 128]))
            kb.mile("v", sV)
            B.rel(5, sV)

            tap(f"co{c}", lambda: co.ap(), [128, 34, CHUNK], BF16,
                [(sA, co_ready)])
            tap(f"acs{c}", lambda: acs.ap(), [64, CHUNK], F32, [(sV, sV.n)])

            kb.dma("y", zstr[0][:, :], silz_v[:, 0, tok0:tok0 + CHUNK], sLD)
            zq = [sLD.n]

            for hp in range(32):
                h0 = 2 * hp
                if hp + 1 < 32:
                    par = (hp + 1) % 2
                    war_wait("y", zstr[par])
                    kb.dma("y", zstr[par][:, :],
                           silz_v[:, hp + 1, tok0:tok0 + CHUNK], sLD)
                    zq.append(sLD.n)
                ps_x = B.acq(6)
                px_bf = ps_x.ap().bitcast(BF16)
                kb.wait("t", sA, co_ready)
                for lt in range(2):
                    kb.op("t", lambda E, lt=lt, hp=hp, px=px_bf: E.transpose(
                        px[:, lt * 128:lt * 128 + 128],
                        co[:, hp, lt * 128:lt * 128 + 128], ident[:, :]))
                kb.mile("t", sT)
                kb.wait("v", sT)
                kb.wait("v", sP, sP.n)
                for lt in range(2):
                    kb.op("v", lambda E, lt=lt, px=px_bf: E.tensor_copy(
                        xT[:, lt, :], px[:, lt * 128:lt * 128 + 128]))
                kb.mile("v", sV)
                B.rel(6, sV)
                kb.wait("p", sV)
                kb.wait("p", sT, sT.n)
                for lt in range(2):
                    kb.op("p", lambda E, lt=lt, hp=hp, c=c, h0=h0:
                          E.tensor_tensor(
                              xdY[:, lt, :].rearrange("p (h q) -> p h q", h=2),
                              xT[:, lt, :].rearrange("p (h q) -> p h q", h=2),
                              dtvT[:, lt, h0:h0 + 2].unsqueeze(2)
                              .broadcast_to([128, 2, 64]),
                              ALU.mult))
                    kb.op("p", lambda E, lt=lt, hp=hp, c=c, h0=h0:
                          E.tensor_tensor(
                              xd[:, lt, :].rearrange("p (h q) -> p h q", h=2),
                              xT[:, lt, :].rearrange("p (h q) -> p h q", h=2),
                              endtT[:, c, lt, h0:h0 + 2].unsqueeze(2)
                              .broadcast_to([128, 2, 64]),
                              ALU.mult))
                kb.mile("p", sP)
                ps_y = B.acq(7)
                for hh in range(2):
                    h = h0 + hh
                    pm5 = B.acq(5)
                    kb.wait("t", sV, sV.n)
                    for lt in range(2):
                        kb.op("t", lambda E, lt=lt, h=h, pm=pm5: E.transpose(
                            pm[0:1, lt * 128:lt * 128 + 128],
                            acsT[:, lt, h:h + 1], identf[:, :]))
                    kb.mile("t", sT)
                    kb.wait("a", sT)
                    kb.op("a", lambda E, pm=pm5: E.activation(
                        arow[:1, :], pm[0:1, 0:CHUNK], AF.Copy))
                    kb.mile("a", sA)
                    kb.wait("t", sA)
                    kb.op("t", lambda E, pm=pm5: E.matmul(
                        pm[:, 256:512], onesf_row[:1, :], arow[:1, :],
                        start=True, stop=True))
                    kb.mile("t", sT)
                    kb.wait("v", sT)
                    war_wait("v", Larg)
                    for st in range(2):
                        kb.op("v", lambda E, st=st, h=h, pm=pm5:
                              E.tensor_scalar(
                                  Larg[:, :], pm[:, 256:512],
                                  acsT[:, st, h:h + 1], 0.0,
                                  ALU.subtract, ALU.min))
                        kb.mile("v", sV)
                        kb.wait("a", sV)
                        kb.op("a", lambda E, st=st: E.activation(
                            SL[:, st, :], Larg[:, :], AF.Exp))
                        kb.mile("a", sA)
                        kb.wait("v", sA)
                    war_set(Larg, sA, sA.n)
                    B.rel(5, sV)
                    kb.wait("p", sA)
                    kb.wait("p", sT, sT.n)
                    for st in range(2):
                        kb.op("p", lambda E, st=st: E.tensor_mul(
                            SL[:, st, :], SL[:, st, :], SmaskT[:, st, :]))
                    kb.mile("p", sP)
                    kb.wait("t", sP)
                    for lt in range(2):
                        for st in range(2):
                            kb.op("t", lambda E, lt=lt, st=st, hh=hh,
                                  ps=ps_y: E.matmul(
                                      ps[:, lt * 128 + hh * 64:
                                         lt * 128 + hh * 64 + 64],
                                      SL[:, st, lt * 128:lt * 128 + 128],
                                      xdY[:, st, hh * 64:hh * 64 + 64],
                                      start=(st == 0), stop=(st == 1)))
                    kb.mile("t", sT)
                kb.wait("t", sV, sV.n)
                for lt in range(2):
                    kb.op("t", lambda E, lt=lt, hp=hp, ps=ps_y: E.matmul(
                        ps[:, 256 + lt * 128:256 + lt * 128 + 128],
                        co[:, 33, lt * 128:lt * 128 + 128],
                        state_bf[:, hp * 128:hp * 128 + 128],
                        start=True, stop=True))
                kb.mile("t", sT)
                kb.wait("v", sT)
                for lt in range(2):
                    for hh in range(2):
                        kb.op("v", lambda E, lt=lt, hh=hh, c=c, h0=h0,
                              ps=ps_y: E.tensor_scalar(
                                  uw[:, lt, hh * 64:hh * 64 + 64],
                                  ps[:, 256 + lt * 128 + hh * 64:
                                     256 + lt * 128 + hh * 64 + 64],
                                  expT[:, c, lt, h0 + hh:h0 + hh + 1],
                                  None, ALU.mult))
                kb.mile("v", sV)
                kb.wait("v", sV)
                for lt in range(2):
                    kb.op("v", lambda E, lt=lt, ps=ps_y: E.tensor_add(
                        u_bf[:, lt, :], ps[:, lt * 128:lt * 128 + 128],
                        uw[:, lt, :]))
                kb.mile("v", sV)
                B.rel(7, sV)
                ps_c = B.acq(6)
                kb.wait("t", sV)
                for lt in range(2):
                    kb.op("t", lambda E, lt=lt, ps=ps_c: E.matmul(
                        ps[:, 0:128], btok[:, lt, :], xd[:, lt, :],
                        start=(lt == 0), stop=(lt == 1)))
                kb.mile("t", sT)
                kb.wait("v", sT)
                for hh in range(2):
                    kb.op("v", lambda E, hh=hh, hp=hp, c=c, h0=h0:
                          E.tensor_scalar(
                              stw[:, hh * 64:hh * 64 + 64],
                              state[:, hp * 128 + hh * 64:
                                    hp * 128 + hh * 64 + 64],
                              eLbc[:, c, h0 + hh:h0 + hh + 1], None,
                              ALU.mult))
                kb.mile("v", sV)
                kb.wait("v", sV)
                kb.op("v", lambda E, hp=hp, ps=ps_c: E.tensor_add(
                    state[:, hp * 128:hp * 128 + 128], stw[:, :],
                    ps[:, 0:128]))
                kb.mile("v", sV)
                B.rel(6, sV)
                kb.wait("v", sV)
                kb.op("v", lambda E, hp=hp: E.tensor_copy(
                    state_bf[:, hp * 128:hp * 128 + 128],
                    state[:, hp * 128:hp * 128 + 128]))
                kb.mile("v", sV)
                ps_f = B.acq(6)
                pf_bf = ps_f.ap().bitcast(BF16)
                kb.wait("t", sV)
                for lt in range(2):
                    kb.op("t", lambda E, lt=lt, pf=pf_bf: E.transpose(
                        pf[:, lt * 128:lt * 128 + 128], u_bf[:, lt, :],
                        ident[:, :]))
                kb.mile("t", sT)
                kb.wait("v", sT)
                kb.wait("v", sA, sA.n)
                kb.wait("v", sLD, zq[hp])
                kb.op("v", lambda E, hp=hp: E.tensor_scalar(
                    xD[:, :], co[:, hp, :], Dbc_sb[:, hp:hp + 1], None,
                    ALU.mult))
                kb.mile("v", sV)
                kb.wait("v", sV)
                kb.op("v", lambda E, pf=pf_bf: E.tensor_add(
                    yt[:, :], pf[:, 0:256], xD[:, :]))
                kb.mile("v", sV)
                B.rel(6, sV)
                kb.wait("v", sV)
                kb.op("v", lambda E, hp=hp: E.tensor_mul(
                    yt[:, :], yt[:, :], zstr[hp % 2][:, :]))
                kb.mile("v", sV)
                war_set(zstr[hp % 2], sV)
                kb.wait("a", sV)
                kb.op("a", lambda E: E.activation(y2s[:, :], yt[:, :],
                                                  AF.Square))
                kb.mile("a", sA)
                ps_v = B.acq(2) if hp == 0 else B.b[2]
                kb.wait("t", sA)
                kb.op("t", lambda E, hp=hp, ps=ps_v: E.matmul(
                    ps[0:1, 0:CHUNK], ones_col[:, :], y2s[:, :],
                    start=(hp == 0), stop=(hp == 31)), inc=sT)
                kb.wait("v", sV)
                kb.op("v", lambda E, hp=hp: E.tensor_scalar(
                    y32w[:, hp, :], yt[:, :], normw_sb[:, hp:hp + 1],
                    None, ALU.mult))
                kb.mile("v", sV)

            tap(f"y32w{c}", lambda: y32w.ap(), [128, 32, CHUNK], BF16,
                [(sV, sV.n)])
            tap(f"state{c}", lambda: state.ap(), [128, NH * HD], F32,
                [(sV, sV.n)])

            kb.wait("a", sT)
            kb.op("a", lambda E: E.activation(
                row_a[:1, 0:CHUNK], B.b[2][0:1, 0:CHUNK], AF.Copy,
                scale=1.0 / DIN))
            kb.mile("a", sA)
            B.rel(2, sA)
            kb.wait("a", sA)
            kb.op("a", lambda E: E.activation(row_a[:1, 0:CHUNK],
                                              row_a[:1, 0:CHUNK], AF.Sqrt,
                                              bias=EPS))
            kb.mile("a", sA)
            kb.wait("v", sA)
            kb.op("v", lambda E: E.reciprocal(row_b[:1, 0:CHUNK],
                                              row_a[:1, 0:CHUNK]))
            kb.mile("v", sV)
            kb.wait("v", sV)
            kb.op("v", lambda E: E.tensor_copy(rst2_bfr[:1, :],
                                               row_b[:1, 0:CHUNK]))
            kb.mile("v", sV)
            pm2 = B.acq(2)
            kb.wait("t", sV)
            kb.op("t", lambda E, pm=pm2: E.matmul(
                pm[:, 0:CHUNK], ones_row[:1, :], rst2_bfr[:1, :],
                start=True, stop=True))
            kb.mile("t", sT)
            kb.wait("v", sT)
            kb.op("v", lambda E, pm=pm2: E.tensor_copy(rst2_bc[:, :],
                                                       pm[:, 0:CHUNK]))
            kb.mile("v", sV)
            B.rel(2, sV)

            col0 = c * CHUNK + HALO
            w0 = sW.n
            wload(WoaT, 0, 128, wbufL, 0, krange=(0, 16))
            for mi in range(16):
                wload(WoaT, mi * 128, 128, wbufL, 1, krange=(16, 32))
                bi = mi % 2
                ps = B.acq(bi)
                kb.wait("t", sW, w0 + (2 * mi + 2) * 16)
                for k in range(32):
                    par = 0 if k < 16 else 1
                    kb.op("t", lambda E, k=k, ps=ps, par=par: E.matmul(
                        ps[:, 0:CHUNK], wbufL[par][:, k % 16, :],
                        y32w[:, k, :], start=(k == 0), stop=(k == 31)))
                kb.mile("t", sT)
                wdone(wbufL, 0)
                wdone(wbufL, 1)
                if mi + 1 < 16:
                    wload(WoaT, (mi + 1) * 128, 128, wbufL, 0, krange=(0, 16))
                kb.wait("v", sT)
                war_wait("v", yt)
                kb.op("v", lambda E, ps=ps: E.tensor_mul(
                    yt[:, :], ps[:, 0:CHUNK], rst2_bc[:, :]))
                kb.mile("v", sV)
                B.rel(bi, sV)
                war_set(yt, sV, sV.n + 1)
                kb.wait("v", sV)
                kb.op("v", lambda E, mi=mi, col0=col0: E.tensor_add(
                    x_fm[:, mi, col0:col0 + CHUNK],
                    x_fm[:, mi, col0:col0 + CHUNK], yt[:, :]))
                kb.mile("v", sV)

        kb.wait("p", sV, sV.n)

        def tx_state(E):
            with E.register("txmh") as mh, E.register("txpc") as pc:
                E.reg_load(mh, flags_sb[0:1, 0:1])
                E.reg_mov(pc, 0)
                with E.If_eq(mh, 0):
                    E.dma_start(out=sh_state[0:128, :],
                                in_=state[:, :]).then_inc(sPL2.h, 16)
                    E.reg_add(pc, pc, 16)
                    E.wait_ge(sPL2.h, pc)
                    E.dma_start(out=sh_flag[0:1, :],
                                in_=nonce_sb[:1, :]).then_inc(sPL2.h, 16)
                with E.Else():
                    E.dma_start(out=sh_state[128:256, :],
                                in_=state[:, :]).then_inc(sPL2.h, 16)
                    E.reg_add(pc, pc, 16)
                    E.wait_ge(sPL2.h, pc)
                    E.dma_start(out=sh_flag[1:2, :],
                                in_=nonce_sb[:1, :]).then_inc(sPL2.h, 16)
            return E.drain()

        kb.op("p", tx_state, inc=sEX)
        stk2.close()
        stk.close()

    emit_attn_a()

    def emit_ffn(tag):
        stkf = ExitStack()
        wb = [kb.sb(f"wf{tag}{i}", [128, KCH, 128], BF16, stack=stkf)
              for i in range(2)]
        xcf = kb.sb(f"xcf{tag}", [128, KCH, SBW], BF16, stack=stkf)
        h1sb = kb.sb(f"h1{tag}", [128, 32, SBW], BF16, stack=stkf)
        h2sb = kb.sb(f"h2{tag}", [128, 32, SBW], BF16, stack=stkf)
        t1 = kb.sb(f"t1{tag}", [128, SBW], F32, stack=stkf)

        for sbi in range(2):
            c0 = HALO + sbi * SBW
            emit_ln(c0, SBW, xcf)
            emit_xc(c0, SBW, xcf)
            kb.wait("t", sV)
            w0 = sW.n
            par = 0
            wload(W1T, 0, 128, wb, 0)
            for mi in range(32):
                if mi + 1 < 32:
                    wload(W1T, (mi + 1) * 128, 128, wb, 1 - par)
                bi = mi % 2
                ps = B.acq(bi)
                kb.wait("t", sW, w0 + (mi + 1) * 16)
                for k in range(KCH):
                    kb.op("t", lambda E, k=k, ps=ps, par=par: E.matmul(
                        ps[:, :], wb[par][:, k, :], xcf[:, k, :],
                        start=(k == 0), stop=(k == KCH - 1)))
                kb.mile("t", sT)
                wdone(wb, par)
                kb.wait("v", sT)
                war_wait("v", t1)
                kb.op("v", lambda E, ps=ps: E.tensor_mul(
                    t1[:, :], ps[:, :], rstd_bc[:, :SBW]))
                kb.mile("v", sV)
                B.rel(bi, sV)
                kb.wait("a", sV)
                war_set(t1, sA, sA.n + 1)
                kb.op("a", lambda E, mi=mi: E.activation(
                    h1sb[:, mi, :], t1[:, :], AF.Relu,
                    bias=b1_sb[:, mi:mi + 1]))
                kb.mile("a", sA)
                par = 1 - par
            kb.wait("t", sA)
            w0 = sW.n
            wload(W2T, 0, 128, wb, 0, krange=(0, 16))
            for mi in range(32):
                wload(W2T, mi * 128, 128, wb, 1, krange=(16, 32))
                bi = mi % 2
                ps = B.acq(bi)
                kb.wait("t", sW, w0 + (2 * mi + 2) * 16)
                for k in range(32):
                    par = 0 if k < 16 else 1
                    kb.op("t", lambda E, k=k, ps=ps, par=par: E.matmul(
                        ps[:, :], wb[par][:, k % 16, :], h1sb[:, k, :],
                        start=(k == 0), stop=(k == 31)))
                kb.mile("t", sT)
                wdone(wb, 0)
                wdone(wb, 1)
                if mi + 1 < 32:
                    wload(W2T, (mi + 1) * 128, 128, wb, 0, krange=(0, 16))
                kb.wait("a", sT)
                kb.op("a", lambda E, mi=mi, ps=ps: E.activation(
                    h2sb[:, mi, :], ps[:, :], AF.Relu,
                    bias=b2_sb[:, mi:mi + 1]))
                kb.mile("a", sA)
                B.rel(bi, sA)
            kb.wait("t", sA)
            w0 = sW.n
            wload(W3T, 0, 128, wb, 0, krange=(0, 16))
            for mi in range(16):
                wload(W3T, mi * 128, 128, wb, 1, krange=(16, 32))
                bi = mi % 2
                ps = B.acq(bi)
                kb.wait("t", sW, w0 + (2 * mi + 2) * 16)
                for k in range(32):
                    par = 0 if k < 16 else 1
                    kb.op("t", lambda E, k=k, ps=ps, par=par: E.matmul(
                        ps[:, :], wb[par][:, k % 16, :], h2sb[:, k, :],
                        start=(k == 0), stop=(k == 31)))
                kb.mile("t", sT)
                wdone(wb, 0)
                wdone(wb, 1)
                if mi + 1 < 16:
                    wload(W3T, (mi + 1) * 128, 128, wb, 0, krange=(0, 16))
                kb.wait("a", sT)
                war_wait("a", t1)
                kb.op("a", lambda E, mi=mi, ps=ps: E.activation(
                    t1[:, :], ps[:, :], AF.Identity,
                    bias=b3_sb[:, mi:mi + 1]))
                kb.mile("a", sA)
                B.rel(bi, sA)
                kb.wait("v", sA)
                war_set(t1, sV, sV.n + 1)
                kb.op("v", lambda E, mi=mi, c0=c0: E.tensor_add(
                    x_fm[:, mi, c0:c0 + SBW], x_fm[:, mi, c0:c0 + SBW],
                    t1[:, :]))
                kb.mile("v", sV)
        stkf.close()

    if build_upto >= 1:
        emit_ffn("A")

    def emit_attn_b():
        stkb = ExitStack()
        wb = [kb.sb(f"wb{i}", [128, KCH, 128], BF16, stack=stkb)
              for i in range(2)]
        k_full = kb.sb("k_full", [128, KVH, 2 * T], BF16, stack=stkb)
        vtok = kb.sb("vtok", [128, KVH, 16, 128], BF16, stack=stkb)
        q_bf = kb.sb("q_bf", [128, HB, T], BF16, stack=stkb)
        qsc_bc = kb.sb("qsc_bc", [128, SBW], F32, stack=stkb)
        qrow = kb.sb("qrow", [1, SBW], F32, stack=stkb)
        qrow_bf = kb.sb("qrow_bf", [1, SBW], BF16, stack=stkb)
        negmax = kb.sb("negmax", [128, 1], F32, stack=stkb)
        rowsum = kb.sb("rowsum", [128, 1], F32, stack=stkb)
        rsum_i = kb.sb("rsum_i", [128, 1], F32, stack=stkb)
        stkv = ExitStack()
        v_fm = kb.sb("v_fm", [128, KVH, 2 * T], BF16, stack=stkv)
        xcb = kb.sb("xcb", [128, KCH, SBW], BF16, stack=stkv)
        astr = [kb.sb(f"astr{i}", [128, SBW], BF16, stack=stkv)
                for i in range(2)]

        myt0 = None

        for sbi in range(2):
            c0 = HALO + sbi * SBW
            t0 = sbi * SBW
            emit_ln(c0, SBW, xcb)
            emit_xc(c0, SBW, xcb)
            ps_cw = B.acq(3)
            for k in range(KCH):
                par = k % 2
                kb.wait("a", sV)
                war_wait("a", astr[par])
                kb.op("a", lambda E, k=k, par=par: E.activation(
                    astr[par][:, :], xcb[:, k, :], AF.Abs))
                kb.mile("a", sA)
                kb.wait("t", sA)
                kb.op("t", lambda E, k=k, par=par, ps=ps_cw: E.matmul(
                    ps[0:1, :], ones_col[:, :], astr[par][:, :],
                    start=(k == 0), stop=(k == KCH - 1)))
                kb.mile("t", sT)
                war_set(astr[par], sT)
            kb.wait("a", sT)
            kb.op("a", lambda E, ps=ps_cw: E.activation(
                qrow[:1, :], ps[0:1, :], AF.Copy))
            kb.mile("a", sA)
            B.rel(3, sA)
            kb.wait("v", sA)
            kb.op("v", lambda E: E.tensor_mul(qrow[:1, :], qrow[:1, :],
                                              rstd_row[:1, 0:SBW]))
            kb.mile("v", sV)
            kb.wait("a", sV)
            kb.op("a", lambda E: E.activation(qrow[:1, :], qrow[:1, :],
                                              AF.Sigmoid))
            kb.mile("a", sA)
            kb.wait("v", sA)
            kb.op("v", lambda E: E.tensor_scalar(
                qrow[:1, :], qrow[:1, :], float(ASCALE), None, ALU.mult))
            kb.mile("v", sV)
            kb.wait("v", sV)
            kb.op("v", lambda E: E.tensor_mul(qrow[:1, :], qrow[:1, :],
                                              rstd_row[:1, 0:SBW]))
            kb.mile("v", sV)
            kb.wait("v", sV)
            kb.op("v", lambda E: E.tensor_copy(qrow_bf[:1, :], qrow[:1, :]))
            kb.mile("v", sV)
            pb = B.acq(3)
            kb.wait("t", sV)
            kb.op("t", lambda E, pb=pb: E.matmul(
                pb[:, :], ones_row[:1, :], qrow_bf[:1, :],
                start=True, stop=True))
            kb.mile("t", sT)
            kb.wait("v", sT)
            kb.op("v", lambda E, pb=pb: E.tensor_copy(qsc_bc[:, :], pb[:, :]))
            kb.mile("v", sV)
            B.rel(3, sV)

            kb.wait("t", sV)
            w0 = sW.n
            par = 0
            wload(WkvT, 0, 128, wb, 0)
            for mi in range(8):
                if mi + 1 < 8:
                    wload(WkvT, (mi + 1) * 128, 128, wb, 1 - par)
                bi = mi % 2
                ps = B.acq(bi)
                kb.wait("t", sW, w0 + (mi + 1) * 16)
                for k in range(KCH):
                    kb.op("t", lambda E, k=k, ps=ps, par=par: E.matmul(
                        ps[:, :], wb[par][:, k, :], xcb[:, k, :],
                        start=(k == 0), stop=(k == KCH - 1)))
                kb.mile("t", sT)
                wdone(wb, par)
                kb.wait("v", sT)

                def evkv(E, mi=mi, ps=ps, t0=t0, rn=f"kvh{sbi}_{mi}"):
                    with E.register(rn) as mh:
                        E.reg_load(mh, flags_sb[0:1, 0:1])
                        dst = k_full if mi < 4 else v_fm
                        mm = mi % 4
                        with E.If_eq(mh, 0):
                            E.tensor_tensor(dst[:, mm, t0:t0 + SBW],
                                            ps[:, :], rstd_bc[:, 0:SBW],
                                            ALU.mult)
                        with E.Else():
                            E.tensor_tensor(dst[:, mm, T + t0:T + t0 + SBW],
                                            ps[:, :], rstd_bc[:, 0:SBW],
                                            ALU.mult)
                    return E.drain()
                kb.op("v", evkv, inc=sV)
                B.rel(bi, sV)
                par = 1 - par

            kb.wait("t", sV)
            w0 = sW.n
            par = 0
            wload(WqT, 0, 128, wb, 0)
            for mi in range(16):
                if mi + 1 < 16:
                    wload(WqT, (mi + 1) * 128, 128, wb, 1 - par)
                bi = mi % 2
                ps = B.acq(bi)
                kb.wait("t", sW, w0 + (mi + 1) * 16)
                for k in range(KCH):
                    kb.op("t", lambda E, k=k, ps=ps, par=par: E.matmul(
                        ps[:, :], wb[par][:, k, :], xcb[:, k, :],
                        start=(k == 0), stop=(k == KCH - 1)))
                kb.mile("t", sT)
                wdone(wb, par)
                kb.wait("v", sT)
                kb.op("v", lambda E, mi=mi, ps=ps, t0=t0: E.tensor_mul(
                    q_bf[:, mi, t0:t0 + SBW], ps[:, :], qsc_bc[:, :]))
                kb.mile("v", sV)
                B.rel(bi, sV)
                par = 1 - par

        shkv_v = sh_kv.rearrange("(o m p) t -> p o m t", p=128, o=2)
        kb.wait("p", sV, sV.n)

        def kv_xch(E):
            with E.register("xmh") as mh, E.register("xfv") as fv, \
                 E.register("xnr") as nr, E.register("xpc") as pc, \
                 E.register("xnv") as nv:
                E.reg_load(mh, flags_sb[0:1, 0:1])
                E.reg_load(nv, flags_sb[0:1, 1:2])
                E.reg_mov(pc, 0)
                with E.If_eq(mh, 0):
                    E.dma_start(out=shkv_v[:, 0, 0:4, :],
                                in_=k_full[:, :, 0:T]).then_inc(sPL3.h, 16)
                    E.dma_start(out=shkv_v[:, 0, 4:8, :],
                                in_=v_fm[:, :, 0:T]).then_inc(sPL3.h, 16)
                    E.reg_add(pc, pc, 32)
                    E.wait_ge(sPL3.h, pc)
                    E.dma_start(out=sh_flag[2:3, :],
                                in_=nonce_sb[:1, :]).then_inc(sPL3.h, 16)
                    E.reg_add(pc, pc, 16)

                    def poll0():
                        E.dma_start(out=flagrx[:1, :],
                                    in_=sh_flag[3:4, :]).then_inc(sPL3.h, 16)
                        E.reg_add(pc, pc, 16)
                        E.wait_ge(sPL3.h, pc)
                        E.reg_load(fv, flagrx[0:1, 0:1])
                        E.reg_sub(nr, fv, nv)
                    poll0()
                    with E.While(nr):
                        poll0()
                    E.dma_start(out=k_full[:, :, T:2 * T],
                                in_=shkv_v[:, 1, 0:4, :]).then_inc(sPL3.h, 16)
                    E.dma_start(out=v_fm[:, :, T:2 * T],
                                in_=shkv_v[:, 1, 4:8, :]).then_inc(sPL3.h, 16)
                    E.reg_add(pc, pc, 32)
                    E.wait_ge(sPL3.h, pc)
                with E.Else():
                    E.dma_start(out=shkv_v[:, 1, 0:4, :],
                                in_=k_full[:, :, T:2 * T]).then_inc(sPL3.h, 16)
                    E.dma_start(out=shkv_v[:, 1, 4:8, :],
                                in_=v_fm[:, :, T:2 * T]).then_inc(sPL3.h, 16)
                    E.reg_add(pc, pc, 32)
                    E.wait_ge(sPL3.h, pc)
                    E.dma_start(out=sh_flag[3:4, :],
                                in_=nonce_sb[:1, :]).then_inc(sPL3.h, 16)
                    E.reg_add(pc, pc, 16)

                    def poll1():
                        E.dma_start(out=flagrx[:1, :],
                                    in_=sh_flag[2:3, :]).then_inc(sPL3.h, 16)
                        E.reg_add(pc, pc, 16)
                        E.wait_ge(sPL3.h, pc)
                        E.reg_load(fv, flagrx[0:1, 0:1])
                        E.reg_sub(nr, fv, nv)
                    poll1()
                    with E.While(nr):
                        poll1()
                    E.dma_start(out=k_full[:, :, 0:T],
                                in_=shkv_v[:, 0, 0:4, :]).then_inc(sPL3.h, 16)
                    E.dma_start(out=v_fm[:, :, 0:T],
                                in_=shkv_v[:, 0, 4:8, :]).then_inc(sPL3.h, 16)
                    E.reg_add(pc, pc, 32)
                    E.wait_ge(sPL3.h, pc)
            return E.drain()

        kb.op("p", kv_xch, inc=sEX)
        for e in ("t", "v", "a"):
            kb.wait(e, sEX)

        for kv in range(KVH):
            for g in range(4):
                ps_t = B.acq(4 + (g % 2))
                pt_bf = ps_t.ap().bitcast(BF16)
                for jt4 in range(4):
                    jt = g * 4 + jt4
                    kb.op("t", lambda E, kv=kv, jt=jt, jt4=jt4, pt=pt_bf:
                          E.transpose(pt[:, jt4 * 128:jt4 * 128 + 128],
                                      v_fm[:, kv, jt * 128:jt * 128 + 128],
                                      ident[:, :]))
                kb.mile("t", sT)
                kb.wait("v", sT)
                for jt4 in range(4):
                    jt = g * 4 + jt4
                    kb.op("v", lambda E, kv=kv, jt=jt, jt4=jt4, pt=pt_bf:
                          E.tensor_copy(vtok[:, kv, jt, :],
                                        pt[:, jt4 * 128:jt4 * 128 + 128]))
                kb.mile("v", sV)
                B.rel(4 + (g % 2), sV)
        stkv.close()

        stkh = ExitStack()
        P_bf = kb.sb("P_bf", [128, 2, 2 * T], BF16, stack=stkh)
        PT_bf = kb.sb("PT_bf", [128, 16, 256], BF16, stack=stkh)
        S_sb = kb.sb("S_sb", [128, 2 * T], F32, stack=stkh)
        dbase = kb.sb("dbase", [128, 2, 2 * T], F32, stack=stkh)
        band01 = kb.sb("band01", [128, 2 * T], BF16, stack=stkh)
        aostr = kb.sb("aostr", [128, 256], BF16, stack=stkh)

        for itp in range(4):
            for sl in range(2):
                it = 2 * itp + sl

                def mk_dbase(E, it=it, sl=sl, rn=f"dmh{it}"):
                    with E.register(rn) as mh:
                        E.reg_load(mh, flags_sb[0:1, 0:1])
                        with E.If_eq(mh, 0):
                            E.iota(dbase[:, sl, :], [[1, 2 * T]],
                                   base=-(it * 128), channel_multiplier=-1,
                                   allow_small_or_imprecise_dtypes=True)
                        with E.Else():
                            E.iota(dbase[:, sl, :], [[1, 2 * T]],
                                   base=-(T + it * 128),
                                   channel_multiplier=-1,
                                   allow_small_or_imprecise_dtypes=True)
                        E.memset(band01[:, :], 1.0)
                    return E.drain()
                kb.wait("p", sA, sA.n)
                kb.wait("p", sV, sV.n)
                kb.op("p", mk_dbase, inc=sP)
                kb.wait("a", sP)
                kb.op("a", lambda E, sl=sl: E.activation(
                    dbase[:, sl, :], dbase[:, sl, :], AF.Abs))
                kb.mile("a", sA)

                def mk_band(E, it=it, rn=f"bmh{it}"):
                    with E.register(rn) as mh:
                        E.reg_load(mh, flags_sb[0:1, 0:1])
                        for half in range(2):
                            base_i = half * T + it * 128
                            with (E.If_eq(mh, 0) if half == 0 else E.Else()):
                                E.affine_select(band01[:, :], band01[:, :],
                                                [[1, 2 * T]], ALU.is_ge, 0.0,
                                                base=-base_i + 256,
                                                channel_multiplier=-1)
                                E.drain()
                                E.affine_select(band01[:, :], band01[:, :],
                                                [[-1, 2 * T]], ALU.is_ge, 0.0,
                                                base=base_i + 256,
                                                channel_multiplier=1)
                    return E.drain()
                kb.op("p", mk_band, inc=sP)
                kb.wait("v", sP)
                kb.wait("v", sA)
                kb.op("v", lambda E, sl=sl: E.scalar_tensor_tensor(
                    dbase[:, sl, :], band01[:, :], 4.0e9, dbase[:, sl, :],
                    ALU.mult, ALU.add))
                kb.mile("v", sV)
                kb.wait("v", sV)
                kb.op("v", lambda E, sl=sl: E.memset(dbase[:, sl, 0:64],
                                                     4.0e9))
                kb.mile("v", sV)
            dbase_ready = sV.n

            for h in range(HB):
                kv = h // 4
                for sl in range(2):
                    it = 2 * itp + sl
                    kb.wait("v", sA, sA.n)
                    for js in range(4):
                        bi = js % 2
                        ps = B.acq(bi)
                        kb.wait("t", sEX)
                        kb.wait("t", sV, dbase_ready)
                        kb.op("t", lambda E, h=h, it=it, js=js, kv=kv, ps=ps:
                              E.matmul(ps[:, :],
                                       q_bf[:, h, it * 128:it * 128 + 128],
                                       k_full[:, kv, js * 512:js * 512 + 512],
                                       start=True, stop=True))
                        kb.mile("t", sT)
                        kb.wait("v", sT)
                        kb.op("v", lambda E, h=h, js=js, sl=sl, ps=ps:
                              E.scalar_tensor_tensor(
                                  S_sb[:, js * 512:js * 512 + 512],
                                  dbase[:, sl, js * 512:js * 512 + 512],
                                  slopes_sb[:, h:h + 1], ps[:, :],
                                  ALU.mult, ALU.add))
                        kb.mile("v", sV)
                        B.rel(bi, sV)
                    kb.wait("v", sV)
                    kb.op("v", lambda E: E.reduce_max(
                        negmax[:, :], S_sb[:, :], axis=mybir.AxisListType.X,
                        negate=True))
                    kb.mile("v", sV)
                    kb.wait("a", sV)
                    kb.op("a", lambda E, sl=sl: E.activation(
                        P_bf[:, sl, :], S_sb[:, :], AF.Exp,
                        bias=negmax[:, 0:1], accum_out=rowsum[:, 0:1]))
                    kb.mile("a", sA)
                    kb.wait("v", sA)
                    if it == 0:
                        def fix_glob(E, sl=sl, rn=f"gmh{h}"):
                            with E.register(rn) as mh:
                                E.reg_load(mh, flags_sb[0:1, 0:1])
                                with E.If_eq(mh, 0):
                                    E.memset(P_bf[0:64, sl, :], 1.0)
                                    E.memset(rowsum[0:64, 0:1], 2048.0)
                            return E.drain()
                        kb.op("v", fix_glob, inc=sV)
                        kb.wait("v", sV)
                    kb.op("v", lambda E: E.reciprocal(rsum_i[:, :],
                                                      rowsum[:, :]))
                    kb.mile("v", sV)
                    kb.wait("v", sV)
                    kb.op("v", lambda E, sl=sl: E.tensor_scalar(
                        P_bf[:, sl, :], P_bf[:, sl, :], rsum_i[:, 0:1], None,
                        ALU.mult))
                    kb.mile("v", sV)

                for jt in range(16):
                    ps_t = B.acq(4 + (jt % 2))
                    pt_bf = ps_t.ap().bitcast(BF16)
                    kb.wait("t", sV, sV.n)
                    for i4 in range(2):
                        kb.op("t", lambda E, jt=jt, i4=i4, pt=pt_bf:
                              E.transpose(
                                  pt[:, i4 * 128:i4 * 128 + 128],
                                  P_bf[:, i4, jt * 128:jt * 128 + 128],
                                  ident[:, :]))
                    kb.mile("t", sT)
                    kb.wait("v", sT)
                    kb.op("v", lambda E, jt=jt, pt=pt_bf: E.tensor_copy(
                        PT_bf[:, jt, :], pt[:, 0:256]))
                    kb.mile("v", sV)
                    B.rel(4 + (jt % 2), sV)
                ps_av = B.acq(6)
                kb.wait("t", sV)
                for jt in range(16):
                    kb.op("t", lambda E, jt=jt, kv=kv, ps=ps_av: E.matmul(
                        ps[:, 0:256], vtok[:, kv, jt, :], PT_bf[:, jt, :],
                        start=(jt == 0), stop=(jt == 15)))
                kb.mile("t", sT)
                kb.wait("v", sT)
                war_wait("v", aostr)
                kb.op("v", lambda E, ps=ps_av: E.tensor_copy(
                    aostr[:, :], ps[:, 0:256]))
                kb.mile("v", sV)
                B.rel(6, sV)
                kb.wait("y", sV)
                kb.dma("y", ao_v[:, h, itp * 256:itp * 256 + 256],
                       aostr[:, :], sST)
                war_set(aostr, sST)
        stkh.close()

        tap("aoscr", lambda: ao_scr, [HB * HDB, T], BF16, [(sST, sST.n)])
        tap("kfull", lambda: k_full.ap(), [128, KVH, 2 * T], BF16,
            [(sEX, sEX.n)])
        tap("qbf", lambda: q_bf.ap(), [128, HB, T], BF16, [(sV, sV.n)])
        stko = ExitStack()
        aosb = kb.sb("aosb", [128, HB, T], BF16, stack=stko)
        t1b = kb.sb("t1b", [128, SBW], F32, stack=stko)
        kb.wait("y", sST)
        kb.dma("y", aosb.ap(), ao_v, sLD)
        for sbi in range(2):
            t0 = sbi * SBW
            c0 = HALO + t0
            w0 = sW.n
            wload(WobT, 0, 128, wb, 0)
            for mi in range(16):
                if mi + 1 < 16:
                    wload(WobT, (mi + 1) * 128, 128, wb, 1 - (mi % 2))
                bi = mi % 2
                ps = B.acq(bi)
                kb.wait("t", sW, w0 + (mi + 1) * 16)
                kb.wait("t", sLD)
                for k in range(HB):
                    kb.op("t", lambda E, k=k, ps=ps, par=mi % 2, t0=t0:
                          E.matmul(ps[:, :], wb[par][:, k, :],
                                   aosb[:, k, t0:t0 + SBW],
                                   start=(k == 0), stop=(k == HB - 1)))
                kb.mile("t", sT)
                wdone(wb, mi % 2)
                kb.wait("a", sT)
                war_wait("a", t1b)
                kb.op("a", lambda E, mi=mi, ps=ps: E.activation(
                    t1b[:, :], ps[:, :], AF.Identity,
                    bias=bo_sb[:, mi:mi + 1]))
                kb.mile("a", sA)
                B.rel(bi, sA)
                kb.wait("v", sA)
                war_set(t1b, sV, sV.n + 1)
                kb.op("v", lambda E, mi=mi, c0=c0: E.tensor_add(
                    x_fm[:, mi, c0:c0 + SBW], x_fm[:, mi, c0:c0 + SBW],
                    t1b[:, :]))
                kb.mile("v", sV)
            w0 = sW.n
        stko.close()
        stkb.close()

    if build_upto >= 2:
        emit_attn_b()
    if build_upto >= 3:
        emit_ffn("B")


    for e in ("v", "a", "p", "t"):
        kb.mile(e, sXC)
    kb.wait("y", sXC)
    kb.dma("y", out_fm.rearrange("(k p) t -> p k t", p=128),
           x_fm.ap()[:, :, HALO:], sST)
    kb.wait("y", sST)

    with nc.Block() as block:
        @block.tensor
        def _(E):
            for f in kb.cl["t"]:
                f(E)

        @block.vector
        def _(E):
            for f in kb.cl["v"]:
                f(E)

        @block.scalar
        def _(E):
            for f in kb.cl["a"]:
                f(E)

        @block.gpsimd
        def _(E):
            for f in kb.cl["p"]:
                f(E)

        @block.sync
        def _(E):
            for f in kb.cl["y"]:
                f(E)

    kb.es.close()
    return nc, tap_names



def _prep_inputs(inputs):
    x = np.ascontiguousarray(np.asarray(inputs["L0_attn_a_hidden_states"],
                                        np.float32))
    f32 = np.float32

    def c(a):
        return np.ascontiguousarray(np.asarray(a, f32))

    Dp = np.asarray(inputs["L0_attn_a_D"], f32)
    shared = {
        "WinT": c(np.asarray(inputs["L0_attn_a_in_proj_w"]).T),
        "convw": c(np.asarray(inputs["L0_attn_a_conv_w"])
                   .reshape(34, 128, 4).transpose(1, 0, 2).reshape(128, 136)),
        "convb": c(np.asarray(inputs["L0_attn_a_conv_b"]).reshape(34, 128).T),
        "dtbias": c(np.asarray(inputs["L0_attn_a_dt_bias"]).reshape(64, 1)),
        "A_log": c(np.asarray(inputs["L0_attn_a_A_log"]).reshape(64, 1)),
        "D_bc": c(Dp.repeat(64).reshape(32, 128).T),
        "normw": c(np.asarray(inputs["L0_attn_a_norm_w"]).reshape(32, 128).T),
        "WoaT": c(np.asarray(inputs["L0_attn_a_out_proj_w"]).T),
        "W1T": c(np.asarray(inputs["L0_ffn_W1"]).T),
        "W2T": c(np.asarray(inputs["L0_ffn_W2"]).T),
        "W3T": c(np.asarray(inputs["L0_ffn_W3"]).T),
        "b1c": c(np.asarray(inputs["L0_ffn_b1"]).reshape(32, 128).T),
        "b2c": c(np.asarray(inputs["L0_ffn_b2"]).reshape(32, 128).T),
        "b3c": c(np.asarray(inputs["L0_ffn_b3"]).reshape(16, 128).T),
        "WqT": c(inputs["L0_attn_b_Wq"]),
        "WkvT": c(inputs["L0_attn_b_Wkv"]),
        "WobT": c(inputs["L0_attn_b_Wo"]),
        "boc": c(np.asarray(inputs["L0_attn_b_bo"]).reshape(16, 128).T),
    }
    cp = 2.0 ** np.floor(np.log2(HB))
    base = 2.0 ** (-(2.0 ** (-(np.log2(cp) - 3.0))))
    slopes = (base ** np.arange(1, HB + 1, dtype=np.float64)).astype(f32)
    shared["slopes_n"] = c(np.broadcast_to(-slopes[None, :], (128, HB)))

    rng = np.random.default_rng()
    nonce = np.uint32(rng.integers(1, 2**32 - 1, dtype=np.uint64))

    in_maps = []
    for cid in range(8):
        b, h = cid // 2, cid % 2
        xi = x[b]
        t0 = h * T
        halo = np.zeros((HALO, D), f32) if h == 0 else xi[t0 - HALO:t0]
        xslab = np.concatenate([halo, xi[t0:t0 + T]], axis=0)
        m = dict(shared)
        m["x_fm"] = c(xslab.T)
        m["flags"] = np.array([[h, nonce, 0, 0, 0, 0, 0, 0]], np.uint32)
        m["halfsc"] = np.full((128, 1), float(h), f32)
        in_maps.append(m)
    return in_maps


def kernel(**inputs):
    in_maps = _prep_inputs(inputs)
    nc, _ = build()
    res = run_bass_kernel_spmd(nc, in_maps, core_ids=list(range(8)))
    out = np.empty((4, 2048, 2048), np.float32)
    for cid in range(8):
        b, h = cid // 2, cid % 2
        out[b, h * T:(h + 1) * T, :] = res.results[cid]["out_fm"].T
    return out


# version 05
# speedup vs baseline: 1.2053x; 1.1574x over previous
import numpy as np
import ml_dtypes
import concourse.bass as bass
import concourse.mybir as mybir
from concourse.bass_utils import run_bass_kernel_spmd
from contextlib import ExitStack

F32 = mybir.dt.float32
BF16 = mybir.dt.bfloat16
U32 = mybir.dt.uint32
AF = mybir.ActivationFunctionType
ALU = mybir.AluOpType

T = 1024
HALO = 3
TF = T + HALO
D = 2048
KCH = 16
PROJ = 8512
DIN = 4096
NH = 64
HD = 64
CHUNK = 256
NCHUNK = 4
SBW = 512
EPS = 1e-5
HB = 16
KVH = 4
HDB = 128
ASCALE = float(1.0 / np.sqrt(128.0))

ENGS = ("t", "v", "a", "p", "y")


class Sem:
    def __init__(self, es, nc, name):
        self.h = es.enter_context(nc.semaphore(name))
        self.n = 0


class KB:
    def __init__(self, nc):
        self.nc = nc
        self.es = ExitStack()
        self.cl = {e: [] for e in ENGS}
        self._i = 0

    def sem(self, name=None):
        self._i += 1
        return Sem(self.es, self.nc, name or f"s{self._i}")

    def sb(self, name, shape, dt, stack=None):
        return (stack or self.es).enter_context(
            self.nc.sbuf_tensor(name, shape, dt))

    def emit(self, e, fn):
        self.cl[e].append(fn)

    def op(self, e, fn, inc=None, k=1):
        if inc is None:
            d = self.esem.get(e)
            if d is None:
                self.emit(e, fn)
            else:
                d.n += 1
                self.emit(e, lambda E, fn=fn, s=d: fn(E).then_inc(s.h, 1))
        else:
            inc.n += k
            self.emit(e, lambda E, fn=fn, s=inc, kk=k: fn(E).then_inc(s.h, kk))

    def wait(self, e, sem, at=None):
        v = sem.n if at is None else at
        if v > 0:
            self.emit(e, lambda E, h=sem.h, v=v: E.wait_ge(h, v))

    def mile(self, e, sem, k=1):
        if self.esem.get(e) is sem:
            return
        sem.n += k
        self.emit(e, lambda E, h=sem.h, k=k: E.drain().then_inc(h, k))

    def dma(self, e, out, in_, sem, **kw):
        sem.n += 16
        self.emit(e, lambda E, o=out, i=in_, h=sem.h, kw=kw:
                  E.dma_start(out=o, in_=i, **kw).then_inc(h, 16))


class Banks:
    def __init__(self, kb):
        self.kb = kb
        self.b = [kb.es.enter_context(
            kb.nc.psum_tensor(f"bank{i}", [128, 512], F32)) for i in range(8)]
        self.tag = [None] * 8

    def acq(self, i, weng="t"):
        if self.tag[i] is not None:
            s, v = self.tag[i]
            self.kb.wait(weng, s, v)
            self.tag[i] = None
        return self.b[i]

    def rel(self, i, sem, at=None):
        self.tag[i] = (sem, sem.n if at is None else at)


def build(build_upto=3, taps=(), sim_local=False):
    nc = bass.Bass("TRN2", target_bir_lowering=False)
    kb = KB(nc)
    tap_req = set(taps)
    tap_names = []

    din = {}

    def dram_in(name, shape, dt=F32):
        din[name] = nc.dram_tensor(name, shape, dt, kind="ExternalInput").ap()
        return din[name]

    x_in = dram_in("x_fm", [D, TF])
    WinT = dram_in("WinT", [D, PROJ])
    dram_in("convw", [128, 34 * 4])
    dram_in("convb", [128, 34])
    dram_in("dtbias", [64, 1])
    dram_in("A_log", [64, 1])
    dram_in("D_bc", [128, 32])
    dram_in("normw", [128, 32])
    WoaT = dram_in("WoaT", [DIN, D])
    W1T = dram_in("W1T", [D, DIN])
    W2T = dram_in("W2T", [DIN, DIN])
    W3T = dram_in("W3T", [DIN, D])
    dram_in("b1c", [128, 32])
    dram_in("b2c", [128, 32])
    dram_in("b3c", [128, 16])
    WqT = dram_in("WqT", [D, D])
    WkvT = dram_in("WkvT", [D, 1024])
    WobT = dram_in("WobT", [D, D])
    dram_in("boc", [128, 16])
    dram_in("slopes_n", [128, HB])
    dram_in("flags", [1, 8], U32)
    dram_in("halfsc", [128, 1])

    out_fm = nc.dram_tensor("out_fm", [D, T], F32, kind="ExternalOutput").ap()
    ao_scr = nc.dram_tensor("ao_scr", [HB * HDB, T], BF16,
                            kind="Internal").ap()
    xbc_scr = nc.dram_tensor("xbc_scr", [34 * 128, TF], BF16,
                             kind="Internal").ap()
    silz_scr = nc.dram_tensor("silz_scr", [32 * 128, T], BF16,
                              kind="Internal").ap()
    h1_scr = nc.dram_tensor("h1_scr", [32 * 128, T], BF16,
                            kind="Internal").ap()
    sh_state = nc.dram_tensor("sh_state", [2 * 128, NH * HD], F32,
                              kind="Internal", addr_space="Shared").ap()
    sh_kv = nc.dram_tensor("sh_kv", [2 * 1024, T], BF16,
                           kind="Internal", addr_space="Shared").ap()
    sh_flag = nc.dram_tensor("sh_flag", [4, 16], U32,
                             kind="Internal", addr_space="Shared").ap()

    xbc_v = xbc_scr.rearrange("(m p) t -> p m t", p=128)
    silz_v = silz_scr.rearrange("(m p) t -> p m t", p=128)
    h1_v = h1_scr.rearrange("(m p) t -> p m t", p=128)
    ao_v = ao_scr.rearrange("(m p) t -> p m t", p=128)

    x_fm = kb.sb("x_fm_sb", [128, KCH, TF], F32)
    ident = kb.sb("ident", [128, 128], BF16)
    identf = kb.sb("identf", [128, 128], F32)
    ones_col = kb.sb("ones_col", [128, 1], BF16)
    ones_row = kb.sb("ones_row", [1, 128], BF16)
    convw_sb = kb.sb("convw_sb", [128, 34 * 4], F32)
    convb_sb = kb.sb("convb_sb", [128, 34], F32)
    dtbias_sb = kb.sb("dtbias_sb", [64, 1], F32)
    Acol = kb.sb("Acol", [64, 1], F32)
    Dbc_sb = kb.sb("Dbc_sb", [128, 32], F32)
    normw_sb = kb.sb("normw_sb", [128, 32], F32)
    b1_sb = kb.sb("b1_sb", [128, 32], F32)
    b2_sb = kb.sb("b2_sb", [128, 32], F32)
    b3_sb = kb.sb("b3_sb", [128, 16], F32)
    bo_sb = kb.sb("bo_sb", [128, 16], F32)
    slopes_sb = kb.sb("slopes_sb", [128, HB], F32)
    halfsc_sb = kb.sb("halfsc_sb", [128, 1], F32)
    flags_sb = kb.sb("flags_sb", [1, 8], U32)
    zeros64 = kb.sb("zeros64", [64, CHUNK], F32)
    eps_c = kb.sb("eps_c", [128, 1], F32)
    nc.const_aps.register(F32, EPS, eps_c[:, :])
    nonce_sb = kb.sb("nonce_sb", [1, 16], U32)
    flagrx = kb.sb("flagrx", [1, 16], U32)
    mean_bfr = kb.sb("mean_bfr", [1, SBW], BF16)
    rstd_row = kb.sb("rstd_row", [1, SBW], F32)
    rstd_bfr = kb.sb("rstd_bfr", [1, SBW], BF16)
    rstd_bc = kb.sb("rstd_bc", [128, SBW], F32)
    row_a = kb.sb("row_a", [1, SBW], F32)
    row_b = kb.sb("row_b", [1, SBW], F32)

    B = Banks(kb)

    sLD = kb.sem("sLD")
    sWp = [kb.sem("sW0"), kb.sem("sW1")]
    sT = kb.sem("sT")
    sV = kb.sem("sV")
    sA = kb.sem("sA")
    sP = kb.sem("sP")
    sST = kb.sem("sST")
    sSS = [kb.sem("sSS0"), kb.sem("sSS1")]
    sAO = kb.sem("sAO")
    sXB = [kb.sem("sXB0"), kb.sem("sXB1")]
    sZS = [kb.sem("sZS0"), kb.sem("sZS1")]
    sEX = kb.sem("sEX")
    sXC = kb.sem("sXC")
    kb.esem = {"t": sT, "v": sV, "a": sA, "p": sP}
    sPL = kb.sem("sPL")
    sPL2 = kb.sem("sPL2")
    sPL3 = kb.sem("sPL3")

    for name, dst in [("convw", convw_sb), ("convb", convb_sb),
                      ("dtbias", dtbias_sb), ("A_log", Acol),
                      ("D_bc", Dbc_sb), ("normw", normw_sb),
                      ("b1c", b1_sb), ("b2c", b2_sb), ("b3c", b3_sb),
                      ("boc", bo_sb), ("slopes_n", slopes_sb),
                      ("halfsc", halfsc_sb), ("flags", flags_sb)]:
        kb.dma("y", dst.ap(), din[name], sLD)
    kb.dma("y", x_fm.ap(), x_in.rearrange("(k p) t -> p k t", p=128), sLD)

    kb.op("p", lambda E: E.memset(ident[:, :], 1.0))
    kb.op("p", lambda E: E.memset(ones_col[:, :], 1.0))
    kb.op("p", lambda E: E.memset(ones_row[:, :], 1.0))
    kb.op("p", lambda E: E.memset(zeros64[:, :], 0.0))
    kb.op("p", lambda E: E.memset(eps_c[:, :], EPS))
    kb.wait("p", sP)
    kb.op("p", lambda E: E.affine_select(ident[:, :], ident[:, :], [[1, 128]],
                                         ALU.is_equal, 0.0, base=0,
                                         channel_multiplier=-1))
    kb.wait("p", sP)
    kb.op("p", lambda E: E.tensor_copy(identf[:, :], ident[:, :]))
    kb.wait("p", sLD)
    kb.op("p", lambda E: E.tensor_copy(
        nonce_sb[:1, :], flags_sb[0:1, 1:2].broadcast_to([1, 16])))
    kb.mile("p", sP)
    kb.wait("a", sLD)
    kb.op("a", lambda E: E.activation(Acol[:, :], Acol[:, :], AF.Exp))
    kb.mile("a", sA)
    kb.wait("v", sA)
    kb.op("v", lambda E: E.tensor_scalar(Acol[:, :], Acol[:, :], -1.0, None,
                                         ALU.mult))
    kb.mile("v", sV)

    _st = (sLD.n, sP.n, sV.n, sA.n)
    for e in ENGS:
        kb.wait(e, sLD, _st[0])
        kb.wait(e, sP, _st[1])
        kb.wait(e, sV, _st[2])
        kb.wait(e, sA, _st[3])

    def tap(name, ap_factory, shape, dt, waits):
        if name not in tap_req:
            return
        th = nc.dram_tensor("tap_" + name, list(shape), dt,
                            kind="ExternalOutput").ap()
        tap_names.append("tap_" + name)
        for (sm, v) in waits:
            kb.wait("y", sm, v)
        kb.dma("y", th, ap_factory(), sST)

    war = {}

    def war_wait(e, buf):
        t = war.get(id(buf))
        if t is not None:
            kb.wait(e, t[0], t[1])

    def war_set(buf, sem, at=None):
        war[id(buf)] = (sem, sem.n if at is None else at)

    def emit_ln(c0, n, xcbuf, rbase=0):
        kb.wait("p", sV, sV.n)
        kb.wait("p", sT, sT.n)
        for k in range(KCH):
            kb.op("p", lambda E, k=k, n=n, c0=c0: E.tensor_copy(
                xcbuf[:, k, :n], x_fm[:, k, c0:c0 + n]))
        kb.mile("p", sP)
        ps_s = B.acq(3)
        kb.wait("t", sP)
        for k in range(KCH):
            kb.op("t", lambda E, k=k, ps=ps_s, n=n: E.matmul(
                ps[0:1, :n], ones_col[:, :], xcbuf[:, k, :n],
                start=(k == 0), stop=(k == KCH - 1)))
        kb.mile("t", sT)
        kb.wait("a", sT)
        for k in range(KCH):
            kb.op("a", lambda E, k=k, n=n: E.activation(
                xcbuf[:, k, :n], xcbuf[:, k, :n], AF.Square))
        kb.mile("a", sA)
        ps_q = B.acq(4)
        kb.wait("t", sA)
        for k in range(KCH):
            kb.op("t", lambda E, k=k, ps=ps_q, n=n: E.matmul(
                ps[0:1, :n], ones_col[:, :], xcbuf[:, k, :n],
                start=(k == 0), stop=(k == KCH - 1)))
        kb.mile("t", sT)
        r0 = rbase
        kb.wait("a", sT)
        kb.op("a", lambda E, ps=ps_s, n=n, r0=r0: E.activation(
            row_a[:, r0:r0 + n], ps[0:1, :n], AF.Copy, scale=1.0 / D))
        kb.op("a", lambda E, ps=ps_s, n=n, r0=r0: E.activation(
            mean_bfr[:, r0:r0 + n], ps[0:1, :n], AF.Copy, scale=1.0 / D))
        kb.op("a", lambda E, ps=ps_q, n=n, r0=r0: E.activation(
            row_b[:, r0:r0 + n], ps[0:1, :n], AF.Copy, scale=1.0 / D))
        kb.mile("a", sA)
        B.rel(3, sA)
        B.rel(4, sA)
        kb.wait("a", sA)
        kb.op("a", lambda E, n=n, r0=r0: E.activation(
            row_a[:, r0:r0 + n], row_a[:, r0:r0 + n], AF.Square))
        kb.mile("a", sA)
        kb.wait("v", sA)
        kb.op("v", lambda E, n=n, r0=r0: E.tensor_sub(
            row_b[:, r0:r0 + n], row_b[:, r0:r0 + n], row_a[:, r0:r0 + n]))
        kb.mile("v", sV)
        kb.wait("a", sV)
        kb.op("a", lambda E, n=n, r0=r0: E.activation(
            row_b[:, r0:r0 + n], row_b[:, r0:r0 + n], AF.Sqrt, bias=EPS))
        kb.mile("a", sA)
        kb.wait("v", sA)
        kb.op("v", lambda E, n=n, r0=r0: E.reciprocal(
            rstd_row[:, r0:r0 + n], row_b[:, r0:r0 + n]))
        kb.mile("v", sV)
        kb.wait("v", sV)
        kb.op("v", lambda E, n=n, r0=r0: E.tensor_copy(
            rstd_bfr[:, r0:r0 + n], rstd_row[:, r0:r0 + n]))
        kb.mile("v", sV)
        pb = B.acq(2)
        kb.wait("t", sV)
        kb.op("t", lambda E, pb=pb, n=n, r0=r0: E.matmul(
            pb[:, :n], ones_row[:1, :], rstd_bfr[:1, r0:r0 + n],
            start=True, stop=True))
        kb.mile("t", sT)
        kb.wait("v", sT)
        kb.op("v", lambda E, pb=pb, n=n, r0=r0: E.tensor_copy(
            rstd_bc[:, r0:r0 + n], pb[:, :n]))
        kb.mile("v", sV)
        B.rel(2, sV)

    xcs_f32 = [kb.sb(f"xcs{i}", [128, SBW], F32) for i in range(2)]

    def emit_xc(c0, n, xcbuf, rbase=0):
        pb = B.acq(2)
        kb.wait("t", sV)
        kb.op("t", lambda E, pb=pb, n=n, rbase=rbase: E.matmul(
            pb[:, :n], ones_row[:1, :], mean_bfr[:1, rbase:rbase + n],
            start=True, stop=True))
        kb.mile("t", sT)
        kb.wait("v", sT)
        for k in range(KCH):
            par = k % 2
            if k >= 2:
                kb.wait("v", sV, sV.n - 1)
            kb.op("v", lambda E, k=k, pb=pb, n=n, c0=c0, par=par:
                  E.tensor_sub(xcs_f32[par][:, :n], x_fm[:, k, c0:c0 + n],
                               pb[:, :n]))
            kb.wait("v", sV)
            kb.op("v", lambda E, k=k, n=n, rbase=rbase, par=par:
                  E.tensor_mul(xcbuf[:, k, :n], xcs_f32[par][:, :n],
                               rstd_bc[:, rbase:rbase + n]))
        kb.mile("v", sV)
        B.rel(2, sV)

    wfree = {}
    wready = {}

    def wload(WT_ap, m0, mw, wbuf, par, krange=None):
        t = wfree.get(id(wbuf[par]))
        if t is not None:
            kb.wait("p", t[0], t[1])
        src = WT_ap.rearrange("(k p) m -> p k m", p=128)
        if krange is None:
            kcnt = src.shape[1]
            src = src[:, :, m0:m0 + mw]
        else:
            kcnt = krange[1] - krange[0]
            src = src[:, krange[0]:krange[1], m0:m0 + mw]
        sem = sWp[par]
        kb.wait("p", sem, sem.n)
        kb.dma("p", wbuf[par][:, :kcnt, :mw], src, sem)
        wready[id(wbuf[par])] = (sem, sem.n)

    def wwait(e, wbuf, par):
        s, v = wready[id(wbuf[par])]
        kb.wait(e, s, v)

    def wdone(wbuf, par):
        wfree[id(wbuf[par])] = (sT, sT.n)

    def emit_attn_a():
        stk = ExitStack()
        state = kb.sb("state", [128, NH * HD], F32, stack=stk)
        state_bf = kb.sb("state_bf", [128, NH * HD], BF16, stack=stk)
        dt_pre = kb.sb("dt_pre", [64, T], F32, stack=stk)
        expT = kb.sb("expT", [128, NCHUNK, 2, 64], F32, stack=stk)
        endtT = kb.sb("endtT", [128, NCHUNK, 2, 64], F32, stack=stk)
        eLbc = kb.sb("eLbc", [128, NCHUNK, 64], F32, stack=stk)
        acsLbc = kb.sb("acsLbc", [128, NCHUNK, 64], F32, stack=stk)
        kb.op("p", lambda E: E.memset(state[:, :], 0.0))
        kb.op("p", lambda E: E.memset(state_bf[:, :], 0.0))
        kb.mile("p", sP)
        stk1 = ExitStack()
        wbuf = [kb.sb(f"wa{i}", [128, KCH, 128], BF16, stack=stk1)
                for i in range(2)]
        xcbuf = kb.sb("xcbuf", [128, KCH, SBW], BF16, stack=stk1)
        strA = [kb.sb(f"strA{i}", [128, SBW], BF16, stack=stk1)
                for i in range(2)]
        zt1 = kb.sb("zt1", [128, SBW], F32, stack=stk1)
        xbchalo = kb.sb("xbchalo", [128, 34, HALO], BF16, stack=stk1)

        def evac_zpath(m, ps, bi, c0, n):
            t0 = c0 - HALO
            par = m % 2
            kb.wait("a", sT)
            war_wait("a", strA[par])
            kb.op("a", lambda E, ps=ps, par=par, n=n: E.activation(
                strA[par][:, :n], ps[:128, :n], AF.Silu))
            kb.mile("a", sA)
            B.rel(bi, sA)
            kb.wait("y", sA)
            kb.dma("y", silz_v[:, m, t0:t0 + n], strA[par][:, :n], sSS[par])
            war_set(strA[par], sSS[par])

        def evac_xbc(m, ps, bi, c0, n):
            par = m % 2
            kb.wait("a", sT)
            war_wait("a", strA[par])
            kb.op("a", lambda E, ps=ps, n=n, par=par: E.activation(
                strA[par][:, :n], ps[:128, :n], AF.Copy))
            kb.mile("a", sA)
            B.rel(bi, sA)
            kb.wait("y", sA)
            kb.dma("y", xbc_v[:, m - 32, c0:c0 + n], strA[par][:, :n],
                   sSS[par])
            war_set(strA[par], sSS[par])

        def evac_xbc_halo(m, ps, bi):
            kb.wait("a", sT)
            kb.op("a", lambda E, ps=ps, m=m: E.activation(
                xbchalo[:, m - 32, :], ps[:128, :HALO], AF.Copy))
            kb.mile("a", sA)
            B.rel(bi, sA)

        def evac_dt(m, ps, bi, c0, n):
            t0 = c0 - HALO
            kb.wait("a", sT)
            kb.op("a", lambda E, ps=ps, n=n, t0=t0: E.activation(
                dt_pre[:, t0:t0 + n], ps[0:64, :n], AF.Copy))
            kb.mile("a", sA)
            B.rel(bi, sA)

        def passA(c0, n, mlist, outfn):
            kb.wait("t", sV)
            par = 0
            wload(WinT, mlist[0] * 128, 64 if mlist[0] == 66 else 128,
                  wbuf, par)
            for mi, m in enumerate(mlist):
                if mi + 1 < len(mlist):
                    nm = mlist[mi + 1]
                    wload(WinT, nm * 128, 64 if nm == 66 else 128,
                          wbuf, 1 - par)
                mw = 64 if m == 66 else 128
                bi = mi % 2
                ps = B.acq(bi)
                wwait("t", wbuf, par)
                for k in range(KCH):
                    kb.op("t", lambda E, k=k, ps=ps, par=par, mw=mw, n=n:
                          E.matmul(ps[:mw, :n], wbuf[par][:, k, :mw],
                                   xcbuf[:, k, :n],
                                   start=(k == 0), stop=(k == KCH - 1)))
                kb.mile("t", sT)
                wdone(wbuf, par)
                outfn(m, ps, bi)
                par = 1 - par

        emit_ln(0, HALO, xcbuf)
        emit_xc(0, HALO, xcbuf)
        passA(0, HALO, list(range(32, 66)),
              lambda m, ps, bi: evac_xbc_halo(m, ps, bi))
        kb.wait("y", sA)
        kb.dma("y", xbc_v[:, :, 0:HALO], xbchalo.ap(), sST)

        for sbi in range(2):
            c0 = HALO + sbi * SBW
            emit_ln(c0, SBW, xcbuf)
            emit_xc(c0, SBW, xcbuf)
            passA(c0, SBW, list(range(0, 67)),
                  lambda m, ps, bi, c0=c0: (
                      evac_zpath(m, ps, bi, c0, SBW) if m < 32 else
                      (evac_dt(m, ps, bi, c0, SBW) if m == 66 else
                       evac_xbc(m, ps, bi, c0, SBW))))

        tap("dtpre", lambda: dt_pre.ap(), [64, T], F32,
            [(sV, sV.n), (sA, sA.n)])
        tap("xbcscr", lambda: xbc_scr, [34 * 128, TF], BF16,
            [(sST, sST.n)])
        tap("silzscr", lambda: silz_scr, [32 * 128, T], BF16,
            [(sST, sST.n)])
        for _e in ("t", "v", "a", "p"):
            kb.mile(_e, sXC)
        for _e in ("t", "v", "a", "p", "y"):
            kb.wait(_e, sXC)
            kb.wait(_e, sST)
            kb.wait(_e, sSS[0])
            kb.wait(_e, sSS[1])
        stk1.close()
        stk2 = ExitStack()
        y32w = kb.sb("y32w", [128, 32, CHUNK], BF16, stack=stk2)
        co = kb.sb("co", [128, 34, CHUNK], BF16, stack=stk2)
        xbcin = [kb.sb(f"xbcin{i}", [128, 10, CHUNK + HALO], BF16, stack=stk2)
                 for i in range(2)]
        zstr = [kb.sb(f"zstr{i}", [128, CHUNK], BF16, stack=stk2)
                for i in range(2)]
        cvA = kb.sb("cvA", [128, 5, CHUNK], F32, stack=stk2)
        cvB = kb.sb("cvB", [128, 5, CHUNK], F32, stack=stk2)
        Ssb = kb.sb("Ssb", [128, 2, CHUNK], BF16, stack=stk2)
        SmaskT = kb.sb("SmaskT", [128, 2, CHUNK], BF16, stack=stk2)
        btok = kb.sb("btok", [128, 2, 128], BF16, stack=stk2)
        xT = kb.sb("xT", [128, 2, 128], BF16, stack=stk2)
        xd = kb.sb("xd", [128, 2, 128], BF16, stack=stk2)
        u_bf = kb.sb("u_bf", [128, 2, 128], BF16, stack=stk2)
        dt_c = kb.sb("dt_c", [64, CHUNK], F32, stack=stk2)
        dta = kb.sb("dta", [64, CHUNK], F32, stack=stk2)
        dtb = kb.sb("dtb", [64, CHUNK], F32, stack=stk2)
        dtv = kb.sb("dtv", [64, CHUNK], F32, stack=stk2)
        acs = kb.sb("acs", [64, CHUNK], F32, stack=stk2)
        acsT = kb.sb("acsT", [128, 2, 64], F32, stack=stk2)
        dtvT = kb.sb("dtvT", [128, 2, 64], F32, stack=stk2)
        negT = kb.sb("negT", [128, 2, 64], F32, stack=stk2)
        eLrow = kb.sb("eLrow", [1, 64], F32, stack=stk2)
        arow = kb.sb("arow", [1, CHUNK], F32, stack=stk2)
        arow_hi = kb.sb("arow_hi", [1, CHUNK], F32, stack=stk2)
        Larg = kb.sb("Larg", [128, CHUNK], F32, stack=stk2)
        SL = kb.sb("SL", [128, 2, CHUNK], BF16, stack=stk2)
        uw = kb.sb("uw", [128, 2, 128], BF16, stack=stk2)
        xdY = kb.sb("xdY", [128, 2, 128], BF16, stack=stk2)
        onesf_row = kb.sb("onesf_row", [1, 128], F32, stack=stk2)
        xD = kb.sb("xD", [128, CHUNK], F32, stack=stk2)
        yt = kb.sb("yt", [128, CHUNK], F32, stack=stk2)
        y2s = kb.sb("y2s", [128, CHUNK], BF16, stack=stk2)
        stw = kb.sb("stw", [128, 128], F32, stack=stk2)
        rst2_bfr = kb.sb("rst2_bfr", [1, CHUNK], BF16, stack=stk2)
        rst2_bc = kb.sb("rst2_bc", [128, CHUNK], F32, stack=stk2)
        wbfull = [kb.sb(f"wafull{i}", [128, 32, 128], BF16, stack=stk2)
                  for i in range(2)]
        kb.wait("p", sV, sV.n)
        kb.wait("p", sA, sA.n)

        def rx_state(E):
            with E.register("mh") as mh, E.register("fv") as fv, \
                 E.register("nr") as nr, E.register("pc") as pc, \
                 E.register("nv") as nv:
                E.reg_load(mh, flags_sb[0:1, 0:1])
                E.reg_load(nv, flags_sb[0:1, 1:2])
                with E.If_eq(mh, 1):
                    E.reg_mov(pc, 0)

                    def poll():
                        E.dma_start(out=flagrx[:1, :],
                                    in_=sh_flag[0:1, :]).then_inc(sPL.h, 16)
                        E.reg_add(pc, pc, 16)
                        E.wait_ge(sPL.h, pc)
                        E.reg_load(fv, flagrx[0:1, 0:1])
                        E.reg_sub(nr, fv, nv)

                    poll()
                    with E.While(nr):
                        poll()
                    E.dma_start(out=state[:, :],
                                in_=sh_state[0:128, :]).then_inc(sPL.h, 16)
                    E.reg_add(pc, pc, 16)
                    E.wait_ge(sPL.h, pc)
            E.tensor_copy(state_bf[:, :], state[:, :])
            return E.drain()

        def rx_state_sim(E):
            E.tensor_copy(state_bf[:, :], state[:, :])
            return E.drain()

        kb.op("p", rx_state_sim if sim_local else rx_state, inc=sEX)
        for e in ("t", "v"):
            kb.wait(e, sEX)

        wbufL = wbuf
        for c in range(NCHUNK):
            tok0 = c * CHUNK
            kb.wait("a", sT, sT.n)
            kb.wait("a", sV, sV.n)
            kb.wait("v", sT, sT.n)
            kb.wait("v", sA, sA.n)
            kb.wait("t", sA, sA.n)
            kb.wait("p", sT, sT.n)
            kb.wait("p", sV, sV.n)
            kb.wait("p", sA, sA.n)
            groups = [(0, 10), (10, 10), (20, 10), (30, 4)]

            def issue_group(gi):
                g0, gn = groups[gi]
                par = gi % 2
                war_wait("y", xbcin[par])
                kb.dma("y", xbcin[par][:, :gn, :],
                       xbc_v[:, g0:g0 + gn, tok0:tok0 + CHUNK + HALO],
                       sXB[par])
                return (g0, gn, par, sXB[par].n)

            gq = [issue_group(0), issue_group(1)]
            for gi in range(len(groups)):
                (g0, gn, par, ldv) = gq[gi]
                if gi + 2 < len(groups):
                    pass
                kb.wait("p", sXB[par], ldv)
                for sub in range(0, gn, 5):
                    sn = min(5, gn - sub)
                    kb.wait("p", sA, sA.n)

                    def wbc(col):
                        return convw_sb[:, col:col + 1].broadcast_to(
                            [128, CHUNK])

                    for j in range(sn):
                        kb.op("p", lambda E, par=par, sub=sub, j=j, g0=g0:
                              E.tensor_mul(
                                  cvA[:, j, :],
                                  xbcin[par][:, sub + j, 0:CHUNK],
                                  wbc((g0 + sub + j) * 4)))
                    for i in range(1, 4):
                        kb.wait("p", sP, sP.n)
                        for j in range(sn):
                            kb.op("p", lambda E, par=par, sub=sub, j=j, g0=g0,
                                  i=i: E.tensor_mul(
                                      cvB[:, j, :],
                                      xbcin[par][:, sub + j, i:i + CHUNK],
                                      wbc((g0 + sub + j) * 4 + i)))
                        kb.wait("p", sP, sP.n)
                        for j in range(sn):
                            kb.op("p", lambda E, j=j: E.tensor_add(
                                cvA[:, j, :], cvA[:, j, :], cvB[:, j, :]))
                    kb.mile("p", sP)
                    kb.wait("a", sP)
                    for j in range(sn):
                        kb.op("a", lambda E, g0=g0, sub=sub, j=j: E.activation(
                            co[:, g0 + sub + j, :], cvA[:, j, :], AF.Silu,
                            bias=convb_sb[:, g0 + sub + j:g0 + sub + j + 1]))
                    kb.mile("a", sA)
                war_set(xbcin[par], sP)
                if gi + 2 < len(groups):
                    gq.append(issue_group(gi + 2))
            co_ready = sA.n

            kb.wait("a", sV, sV.n)
            kb.wait("a", sA, sA.n)
            kb.op("a", lambda E, tok0=tok0: E.activation(
                dta[:, :], dt_pre[0:64, tok0:tok0 + CHUNK], AF.Abs,
                bias=dtbias_sb[:, 0:1]))
            kb.op("a", lambda E, tok0=tok0: E.activation(
                dt_c[:, :], dt_pre[0:64, tok0:tok0 + CHUNK], AF.Identity,
                bias=dtbias_sb[:, 0:1]))
            kb.mile("a", sA)
            kb.wait("a", sA)
            kb.op("a", lambda E: E.activation(dtb[:, :], dta[:, :], AF.Exp,
                                              scale=-1.0))
            kb.mile("a", sA)
            kb.wait("a", sA)
            kb.op("a", lambda E: E.activation(dtb[:, :], dtb[:, :], AF.Ln,
                                              bias=1.0))
            kb.mile("a", sA)
            kb.wait("v", sA)
            kb.op("v", lambda E: E.tensor_scalar(dta[:, :], dt_c[:, :], 0.0,
                                                 None, ALU.max))
            kb.mile("v", sV)
            kb.wait("v", sV)
            kb.op("v", lambda E: E.tensor_add(dtv[:, :], dta[:, :], dtb[:, :]))
            kb.mile("v", sV)
            kb.wait("v", sV)
            kb.op("v", lambda E: E.tensor_scalar(dta[:, :], dtv[:, :],
                                                 Acol[:, 0:1], None, ALU.mult))
            kb.mile("v", sV)
            kb.wait("v", sV)
            kb.op("v", lambda E: E.tensor_tensor_scan(
                acs[:, :], dta[:, :], zeros64[:, :], 0.0, ALU.add, ALU.add))
            kb.mile("v", sV)
            pm = B.acq(5)
            kb.wait("t", sV)
            for lt in range(2):
                kb.op("t", lambda E, lt=lt, pm=pm: E.transpose(
                    pm[:, lt * 64:lt * 64 + 64],
                    acs[0:64, lt * 128:lt * 128 + 128], identf[0:64, 0:64]))
                kb.op("t", lambda E, lt=lt, pm=pm: E.transpose(
                    pm[:, 128 + lt * 64:128 + lt * 64 + 64],
                    dtv[0:64, lt * 128:lt * 128 + 128], identf[0:64, 0:64]))
            kb.mile("t", sT)
            kb.wait("v", sT)
            for lt in range(2):
                kb.op("v", lambda E, lt=lt, pm=pm: E.tensor_copy(
                    acsT[:, lt, :], pm[:, lt * 64:lt * 64 + 64]))
                kb.op("v", lambda E, lt=lt, pm=pm: E.tensor_copy(
                    dtvT[:, lt, :], pm[:, 128 + lt * 64:128 + lt * 64 + 64]))
            kb.mile("v", sV)
            B.rel(5, sV)
            kb.wait("a", sV)
            for lt in range(2):
                kb.op("a", lambda E, lt=lt, c=c: E.activation(
                    expT[:, c, lt, :], acsT[:, lt, :], AF.Exp))
            kb.mile("a", sA)
            pm = B.acq(5)
            kb.wait("t", sV)
            kb.op("t", lambda E, pm=pm: E.transpose(
                pm[0:1, 0:64], acs[0:64, 255:256], identf[0:64, 0:64]))
            kb.mile("t", sT)
            kb.wait("a", sT)
            kb.op("a", lambda E, pm=pm: E.activation(
                eLrow[:1, :], pm[0:1, 0:64], AF.Copy))
            kb.mile("a", sA)
            kb.wait("p", sA)
            kb.op("p", lambda E: E.memset(onesf_row[:1, :], 1.0))
            kb.mile("p", sP)
            kb.wait("t", sA)
            kb.wait("t", sP)
            kb.op("t", lambda E, pm=pm: E.matmul(
                pm[:, 64:128], onesf_row[:1, :], eLrow[:1, :],
                start=True, stop=True))
            kb.mile("t", sT)
            kb.wait("v", sT)
            kb.op("v", lambda E, pm=pm, c=c: E.tensor_copy(
                acsLbc[:, c, :], pm[:, 64:128]))
            kb.mile("v", sV)
            B.rel(5, sV)
            kb.wait("a", sV)
            kb.op("a", lambda E, c=c: E.activation(
                eLbc[:, c, :], acsLbc[:, c, :], AF.Exp))
            kb.mile("a", sA)
            kb.wait("v", sA)
            for lt in range(2):
                kb.op("v", lambda E, lt=lt, c=c: E.tensor_sub(
                    negT[:, lt, :], acsLbc[:, c, :], acsT[:, lt, :]))
            kb.mile("v", sV)
            kb.wait("a", sV)
            for lt in range(2):
                kb.op("a", lambda E, lt=lt: E.activation(
                    negT[:, lt, :], negT[:, lt, :], AF.Exp))
            kb.mile("a", sA)
            kb.wait("v", sA)
            for lt in range(2):
                kb.op("v", lambda E, lt=lt, c=c: E.tensor_mul(
                    endtT[:, c, lt, :], negT[:, lt, :], dtvT[:, lt, :]))
            kb.mile("v", sV)

            ps_s = B.acq(5)
            kb.wait("t", sA, co_ready)
            for lt in range(2):
                kb.op("t", lambda E, lt=lt, ps=ps_s: E.matmul(
                    ps[:, lt * 256:lt * 256 + 256],
                    co[:, 33, lt * 128:lt * 128 + 128], co[:, 32, :],
                    start=True, stop=True))
            kb.mile("t", sT)
            kb.wait("a", sT)
            for lt in range(2):
                kb.op("a", lambda E, lt=lt, ps=ps_s: E.activation(
                    Ssb[:, lt, :], ps[:, lt * 256:lt * 256 + 256], AF.Copy))
            kb.mile("a", sA)
            B.rel(5, sA)
            ps_t = B.acq(5)
            pt_bf = ps_t.ap().bitcast(BF16)
            kb.wait("t", sA)
            for st in range(2):
                for lt in range(2):
                    kb.op("t", lambda E, st=st, lt=lt, pt=pt_bf: E.transpose(
                        pt[:, st * 512 + lt * 128:st * 512 + lt * 128 + 128],
                        Ssb[:, lt, st * 128:st * 128 + 128], ident[:, :]))
            kb.mile("t", sT)
            kb.wait("v", sT)
            kb.wait("v", sP, sP.n)
            for st in range(2):
                kb.op("v", lambda E, st=st, pt=pt_bf: E.tensor_copy(
                    SmaskT[:, st, :], pt[:, st * 512:st * 512 + 256]))
            kb.mile("v", sV)
            B.rel(5, sV)
            kb.wait("p", sV)
            for st in range(2):
                kb.op("p", lambda E, st=st: E.affine_select(
                    SmaskT[:, st, :], SmaskT[:, st, :], [[1, CHUNK]],
                    ALU.is_ge, 0.0, base=-st * 128, channel_multiplier=-1))
            kb.mile("p", sP)
            ps_b = B.acq(5)
            pb_bf = ps_b.ap().bitcast(BF16)
            kb.wait("t", sP)
            for lt in range(2):
                kb.op("t", lambda E, lt=lt, pb=pb_bf: E.transpose(
                    pb[:, lt * 128:lt * 128 + 128],
                    co[:, 32, lt * 128:lt * 128 + 128], ident[:, :]))
            kb.mile("t", sT)
            kb.wait("v", sT)
            for lt in range(2):
                kb.op("v", lambda E, lt=lt, pb=pb_bf: E.tensor_copy(
                    btok[:, lt, :], pb[:, lt * 128:lt * 128 + 128]))
            kb.mile("v", sV)
            B.rel(5, sV)

            tap(f"co{c}", lambda: co.ap(), [128, 34, CHUNK], BF16,
                [(sA, co_ready)])
            tap(f"acs{c}", lambda: acs.ap(), [64, CHUNK], F32, [(sV, sV.n)])

            kb.dma("y", zstr[0][:, :], silz_v[:, 0, tok0:tok0 + CHUNK],
                   sZS[0])
            zq = [(sZS[0], sZS[0].n)]

            for hp in range(32):
                h0 = 2 * hp
                if hp + 1 < 32:
                    par = (hp + 1) % 2
                    war_wait("y", zstr[par])
                    kb.dma("y", zstr[par][:, :],
                           silz_v[:, hp + 1, tok0:tok0 + CHUNK], sZS[par])
                    zq.append((sZS[par], sZS[par].n))
                ps_x = B.acq(6)
                px_bf = ps_x.ap().bitcast(BF16)
                kb.wait("t", sA, co_ready)
                for lt in range(2):
                    kb.op("t", lambda E, lt=lt, hp=hp, px=px_bf: E.transpose(
                        px[:, lt * 128:lt * 128 + 128],
                        co[:, hp, lt * 128:lt * 128 + 128], ident[:, :]))
                kb.mile("t", sT)
                kb.wait("v", sT)
                kb.wait("v", sP, sP.n)
                for lt in range(2):
                    kb.op("v", lambda E, lt=lt, px=px_bf: E.tensor_copy(
                        xT[:, lt, :], px[:, lt * 128:lt * 128 + 128]))
                kb.mile("v", sV)
                B.rel(6, sV)
                kb.wait("p", sV)
                kb.wait("p", sT, sT.n)
                for lt in range(2):
                    kb.op("p", lambda E, lt=lt, hp=hp, c=c, h0=h0:
                          E.tensor_tensor(
                              xdY[:, lt, :].rearrange("p (h q) -> p h q", h=2),
                              xT[:, lt, :].rearrange("p (h q) -> p h q", h=2),
                              dtvT[:, lt, h0:h0 + 2].unsqueeze(2)
                              .broadcast_to([128, 2, 64]),
                              ALU.mult))
                    kb.op("p", lambda E, lt=lt, hp=hp, c=c, h0=h0:
                          E.tensor_tensor(
                              xd[:, lt, :].rearrange("p (h q) -> p h q", h=2),
                              xT[:, lt, :].rearrange("p (h q) -> p h q", h=2),
                              endtT[:, c, lt, h0:h0 + 2].unsqueeze(2)
                              .broadcast_to([128, 2, 64]),
                              ALU.mult))
                kb.mile("p", sP)
                ps_y = B.acq(7)
                for hh in range(2):
                    h = h0 + hh
                    pm5 = B.acq(5)
                    kb.wait("t", sV, sV.n)
                    for lt in range(2):
                        kb.op("t", lambda E, lt=lt, h=h, pm=pm5: E.transpose(
                            pm[0:1, lt * 128:lt * 128 + 128],
                            acsT[:, lt, h:h + 1], identf[:, :]))
                    kb.mile("t", sT)
                    kb.wait("a", sT)
                    kb.op("a", lambda E, pm=pm5: E.activation(
                        arow[:1, :], pm[0:1, 0:CHUNK], AF.Copy))
                    kb.mile("a", sA)
                    kb.wait("t", sA)
                    kb.op("t", lambda E, pm=pm5: E.matmul(
                        pm[:, 256:512], onesf_row[:1, :], arow[:1, :],
                        start=True, stop=True))
                    kb.mile("t", sT)
                    kb.wait("v", sT)
                    war_wait("v", Larg)
                    for st in range(2):
                        kb.op("v", lambda E, st=st, h=h, pm=pm5:
                              E.tensor_scalar(
                                  Larg[:, :], pm[:, 256:512],
                                  acsT[:, st, h:h + 1], 0.0,
                                  ALU.subtract, ALU.min))
                        kb.mile("v", sV)
                        kb.wait("a", sV)
                        kb.op("a", lambda E, st=st: E.activation(
                            SL[:, st, :], Larg[:, :], AF.Exp))
                        kb.mile("a", sA)
                        kb.wait("v", sA)
                    war_set(Larg, sA, sA.n)
                    B.rel(5, sV)
                    kb.wait("p", sA)
                    kb.wait("p", sT, sT.n)
                    for st in range(2):
                        kb.op("p", lambda E, st=st: E.tensor_mul(
                            SL[:, st, :], SL[:, st, :], SmaskT[:, st, :]))
                    kb.mile("p", sP)
                    kb.wait("t", sP)
                    for lt in range(2):
                        for st in range(2):
                            kb.op("t", lambda E, lt=lt, st=st, hh=hh,
                                  ps=ps_y: E.matmul(
                                      ps[:, lt * 128 + hh * 64:
                                         lt * 128 + hh * 64 + 64],
                                      SL[:, st, lt * 128:lt * 128 + 128],
                                      xdY[:, st, hh * 64:hh * 64 + 64],
                                      start=(st == 0), stop=(st == 1)))
                    kb.mile("t", sT)
                kb.wait("t", sV, sV.n)
                for lt in range(2):
                    kb.op("t", lambda E, lt=lt, hp=hp, ps=ps_y: E.matmul(
                        ps[:, 256 + lt * 128:256 + lt * 128 + 128],
                        co[:, 33, lt * 128:lt * 128 + 128],
                        state_bf[:, hp * 128:hp * 128 + 128],
                        start=True, stop=True))
                kb.mile("t", sT)
                kb.wait("v", sT)
                for lt in range(2):
                    for hh in range(2):
                        kb.op("v", lambda E, lt=lt, hh=hh, c=c, h0=h0,
                              ps=ps_y: E.tensor_scalar(
                                  uw[:, lt, hh * 64:hh * 64 + 64],
                                  ps[:, 256 + lt * 128 + hh * 64:
                                     256 + lt * 128 + hh * 64 + 64],
                                  expT[:, c, lt, h0 + hh:h0 + hh + 1],
                                  None, ALU.mult))
                kb.mile("v", sV)
                kb.wait("v", sV)
                for lt in range(2):
                    kb.op("v", lambda E, lt=lt, ps=ps_y: E.tensor_add(
                        u_bf[:, lt, :], ps[:, lt * 128:lt * 128 + 128],
                        uw[:, lt, :]))
                kb.mile("v", sV)
                B.rel(7, sV)
                ps_c = B.acq(6)
                kb.wait("t", sV)
                for lt in range(2):
                    kb.op("t", lambda E, lt=lt, ps=ps_c: E.matmul(
                        ps[:, 0:128], btok[:, lt, :], xd[:, lt, :],
                        start=(lt == 0), stop=(lt == 1)))
                kb.mile("t", sT)
                kb.wait("v", sT)
                for hh in range(2):
                    kb.op("v", lambda E, hh=hh, hp=hp, c=c, h0=h0:
                          E.tensor_scalar(
                              stw[:, hh * 64:hh * 64 + 64],
                              state[:, hp * 128 + hh * 64:
                                    hp * 128 + hh * 64 + 64],
                              eLbc[:, c, h0 + hh:h0 + hh + 1], None,
                              ALU.mult))
                kb.mile("v", sV)
                kb.wait("v", sV)
                kb.op("v", lambda E, hp=hp, ps=ps_c: E.tensor_add(
                    state[:, hp * 128:hp * 128 + 128], stw[:, :],
                    ps[:, 0:128]))
                kb.mile("v", sV)
                B.rel(6, sV)
                kb.wait("v", sV)
                kb.op("v", lambda E, hp=hp: E.tensor_copy(
                    state_bf[:, hp * 128:hp * 128 + 128],
                    state[:, hp * 128:hp * 128 + 128]))
                kb.mile("v", sV)
                ps_f = B.acq(6)
                pf_bf = ps_f.ap().bitcast(BF16)
                kb.wait("t", sV)
                for lt in range(2):
                    kb.op("t", lambda E, lt=lt, pf=pf_bf: E.transpose(
                        pf[:, lt * 128:lt * 128 + 128], u_bf[:, lt, :],
                        ident[:, :]))
                kb.mile("t", sT)
                kb.wait("v", sT)
                kb.wait("v", sA, sA.n)
                kb.wait("v", zq[hp][0], zq[hp][1])
                kb.op("v", lambda E, hp=hp: E.tensor_scalar(
                    xD[:, :], co[:, hp, :], Dbc_sb[:, hp:hp + 1], None,
                    ALU.mult))
                kb.mile("v", sV)
                kb.wait("v", sV)
                kb.op("v", lambda E, pf=pf_bf: E.tensor_add(
                    yt[:, :], pf[:, 0:256], xD[:, :]))
                kb.mile("v", sV)
                B.rel(6, sV)
                kb.wait("v", sV)
                kb.op("v", lambda E, hp=hp: E.tensor_mul(
                    yt[:, :], yt[:, :], zstr[hp % 2][:, :]))
                kb.mile("v", sV)
                war_set(zstr[hp % 2], sV)
                kb.wait("a", sV)
                kb.op("a", lambda E: E.activation(y2s[:, :], yt[:, :],
                                                  AF.Square))
                kb.mile("a", sA)
                ps_v = B.acq(2) if hp == 0 else B.b[2]
                kb.wait("t", sA)
                kb.op("t", lambda E, hp=hp, ps=ps_v: E.matmul(
                    ps[0:1, 0:CHUNK], ones_col[:, :], y2s[:, :],
                    start=(hp == 0), stop=(hp == 31)), inc=sT)
                kb.wait("v", sV)
                kb.op("v", lambda E, hp=hp: E.tensor_scalar(
                    y32w[:, hp, :], yt[:, :], normw_sb[:, hp:hp + 1],
                    None, ALU.mult))
                kb.mile("v", sV)

            tap(f"y32w{c}", lambda: y32w.ap(), [128, 32, CHUNK], BF16,
                [(sV, sV.n)])
            tap(f"state{c}", lambda: state.ap(), [128, NH * HD], F32,
                [(sV, sV.n)])

            kb.wait("a", sT)
            kb.op("a", lambda E: E.activation(
                row_a[:1, 0:CHUNK], B.b[2][0:1, 0:CHUNK], AF.Copy,
                scale=1.0 / DIN))
            kb.mile("a", sA)
            B.rel(2, sA)
            kb.wait("a", sA)
            kb.op("a", lambda E: E.activation(row_a[:1, 0:CHUNK],
                                              row_a[:1, 0:CHUNK], AF.Sqrt,
                                              bias=EPS))
            kb.mile("a", sA)
            kb.wait("v", sA)
            kb.op("v", lambda E: E.reciprocal(row_b[:1, 0:CHUNK],
                                              row_a[:1, 0:CHUNK]))
            kb.mile("v", sV)
            kb.wait("v", sV)
            kb.op("v", lambda E: E.tensor_copy(rst2_bfr[:1, :],
                                               row_b[:1, 0:CHUNK]))
            kb.mile("v", sV)
            pm2 = B.acq(2)
            kb.wait("t", sV)
            kb.op("t", lambda E, pm=pm2: E.matmul(
                pm[:, 0:CHUNK], ones_row[:1, :], rst2_bfr[:1, :],
                start=True, stop=True))
            kb.mile("t", sT)
            kb.wait("v", sT)
            kb.op("v", lambda E, pm=pm2: E.tensor_copy(rst2_bc[:, :],
                                                       pm[:, 0:CHUNK]))
            kb.mile("v", sV)
            B.rel(2, sV)
            kb.wait("v", sV)

            col0 = c * CHUNK + HALO
            parL = 0
            wload(WoaT, 0, 128, wbfull, 0)
            for mi in range(16):
                if mi + 1 < 16:
                    wload(WoaT, (mi + 1) * 128, 128, wbfull, 1 - parL)
                bi = mi % 2
                ps = B.acq(bi)
                wwait("t", wbfull, parL)
                for k in range(32):
                    kb.op("t", lambda E, k=k, ps=ps, par=parL: E.matmul(
                        ps[:, 0:CHUNK], wbfull[par][:, k, :],
                        y32w[:, k, :], start=(k == 0), stop=(k == 31)))
                kb.mile("t", sT)
                wdone(wbfull, parL)
                parL = 1 - parL
                kb.wait("v", sT)
                war_wait("v", yt)
                kb.op("v", lambda E, ps=ps: E.tensor_mul(
                    yt[:, :], ps[:, 0:CHUNK], rst2_bc[:, :]))
                kb.mile("v", sV)
                B.rel(bi, sV)
                war_set(yt, sV, sV.n + 1)
                kb.wait("v", sV)
                kb.op("v", lambda E, mi=mi, col0=col0: E.tensor_add(
                    x_fm[:, mi, col0:col0 + CHUNK],
                    x_fm[:, mi, col0:col0 + CHUNK], yt[:, :]))
                kb.mile("v", sV)

        kb.wait("p", sV, sV.n)

        def tx_state(E):
            with E.register("txmh") as mh, E.register("txpc") as pc:
                E.reg_load(mh, flags_sb[0:1, 0:1])
                E.reg_mov(pc, 0)
                with E.If_eq(mh, 0):
                    E.dma_start(out=sh_state[0:128, :],
                                in_=state[:, :]).then_inc(sPL2.h, 16)
                    E.reg_add(pc, pc, 16)
                    E.wait_ge(sPL2.h, pc)
                    E.dma_start(out=sh_flag[0:1, :],
                                in_=nonce_sb[:1, :]).then_inc(sPL2.h, 16)
                with E.Else():
                    E.dma_start(out=sh_state[128:256, :],
                                in_=state[:, :]).then_inc(sPL2.h, 16)
                    E.reg_add(pc, pc, 16)
                    E.wait_ge(sPL2.h, pc)
                    E.dma_start(out=sh_flag[1:2, :],
                                in_=nonce_sb[:1, :]).then_inc(sPL2.h, 16)
            return E.drain()

        if sim_local:
            kb.op("p", lambda E: E.drain(), inc=sEX)
        else:
            kb.op("p", tx_state, inc=sEX)
        stk2.close()
        stk.close()

    emit_attn_a()

    def emit_ffn(tag):
        stkf = ExitStack()
        wb = [kb.sb(f"wf{tag}{i}", [128, KCH, 128], BF16, stack=stkf)
              for i in range(2)]
        wbL = [kb.sb(f"wfl{tag}{i}", [128, 32, 128], BF16, stack=stkf)
               for i in range(2)]
        xcf = kb.sb(f"xcf{tag}", [128, KCH, SBW], BF16, stack=stkf)
        h1sb = kb.sb(f"h1{tag}", [128, 32, SBW], BF16, stack=stkf)
        h2sb = kb.sb(f"h2{tag}", [128, 32, SBW], BF16, stack=stkf)
        t1 = kb.sb(f"t1{tag}", [128, SBW], F32, stack=stkf)

        for sbi in range(2):
            c0 = HALO + sbi * SBW
            emit_ln(c0, SBW, xcf)
            emit_xc(c0, SBW, xcf)
            kb.wait("t", sV)
            par = 0
            wload(W1T, 0, 128, wb, 0)
            for mi in range(32):
                if mi + 1 < 32:
                    wload(W1T, (mi + 1) * 128, 128, wb, 1 - par)
                bi = mi % 2
                ps = B.acq(bi)
                wwait("t", wb, par)
                for k in range(KCH):
                    kb.op("t", lambda E, k=k, ps=ps, par=par: E.matmul(
                        ps[:, :], wb[par][:, k, :], xcf[:, k, :],
                        start=(k == 0), stop=(k == KCH - 1)))
                kb.mile("t", sT)
                wdone(wb, par)
                kb.wait("a", sT)
                kb.op("a", lambda E, mi=mi, ps=ps: E.activation(
                    h1sb[:, mi, :], ps[:, :], AF.Relu,
                    bias=b1_sb[:, mi:mi + 1]))
                kb.mile("a", sA)
                B.rel(bi, sA)
                par = 1 - par
            kb.wait("t", sA)
            par = 0
            wload(W2T, 0, 128, wbL, 0)
            for mi in range(32):
                if mi + 1 < 32:
                    wload(W2T, (mi + 1) * 128, 128, wbL, 1 - par)
                bi = mi % 2
                ps = B.acq(bi)
                wwait("t", wbL, par)
                for k in range(32):
                    kb.op("t", lambda E, k=k, ps=ps, par=par: E.matmul(
                        ps[:, :], wbL[par][:, k, :], h1sb[:, k, :],
                        start=(k == 0), stop=(k == 31)))
                kb.mile("t", sT)
                wdone(wbL, par)
                kb.wait("a", sT)
                kb.op("a", lambda E, mi=mi, ps=ps: E.activation(
                    h2sb[:, mi, :], ps[:, :], AF.Relu,
                    bias=b2_sb[:, mi:mi + 1]))
                kb.mile("a", sA)
                B.rel(bi, sA)
                par = 1 - par
            kb.wait("t", sA)
            par = 0
            wload(W3T, 0, 128, wbL, 0)
            for mi in range(16):
                if mi + 1 < 16:
                    wload(W3T, (mi + 1) * 128, 128, wbL, 1 - par)
                bi = mi % 2
                ps = B.acq(bi)
                wwait("t", wbL, par)
                for k in range(32):
                    kb.op("t", lambda E, k=k, ps=ps, par=par: E.matmul(
                        ps[:, :], wbL[par][:, k, :], h2sb[:, k, :],
                        start=(k == 0), stop=(k == 31)))
                kb.mile("t", sT)
                wdone(wbL, par)
                kb.wait("a", sT)
                war_wait("a", t1)
                kb.op("a", lambda E, mi=mi, ps=ps: E.activation(
                    t1[:, :], ps[:, :], AF.Identity,
                    bias=b3_sb[:, mi:mi + 1]))
                kb.mile("a", sA)
                B.rel(bi, sA)
                kb.wait("v", sA)
                war_set(t1, sV, sV.n + 1)
                kb.op("v", lambda E, mi=mi, c0=c0: E.tensor_add(
                    x_fm[:, mi, c0:c0 + SBW], x_fm[:, mi, c0:c0 + SBW],
                    t1[:, :]))
                kb.mile("v", sV)
                par = 1 - par
        stkf.close()

    if build_upto >= 1:
        emit_ffn("A")

    def emit_attn_b():
        stkb = ExitStack()
        wb = [kb.sb(f"wb{i}", [128, KCH, 128], BF16, stack=stkb)
              for i in range(2)]
        k_full = kb.sb("k_full", [128, KVH, 2 * T], BF16, stack=stkb)
        vtok = kb.sb("vtok", [128, KVH, 16, 128], BF16, stack=stkb)
        q_bf = kb.sb("q_bf", [128, HB, T], BF16, stack=stkb)
        qsc_bc = kb.sb("qsc_bc", [128, SBW], F32, stack=stkb)
        qrow = kb.sb("qrow", [1, SBW], F32, stack=stkb)
        qrow_bf = kb.sb("qrow_bf", [1, SBW], BF16, stack=stkb)
        negmax = kb.sb("negmax", [128, 1], F32, stack=stkb)
        rowsum = kb.sb("rowsum", [128, 1], F32, stack=stkb)
        rsum_i = kb.sb("rsum_i", [128, 1], F32, stack=stkb)
        stkv = ExitStack()
        v_fm = kb.sb("v_fm", [128, KVH, 2 * T], BF16, stack=stkv)
        xcb = kb.sb("xcb", [128, KCH, SBW], BF16, stack=stkv)
        astr = [kb.sb(f"astr{i}", [128, SBW], BF16, stack=stkv)
                for i in range(2)]

        myt0 = None

        for sbi in range(2):
            c0 = HALO + sbi * SBW
            t0 = sbi * SBW
            emit_ln(c0, SBW, xcb)
            emit_xc(c0, SBW, xcb)
            ps_cw = B.acq(3)
            for k in range(KCH):
                par = k % 2
                kb.wait("a", sV)
                war_wait("a", astr[par])
                kb.op("a", lambda E, k=k, par=par: E.activation(
                    astr[par][:, :], xcb[:, k, :], AF.Abs))
                kb.mile("a", sA)
                kb.wait("t", sA)
                kb.op("t", lambda E, k=k, par=par, ps=ps_cw: E.matmul(
                    ps[0:1, :], ones_col[:, :], astr[par][:, :],
                    start=(k == 0), stop=(k == KCH - 1)))
                kb.mile("t", sT)
                war_set(astr[par], sT)
            kb.wait("a", sT)
            kb.op("a", lambda E, ps=ps_cw: E.activation(
                qrow[:1, :], ps[0:1, :], AF.Sigmoid))
            kb.mile("a", sA)
            B.rel(3, sA)
            kb.wait("v", sA)
            kb.op("v", lambda E: E.tensor_scalar(
                qrow[:1, :], qrow[:1, :], float(ASCALE), None, ALU.mult))
            kb.mile("v", sV)
            kb.wait("v", sV)
            kb.op("v", lambda E: E.tensor_copy(qrow_bf[:1, :], qrow[:1, :]))
            kb.mile("v", sV)
            pb = B.acq(3)
            kb.wait("t", sV)
            kb.op("t", lambda E, pb=pb: E.matmul(
                pb[:, :], ones_row[:1, :], qrow_bf[:1, :],
                start=True, stop=True))
            kb.mile("t", sT)
            kb.wait("v", sT)
            kb.op("v", lambda E, pb=pb: E.tensor_copy(qsc_bc[:, :], pb[:, :]))
            kb.mile("v", sV)
            B.rel(3, sV)

            kb.wait("t", sV)
            par = 0
            wload(WkvT, 0, 128, wb, 0)
            for mi in range(8):
                if mi + 1 < 8:
                    wload(WkvT, (mi + 1) * 128, 128, wb, 1 - par)
                bi = mi % 2
                ps = B.acq(bi)
                wwait("t", wb, par)
                for k in range(KCH):
                    kb.op("t", lambda E, k=k, ps=ps, par=par: E.matmul(
                        ps[:, :], wb[par][:, k, :], xcb[:, k, :],
                        start=(k == 0), stop=(k == KCH - 1)))
                kb.mile("t", sT)
                wdone(wb, par)
                kb.wait("v", sT)

                def evkv(E, mi=mi, ps=ps, t0=t0, rn=f"kvh{sbi}_{mi}"):
                    with E.register(rn) as mh:
                        E.reg_load(mh, flags_sb[0:1, 0:1])
                        dst = k_full if mi < 4 else v_fm
                        mm = mi % 4
                        with E.If_eq(mh, 0):
                            E.tensor_copy(dst[:, mm, t0:t0 + SBW], ps[:, :])
                        with E.Else():
                            E.tensor_copy(dst[:, mm, T + t0:T + t0 + SBW],
                                          ps[:, :])
                    return E.drain()
                kb.op("v", evkv, inc=sV)
                B.rel(bi, sV)
                par = 1 - par

            kb.wait("t", sV)
            par = 0
            wload(WqT, 0, 128, wb, 0)
            for mi in range(16):
                if mi + 1 < 16:
                    wload(WqT, (mi + 1) * 128, 128, wb, 1 - par)
                bi = mi % 2
                ps = B.acq(bi)
                wwait("t", wb, par)
                for k in range(KCH):
                    kb.op("t", lambda E, k=k, ps=ps, par=par: E.matmul(
                        ps[:, :], wb[par][:, k, :], xcb[:, k, :],
                        start=(k == 0), stop=(k == KCH - 1)))
                kb.mile("t", sT)
                wdone(wb, par)
                kb.wait("v", sT)
                kb.op("v", lambda E, mi=mi, ps=ps, t0=t0: E.tensor_mul(
                    q_bf[:, mi, t0:t0 + SBW], ps[:, :], qsc_bc[:, :]))
                kb.mile("v", sV)
                B.rel(bi, sV)
                par = 1 - par

        shkv_v = sh_kv.rearrange("(o m p) t -> p o m t", p=128, o=2)
        kb.wait("p", sV, sV.n)

        def kv_xch(E):
            with E.register("xmh") as mh, E.register("xfv") as fv, \
                 E.register("xnr") as nr, E.register("xpc") as pc, \
                 E.register("xnv") as nv:
                E.reg_load(mh, flags_sb[0:1, 0:1])
                E.reg_load(nv, flags_sb[0:1, 1:2])
                E.reg_mov(pc, 0)
                with E.If_eq(mh, 0):
                    E.dma_start(out=shkv_v[:, 0, 0:4, :],
                                in_=k_full[:, :, 0:T]).then_inc(sPL3.h, 16)
                    E.dma_start(out=shkv_v[:, 0, 4:8, :],
                                in_=v_fm[:, :, 0:T]).then_inc(sPL3.h, 16)
                    E.reg_add(pc, pc, 32)
                    E.wait_ge(sPL3.h, pc)
                    E.dma_start(out=sh_flag[2:3, :],
                                in_=nonce_sb[:1, :]).then_inc(sPL3.h, 16)
                    E.reg_add(pc, pc, 16)

                    def poll0():
                        E.dma_start(out=flagrx[:1, :],
                                    in_=sh_flag[3:4, :]).then_inc(sPL3.h, 16)
                        E.reg_add(pc, pc, 16)
                        E.wait_ge(sPL3.h, pc)
                        E.reg_load(fv, flagrx[0:1, 0:1])
                        E.reg_sub(nr, fv, nv)
                    poll0()
                    with E.While(nr):
                        poll0()
                    E.dma_start(out=k_full[:, :, T:2 * T],
                                in_=shkv_v[:, 1, 0:4, :]).then_inc(sPL3.h, 16)
                    E.dma_start(out=v_fm[:, :, T:2 * T],
                                in_=shkv_v[:, 1, 4:8, :]).then_inc(sPL3.h, 16)
                    E.reg_add(pc, pc, 32)
                    E.wait_ge(sPL3.h, pc)
                with E.Else():
                    E.dma_start(out=shkv_v[:, 1, 0:4, :],
                                in_=k_full[:, :, T:2 * T]).then_inc(sPL3.h, 16)
                    E.dma_start(out=shkv_v[:, 1, 4:8, :],
                                in_=v_fm[:, :, T:2 * T]).then_inc(sPL3.h, 16)
                    E.reg_add(pc, pc, 32)
                    E.wait_ge(sPL3.h, pc)
                    E.dma_start(out=sh_flag[3:4, :],
                                in_=nonce_sb[:1, :]).then_inc(sPL3.h, 16)
                    E.reg_add(pc, pc, 16)

                    def poll1():
                        E.dma_start(out=flagrx[:1, :],
                                    in_=sh_flag[2:3, :]).then_inc(sPL3.h, 16)
                        E.reg_add(pc, pc, 16)
                        E.wait_ge(sPL3.h, pc)
                        E.reg_load(fv, flagrx[0:1, 0:1])
                        E.reg_sub(nr, fv, nv)
                    poll1()
                    with E.While(nr):
                        poll1()
                    E.dma_start(out=k_full[:, :, 0:T],
                                in_=shkv_v[:, 0, 0:4, :]).then_inc(sPL3.h, 16)
                    E.dma_start(out=v_fm[:, :, 0:T],
                                in_=shkv_v[:, 0, 4:8, :]).then_inc(sPL3.h, 16)
                    E.reg_add(pc, pc, 32)
                    E.wait_ge(sPL3.h, pc)
            return E.drain()

        def kv_xch_sim(E):
            with E.register("xmh_s") as mh:
                E.reg_load(mh, flags_sb[0:1, 0:1])
                with E.If_eq(mh, 0):
                    E.memset(k_full[:, :, T:2 * T], 0.0)
                    E.memset(v_fm[:, :, T:2 * T], 0.0)
                with E.Else():
                    E.memset(k_full[:, :, 0:T], 0.0)
                    E.memset(v_fm[:, :, 0:T], 0.0)
            return E.drain()

        kb.op("p", kv_xch_sim if sim_local else kv_xch, inc=sEX)
        for e in ("t", "v", "a"):
            kb.wait(e, sEX)

        for kv in range(KVH):
            for g in range(4):
                ps_t = B.acq(4 + (g % 2))
                pt_bf = ps_t.ap().bitcast(BF16)
                for jt4 in range(4):
                    jt = g * 4 + jt4
                    kb.op("t", lambda E, kv=kv, jt=jt, jt4=jt4, pt=pt_bf:
                          E.transpose(pt[:, jt4 * 128:jt4 * 128 + 128],
                                      v_fm[:, kv, jt * 128:jt * 128 + 128],
                                      ident[:, :]))
                kb.mile("t", sT)
                kb.wait("v", sT)
                for jt4 in range(4):
                    jt = g * 4 + jt4
                    kb.op("v", lambda E, kv=kv, jt=jt, jt4=jt4, pt=pt_bf:
                          E.tensor_copy(vtok[:, kv, jt, :],
                                        pt[:, jt4 * 128:jt4 * 128 + 128]))
                kb.mile("v", sV)
                B.rel(4 + (g % 2), sV)
        stkv.close()

        stkh = ExitStack()
        P_bf = kb.sb("P_bf", [128, 2, 2 * T], BF16, stack=stkh)
        PT_bf = kb.sb("PT_bf", [128, 16, 256], BF16, stack=stkh)
        S_sb = kb.sb("S_sb", [128, 2 * T], F32, stack=stkh)
        dbase = kb.sb("dbase", [128, 2, 2 * T], F32, stack=stkh)
        band01 = kb.sb("band01", [128, 2 * T], BF16, stack=stkh)
        aostr = kb.sb("aostr", [128, 256], BF16, stack=stkh)

        for itp in range(4):
            for sl in range(2):
                it = 2 * itp + sl

                def mk_dbase(E, it=it, sl=sl, rn=f"dmh{it}"):
                    with E.register(rn) as mh:
                        E.reg_load(mh, flags_sb[0:1, 0:1])
                        with E.If_eq(mh, 0):
                            E.iota(dbase[:, sl, :], [[1, 2 * T]],
                                   base=-(it * 128), channel_multiplier=-1,
                                   allow_small_or_imprecise_dtypes=True)
                        with E.Else():
                            E.iota(dbase[:, sl, :], [[1, 2 * T]],
                                   base=-(T + it * 128),
                                   channel_multiplier=-1,
                                   allow_small_or_imprecise_dtypes=True)
                        E.memset(band01[:, :], 1.0)
                    return E.drain()
                kb.wait("p", sA, sA.n)
                kb.wait("p", sV, sV.n)
                kb.op("p", mk_dbase, inc=sP)
                kb.wait("a", sP)
                kb.op("a", lambda E, sl=sl: E.activation(
                    dbase[:, sl, :], dbase[:, sl, :], AF.Abs))
                kb.mile("a", sA)

                def mk_band(E, it=it, rn=f"bmh{it}"):
                    with E.register(rn) as mh:
                        E.reg_load(mh, flags_sb[0:1, 0:1])
                        for half in range(2):
                            base_i = half * T + it * 128
                            with (E.If_eq(mh, 0) if half == 0 else E.Else()):
                                E.affine_select(band01[:, :], band01[:, :],
                                                [[1, 2 * T]], ALU.is_ge, 0.0,
                                                base=-base_i + 256,
                                                channel_multiplier=-1)
                                E.drain()
                                E.affine_select(band01[:, :], band01[:, :],
                                                [[-1, 2 * T]], ALU.is_ge, 0.0,
                                                base=base_i + 256,
                                                channel_multiplier=1)
                    return E.drain()
                kb.op("p", mk_band, inc=sP)
                kb.wait("v", sP)
                kb.wait("v", sA)
                kb.op("v", lambda E, sl=sl: E.scalar_tensor_tensor(
                    dbase[:, sl, :], band01[:, :], 4.0e9, dbase[:, sl, :],
                    ALU.mult, ALU.add))
                kb.mile("v", sV)
                kb.wait("v", sV)
                kb.op("v", lambda E, sl=sl: E.memset(dbase[:, sl, 0:64],
                                                     4.0e9))
                kb.mile("v", sV)
            dbase_ready = sV.n

            for h in range(HB):
                kv = h // 4
                for sl in range(2):
                    it = 2 * itp + sl
                    kb.wait("v", sA, sA.n)
                    for js in range(4):
                        bi = js % 2
                        ps = B.acq(bi)
                        kb.wait("t", sEX)
                        kb.wait("t", sV, dbase_ready)
                        kb.op("t", lambda E, h=h, it=it, js=js, kv=kv, ps=ps:
                              E.matmul(ps[:, :],
                                       q_bf[:, h, it * 128:it * 128 + 128],
                                       k_full[:, kv, js * 512:js * 512 + 512],
                                       start=True, stop=True))
                        kb.mile("t", sT)
                        kb.wait("v", sT)
                        kb.op("v", lambda E, h=h, js=js, sl=sl, ps=ps:
                              E.scalar_tensor_tensor(
                                  S_sb[:, js * 512:js * 512 + 512],
                                  dbase[:, sl, js * 512:js * 512 + 512],
                                  slopes_sb[:, h:h + 1], ps[:, :],
                                  ALU.mult, ALU.add))
                        kb.mile("v", sV)
                        B.rel(bi, sV)
                    kb.wait("v", sV)
                    kb.op("v", lambda E: E.reduce_max(
                        negmax[:, :], S_sb[:, :], axis=mybir.AxisListType.X,
                        negate=True))
                    kb.mile("v", sV)
                    kb.wait("a", sV)
                    kb.op("a", lambda E, sl=sl: E.activation(
                        P_bf[:, sl, :], S_sb[:, :], AF.Exp,
                        bias=negmax[:, 0:1], accum_out=rowsum[:, 0:1]))
                    kb.mile("a", sA)
                    kb.wait("v", sA)
                    if it == 0:
                        def fix_glob(E, sl=sl, rn=f"gmh{h}"):
                            with E.register(rn) as mh:
                                E.reg_load(mh, flags_sb[0:1, 0:1])
                                with E.If_eq(mh, 0):
                                    E.memset(P_bf[0:64, sl, :], 1.0)
                                    E.memset(rowsum[0:64, 0:1], 2048.0)
                            return E.drain()
                        kb.op("v", fix_glob, inc=sV)
                        kb.wait("v", sV)
                    kb.op("v", lambda E: E.reciprocal(rsum_i[:, :],
                                                      rowsum[:, :]))
                    kb.mile("v", sV)
                    kb.wait("v", sV)
                    kb.op("v", lambda E, sl=sl: E.tensor_scalar(
                        P_bf[:, sl, :], P_bf[:, sl, :], rsum_i[:, 0:1], None,
                        ALU.mult))
                    kb.mile("v", sV)

                for jt in range(16):
                    ps_t = B.acq(4 + (jt % 2))
                    pt_bf = ps_t.ap().bitcast(BF16)
                    kb.wait("t", sV, sV.n)
                    for i4 in range(2):
                        kb.op("t", lambda E, jt=jt, i4=i4, pt=pt_bf:
                              E.transpose(
                                  pt[:, i4 * 128:i4 * 128 + 128],
                                  P_bf[:, i4, jt * 128:jt * 128 + 128],
                                  ident[:, :]))
                    kb.mile("t", sT)
                    kb.wait("v", sT)
                    kb.op("v", lambda E, jt=jt, pt=pt_bf: E.tensor_copy(
                        PT_bf[:, jt, :], pt[:, 0:256]))
                    kb.mile("v", sV)
                    B.rel(4 + (jt % 2), sV)
                ps_av = B.acq(6)
                kb.wait("t", sV)
                for jt in range(16):
                    kb.op("t", lambda E, jt=jt, kv=kv, ps=ps_av: E.matmul(
                        ps[:, 0:256], vtok[:, kv, jt, :], PT_bf[:, jt, :],
                        start=(jt == 0), stop=(jt == 15)))
                kb.mile("t", sT)
                kb.wait("v", sT)
                war_wait("v", aostr)
                kb.op("v", lambda E, ps=ps_av: E.tensor_copy(
                    aostr[:, :], ps[:, 0:256]))
                kb.mile("v", sV)
                B.rel(6, sV)
                kb.wait("y", sV)
                kb.dma("y", ao_v[:, h, itp * 256:itp * 256 + 256],
                       aostr[:, :], sAO)
                war_set(aostr, sAO)
        stkh.close()

        tap("aoscr", lambda: ao_scr, [HB * HDB, T], BF16,
            [(sST, sST.n), (sAO, sAO.n)])
        tap("kfull", lambda: k_full.ap(), [128, KVH, 2 * T], BF16,
            [(sEX, sEX.n)])
        tap("qbf", lambda: q_bf.ap(), [128, HB, T], BF16, [(sV, sV.n)])
        stko = ExitStack()
        aosb = kb.sb("aosb", [128, HB, T], BF16, stack=stko)
        t1b = kb.sb("t1b", [128, SBW], F32, stack=stko)
        kb.wait("y", sST)
        kb.wait("y", sAO)
        kb.dma("y", aosb.ap(), ao_v, sLD)
        for sbi in range(2):
            t0 = sbi * SBW
            c0 = HALO + t0
            wload(WobT, 0, 128, wb, 0)
            for mi in range(16):
                if mi + 1 < 16:
                    wload(WobT, (mi + 1) * 128, 128, wb, 1 - (mi % 2))
                bi = mi % 2
                ps = B.acq(bi)
                wwait("t", wb, mi % 2)
                kb.wait("t", sLD)
                for k in range(HB):
                    kb.op("t", lambda E, k=k, ps=ps, par=mi % 2, t0=t0:
                          E.matmul(ps[:, :], wb[par][:, k, :],
                                   aosb[:, k, t0:t0 + SBW],
                                   start=(k == 0), stop=(k == HB - 1)))
                kb.mile("t", sT)
                wdone(wb, mi % 2)
                kb.wait("a", sT)
                war_wait("a", t1b)
                kb.op("a", lambda E, mi=mi, ps=ps: E.activation(
                    t1b[:, :], ps[:, :], AF.Identity,
                    bias=bo_sb[:, mi:mi + 1]))
                kb.mile("a", sA)
                B.rel(bi, sA)
                kb.wait("v", sA)
                war_set(t1b, sV, sV.n + 1)
                kb.op("v", lambda E, mi=mi, c0=c0: E.tensor_add(
                    x_fm[:, mi, c0:c0 + SBW], x_fm[:, mi, c0:c0 + SBW],
                    t1b[:, :]))
                kb.mile("v", sV)
        stko.close()
        stkb.close()

    if build_upto >= 2:
        emit_attn_b()
    if build_upto >= 3:
        emit_ffn("B")


    for e in ("v", "a", "p", "t"):
        kb.mile(e, sXC)
    kb.wait("y", sXC)
    kb.dma("y", out_fm.rearrange("(k p) t -> p k t", p=128),
           x_fm.ap()[:, :, HALO:], sST)
    kb.wait("y", sST)

    with nc.Block() as block:
        @block.tensor
        def _(E):
            for f in kb.cl["t"]:
                f(E)

        @block.vector
        def _(E):
            for f in kb.cl["v"]:
                f(E)

        @block.scalar
        def _(E):
            for f in kb.cl["a"]:
                f(E)

        @block.gpsimd
        def _(E):
            for f in kb.cl["p"]:
                f(E)

        @block.sync
        def _(E):
            for f in kb.cl["y"]:
                f(E)

    kb.es.close()
    return nc, tap_names



def _prep_inputs(inputs):
    x = np.ascontiguousarray(np.asarray(inputs["L0_attn_a_hidden_states"],
                                        np.float32))
    f32 = np.float32

    def c(a):
        return np.ascontiguousarray(np.asarray(a, f32))

    Dp = np.asarray(inputs["L0_attn_a_D"], f32)
    shared = {
        "WinT": c(np.asarray(inputs["L0_attn_a_in_proj_w"]).T),
        "convw": c(np.asarray(inputs["L0_attn_a_conv_w"])
                   .reshape(34, 128, 4).transpose(1, 0, 2).reshape(128, 136)),
        "convb": c(np.asarray(inputs["L0_attn_a_conv_b"]).reshape(34, 128).T),
        "dtbias": c(np.asarray(inputs["L0_attn_a_dt_bias"]).reshape(64, 1)),
        "A_log": c(np.asarray(inputs["L0_attn_a_A_log"]).reshape(64, 1)),
        "D_bc": c(Dp.repeat(64).reshape(32, 128).T),
        "normw": c(np.asarray(inputs["L0_attn_a_norm_w"]).reshape(32, 128).T),
        "WoaT": c(np.asarray(inputs["L0_attn_a_out_proj_w"]).T),
        "W1T": c(np.asarray(inputs["L0_ffn_W1"]).T),
        "W2T": c(np.asarray(inputs["L0_ffn_W2"]).T),
        "W3T": c(np.asarray(inputs["L0_ffn_W3"]).T),
        "b1c": c(np.asarray(inputs["L0_ffn_b1"]).reshape(32, 128).T),
        "b2c": c(np.asarray(inputs["L0_ffn_b2"]).reshape(32, 128).T),
        "b3c": c(np.asarray(inputs["L0_ffn_b3"]).reshape(16, 128).T),
        "WqT": c(inputs["L0_attn_b_Wq"]),
        "WkvT": c(inputs["L0_attn_b_Wkv"]),
        "WobT": c(inputs["L0_attn_b_Wo"]),
        "boc": c(np.asarray(inputs["L0_attn_b_bo"]).reshape(16, 128).T),
    }
    cp = 2.0 ** np.floor(np.log2(HB))
    base = 2.0 ** (-(2.0 ** (-(np.log2(cp) - 3.0))))
    slopes = (base ** np.arange(1, HB + 1, dtype=np.float64)).astype(f32)
    shared["slopes_n"] = c(np.broadcast_to(-slopes[None, :], (128, HB)))

    rng = np.random.default_rng()
    nonce = np.uint32(rng.integers(1, 2**32 - 1, dtype=np.uint64))

    in_maps = []
    for cid in range(8):
        b, h = cid // 2, cid % 2
        xi = x[b]
        t0 = h * T
        halo = np.zeros((HALO, D), f32) if h == 0 else xi[t0 - HALO:t0]
        xslab = np.concatenate([halo, xi[t0:t0 + T]], axis=0)
        m = dict(shared)
        m["x_fm"] = c(xslab.T)
        m["flags"] = np.array([[h, nonce, 0, 0, 0, 0, 0, 0]], np.uint32)
        m["halfsc"] = np.full((128, 1), float(h), f32)
        in_maps.append(m)
    return in_maps


def kernel(**inputs):
    in_maps = _prep_inputs(inputs)
    nc, _ = build()
    res = run_bass_kernel_spmd(nc, in_maps, core_ids=list(range(8)))
    out = np.empty((4, 2048, 2048), np.float32)
    for cid in range(8):
        b, h = cid // 2, cid % 2
        out[b, h * T:(h + 1) * T, :] = res.results[cid]["out_fm"].T
    return out


# version 06 (final)
# speedup vs baseline: 1.2347x; 1.0244x over previous
import numpy as np
import ml_dtypes
import concourse.bass as bass
import concourse.mybir as mybir
from concourse.bass_utils import run_bass_kernel_spmd
from contextlib import ExitStack

F32 = mybir.dt.float32
BF16 = mybir.dt.bfloat16
U32 = mybir.dt.uint32
AF = mybir.ActivationFunctionType
ALU = mybir.AluOpType

T = 1024
HALO = 3
TF = T + HALO
D = 2048
KCH = 16
PROJ = 8512
DIN = 4096
NH = 64
HD = 64
CHUNK = 256
NCHUNK = 4
SBW = 512
EPS = 1e-5
HB = 16
KVH = 4
HDB = 128
ASCALE = float(1.0 / np.sqrt(128.0))

ENGS = ("t", "v", "a", "p", "y")


class Sem:
    def __init__(self, es, nc, name):
        self.h = es.enter_context(nc.semaphore(name))
        self.n = 0


class KB:
    def __init__(self, nc):
        self.nc = nc
        self.es = ExitStack()
        self.cl = {e: [] for e in ENGS}
        self._i = 0

    def sem(self, name=None):
        self._i += 1
        return Sem(self.es, self.nc, name or f"s{self._i}")

    def sb(self, name, shape, dt, stack=None):
        return (stack or self.es).enter_context(
            self.nc.sbuf_tensor(name, shape, dt))

    def emit(self, e, fn):
        self.cl[e].append(fn)

    def op(self, e, fn, inc=None, k=1):
        if inc is None:
            d = self.esem.get(e)
            if d is None:
                self.emit(e, fn)
            else:
                d.n += 1
                self.emit(e, lambda E, fn=fn, s=d: fn(E).then_inc(s.h, 1))
        else:
            inc.n += k
            self.emit(e, lambda E, fn=fn, s=inc, kk=k: fn(E).then_inc(s.h, kk))

    def wait(self, e, sem, at=None):
        v = sem.n if at is None else at
        if v > 0:
            self.emit(e, lambda E, h=sem.h, v=v: E.wait_ge(h, v))

    def mile(self, e, sem, k=1):
        if self.esem.get(e) is sem:
            return
        sem.n += k
        self.emit(e, lambda E, h=sem.h, k=k: E.drain().then_inc(h, k))

    def dma(self, e, out, in_, sem, **kw):
        sem.n += 16
        self.emit(e, lambda E, o=out, i=in_, h=sem.h, kw=kw:
                  E.dma_start(out=o, in_=i, **kw).then_inc(h, 16))


class Banks:
    def __init__(self, kb):
        self.kb = kb
        self.b = [kb.es.enter_context(
            kb.nc.psum_tensor(f"bank{i}", [128, 512], F32)) for i in range(8)]
        self.tag = [None] * 8

    def acq(self, i, weng="t"):
        if self.tag[i] is not None:
            s, v = self.tag[i]
            self.kb.wait(weng, s, v)
            self.tag[i] = None
        return self.b[i]

    def rel(self, i, sem, at=None):
        self.tag[i] = (sem, sem.n if at is None else at)


def build(build_upto=3, taps=(), sim_local=False):
    nc = bass.Bass("TRN2", target_bir_lowering=False)
    kb = KB(nc)
    tap_req = set(taps)
    tap_names = []

    din = {}

    def dram_in(name, shape, dt=F32):
        din[name] = nc.dram_tensor(name, shape, dt, kind="ExternalInput").ap()
        return din[name]

    x_in = dram_in("x_fm", [D, TF])
    WinT = dram_in("WinT", [D, PROJ])
    dram_in("convw", [128, 34 * 4])
    dram_in("convb", [128, 34])
    dram_in("dtbias", [64, 1])
    dram_in("A_log", [64, 1])
    dram_in("D_bc", [128, 32])
    dram_in("normw", [128, 32])
    WoaT = dram_in("WoaT", [DIN, D])
    W1T = dram_in("W1T", [D, DIN])
    W2T = dram_in("W2T", [DIN, DIN])
    W3T = dram_in("W3T", [DIN, D])
    dram_in("b1c", [128, 32])
    dram_in("b2c", [128, 32])
    dram_in("b3c", [128, 16])
    WqT = dram_in("WqT", [D, D])
    WkvT = dram_in("WkvT", [D, 1024])
    WobT = dram_in("WobT", [D, D])
    dram_in("boc", [128, 16])
    dram_in("slopes_n", [128, HB])
    dram_in("flags", [1, 8], U32)
    dram_in("halfsc", [128, 1])

    out_fm = nc.dram_tensor("out_fm", [D, T], F32, kind="ExternalOutput").ap()
    ao_scr = nc.dram_tensor("ao_scr", [HB * HDB, T], BF16,
                            kind="Internal").ap()
    xbc_scr = nc.dram_tensor("xbc_scr", [34 * 128, TF], BF16,
                             kind="Internal").ap()
    silz_scr = nc.dram_tensor("silz_scr", [32 * 128, T], BF16,
                              kind="Internal").ap()
    h1_scr = nc.dram_tensor("h1_scr", [32 * 128, T], BF16,
                            kind="Internal").ap()
    sh_state = nc.dram_tensor("sh_state", [2 * 128, NH * HD], F32,
                              kind="Internal", addr_space="Shared").ap()
    sh_kv = nc.dram_tensor("sh_kv", [2 * 1024, T], BF16,
                           kind="Internal", addr_space="Shared").ap()
    sh_flag = nc.dram_tensor("sh_flag", [4, 16], U32,
                             kind="Internal", addr_space="Shared").ap()

    xbc_v = xbc_scr.rearrange("(m p) t -> p m t", p=128)
    silz_v = silz_scr.rearrange("(m p) t -> p m t", p=128)
    h1_v = h1_scr.rearrange("(m p) t -> p m t", p=128)
    ao_v = ao_scr.rearrange("(m p) t -> p m t", p=128)

    x_fm = kb.sb("x_fm_sb", [128, KCH, TF], F32)
    ident = kb.sb("ident", [128, 128], BF16)
    identf = kb.sb("identf", [128, 128], F32)
    ones_col = kb.sb("ones_col", [128, 1], BF16)
    ones_row = kb.sb("ones_row", [1, 128], BF16)
    convw_sb = kb.sb("convw_sb", [128, 34 * 4], F32)
    convb_sb = kb.sb("convb_sb", [128, 34], F32)
    dtbias_sb = kb.sb("dtbias_sb", [64, 1], F32)
    Acol = kb.sb("Acol", [64, 1], F32)
    Dbc_sb = kb.sb("Dbc_sb", [128, 32], F32)
    normw_sb = kb.sb("normw_sb", [128, 32], F32)
    b1_sb = kb.sb("b1_sb", [128, 32], F32)
    b2_sb = kb.sb("b2_sb", [128, 32], F32)
    b3_sb = kb.sb("b3_sb", [128, 16], F32)
    bo_sb = kb.sb("bo_sb", [128, 16], F32)
    slopes_sb = kb.sb("slopes_sb", [128, HB], F32)
    halfsc_sb = kb.sb("halfsc_sb", [128, 1], F32)
    flags_sb = kb.sb("flags_sb", [1, 8], U32)
    zeros64 = kb.sb("zeros64", [64, CHUNK], F32)
    eps_c = kb.sb("eps_c", [128, 1], F32)
    nc.const_aps.register(F32, EPS, eps_c[:, :])
    nonce_sb = kb.sb("nonce_sb", [1, 16], U32)
    flagrx = kb.sb("flagrx", [1, 16], U32)
    mean_bfr = kb.sb("mean_bfr", [1, SBW], BF16)
    rstd_row = kb.sb("rstd_row", [1, SBW], F32)
    rstd_bfr = kb.sb("rstd_bfr", [1, SBW], BF16)
    rstd_bc = kb.sb("rstd_bc", [128, SBW], F32)
    row_a = kb.sb("row_a", [1, SBW], F32)
    row_b = kb.sb("row_b", [1, SBW], F32)

    B = Banks(kb)

    sLD = kb.sem("sLD")
    sWp = [kb.sem("sW0"), kb.sem("sW1")]
    sT = kb.sem("sT")
    sV = kb.sem("sV")
    sA = kb.sem("sA")
    sP = kb.sem("sP")
    sST = kb.sem("sST")
    sSS = [kb.sem("sSS0"), kb.sem("sSS1")]
    sAO = kb.sem("sAO")
    sXB = [kb.sem("sXB0"), kb.sem("sXB1")]
    sZS = [kb.sem("sZS0"), kb.sem("sZS1")]
    sEX = kb.sem("sEX")
    sXC = kb.sem("sXC")
    kb.esem = {"t": sT, "v": sV, "a": sA, "p": sP}
    sPL = kb.sem("sPL")
    sPL2 = kb.sem("sPL2")
    sPL3 = kb.sem("sPL3")

    for name, dst in [("convw", convw_sb), ("convb", convb_sb),
                      ("dtbias", dtbias_sb), ("A_log", Acol),
                      ("D_bc", Dbc_sb), ("normw", normw_sb),
                      ("b1c", b1_sb), ("b2c", b2_sb), ("b3c", b3_sb),
                      ("boc", bo_sb), ("slopes_n", slopes_sb),
                      ("halfsc", halfsc_sb), ("flags", flags_sb)]:
        kb.dma("y", dst.ap(), din[name], sLD)
    kb.dma("y", x_fm.ap(), x_in.rearrange("(k p) t -> p k t", p=128), sLD)

    kb.op("p", lambda E: E.memset(ident[:, :], 1.0))
    kb.op("p", lambda E: E.memset(ones_col[:, :], 1.0))
    kb.op("p", lambda E: E.memset(ones_row[:, :], 1.0))
    kb.op("p", lambda E: E.memset(zeros64[:, :], 0.0))
    kb.op("p", lambda E: E.memset(eps_c[:, :], EPS))
    kb.wait("p", sP)
    kb.op("p", lambda E: E.affine_select(ident[:, :], ident[:, :], [[1, 128]],
                                         ALU.is_equal, 0.0, base=0,
                                         channel_multiplier=-1))
    kb.wait("p", sP)
    kb.op("p", lambda E: E.tensor_copy(identf[:, :], ident[:, :]))
    kb.wait("p", sLD)
    kb.op("p", lambda E: E.tensor_copy(
        nonce_sb[:1, :], flags_sb[0:1, 1:2].broadcast_to([1, 16])))
    kb.mile("p", sP)
    kb.wait("a", sLD)
    kb.op("a", lambda E: E.activation(Acol[:, :], Acol[:, :], AF.Exp))
    kb.mile("a", sA)
    kb.wait("v", sA)
    kb.op("v", lambda E: E.tensor_scalar(Acol[:, :], Acol[:, :], -1.0, None,
                                         ALU.mult))
    kb.mile("v", sV)

    _st = (sLD.n, sP.n, sV.n, sA.n)
    for e in ENGS:
        kb.wait(e, sLD, _st[0])
        kb.wait(e, sP, _st[1])
        kb.wait(e, sV, _st[2])
        kb.wait(e, sA, _st[3])

    def tap(name, ap_factory, shape, dt, waits):
        if name not in tap_req:
            return
        th = nc.dram_tensor("tap_" + name, list(shape), dt,
                            kind="ExternalOutput").ap()
        tap_names.append("tap_" + name)
        for (sm, v) in waits:
            kb.wait("y", sm, v)
        kb.dma("y", th, ap_factory(), sST)

    war = {}

    def war_wait(e, buf):
        t = war.get(id(buf))
        if t is not None:
            kb.wait(e, t[0], t[1])

    def war_set(buf, sem, at=None):
        war[id(buf)] = (sem, sem.n if at is None else at)

    def emit_ln(c0, n, xcbuf, rbase=0):
        kb.wait("p", sV, sV.n)
        kb.wait("p", sT, sT.n)
        for k in range(KCH):
            kb.op("p", lambda E, k=k, n=n, c0=c0: E.tensor_copy(
                xcbuf[:, k, :n], x_fm[:, k, c0:c0 + n]))
        kb.mile("p", sP)
        ps_s = B.acq(3)
        kb.wait("t", sP)
        for k in range(KCH):
            kb.op("t", lambda E, k=k, ps=ps_s, n=n: E.matmul(
                ps[0:1, :n], ones_col[:, :], xcbuf[:, k, :n],
                start=(k == 0), stop=(k == KCH - 1)))
        kb.mile("t", sT)
        kb.wait("a", sT)
        for k in range(KCH):
            kb.op("a", lambda E, k=k, n=n: E.activation(
                xcbuf[:, k, :n], xcbuf[:, k, :n], AF.Square))
        kb.mile("a", sA)
        ps_q = B.acq(4)
        kb.wait("t", sA)
        for k in range(KCH):
            kb.op("t", lambda E, k=k, ps=ps_q, n=n: E.matmul(
                ps[0:1, :n], ones_col[:, :], xcbuf[:, k, :n],
                start=(k == 0), stop=(k == KCH - 1)))
        kb.mile("t", sT)
        r0 = rbase
        kb.wait("a", sT)
        kb.op("a", lambda E, ps=ps_s, n=n, r0=r0: E.activation(
            row_a[:, r0:r0 + n], ps[0:1, :n], AF.Copy, scale=1.0 / D))
        kb.op("a", lambda E, ps=ps_s, n=n, r0=r0: E.activation(
            mean_bfr[:, r0:r0 + n], ps[0:1, :n], AF.Copy, scale=1.0 / D))
        kb.op("a", lambda E, ps=ps_q, n=n, r0=r0: E.activation(
            row_b[:, r0:r0 + n], ps[0:1, :n], AF.Copy, scale=1.0 / D))
        kb.mile("a", sA)
        B.rel(3, sA)
        B.rel(4, sA)
        kb.wait("a", sA)
        kb.op("a", lambda E, n=n, r0=r0: E.activation(
            row_a[:, r0:r0 + n], row_a[:, r0:r0 + n], AF.Square))
        kb.mile("a", sA)
        kb.wait("v", sA)
        kb.op("v", lambda E, n=n, r0=r0: E.tensor_sub(
            row_b[:, r0:r0 + n], row_b[:, r0:r0 + n], row_a[:, r0:r0 + n]))
        kb.mile("v", sV)
        kb.wait("a", sV)
        kb.op("a", lambda E, n=n, r0=r0: E.activation(
            row_b[:, r0:r0 + n], row_b[:, r0:r0 + n], AF.Sqrt, bias=EPS))
        kb.mile("a", sA)
        kb.wait("v", sA)
        kb.op("v", lambda E, n=n, r0=r0: E.reciprocal(
            rstd_row[:, r0:r0 + n], row_b[:, r0:r0 + n]))
        kb.mile("v", sV)
        kb.wait("v", sV)
        kb.op("v", lambda E, n=n, r0=r0: E.tensor_copy(
            rstd_bfr[:, r0:r0 + n], rstd_row[:, r0:r0 + n]))
        kb.mile("v", sV)
        pb = B.acq(2)
        kb.wait("t", sV)
        kb.op("t", lambda E, pb=pb, n=n, r0=r0: E.matmul(
            pb[:, :n], ones_row[:1, :], rstd_bfr[:1, r0:r0 + n],
            start=True, stop=True))
        kb.mile("t", sT)
        kb.wait("v", sT)
        kb.op("v", lambda E, pb=pb, n=n, r0=r0: E.tensor_copy(
            rstd_bc[:, r0:r0 + n], pb[:, :n]))
        kb.mile("v", sV)
        B.rel(2, sV)

    xcs_f32 = [kb.sb(f"xcs{i}", [128, SBW], F32) for i in range(2)]

    def emit_xc(c0, n, xcbuf, rbase=0):
        pb = B.acq(2)
        kb.wait("t", sV)
        kb.op("t", lambda E, pb=pb, n=n, rbase=rbase: E.matmul(
            pb[:, :n], ones_row[:1, :], mean_bfr[:1, rbase:rbase + n],
            start=True, stop=True))
        kb.mile("t", sT)
        kb.wait("v", sT)
        for k in range(KCH):
            par = k % 2
            if k >= 2:
                kb.wait("v", sV, sV.n - 1)
            kb.op("v", lambda E, k=k, pb=pb, n=n, c0=c0, par=par:
                  E.tensor_sub(xcs_f32[par][:, :n], x_fm[:, k, c0:c0 + n],
                               pb[:, :n]))
            kb.wait("v", sV)
            kb.op("v", lambda E, k=k, n=n, rbase=rbase, par=par:
                  E.tensor_mul(xcbuf[:, k, :n], xcs_f32[par][:, :n],
                               rstd_bc[:, rbase:rbase + n]))
        kb.mile("v", sV)
        B.rel(2, sV)

    wfree = {}
    wready = {}

    def wload(WT_ap, m0, mw, wbuf, par, krange=None):
        t = wfree.get(id(wbuf[par]))
        if t is not None:
            kb.wait("p", t[0], t[1])
        src = WT_ap.rearrange("(k p) m -> p k m", p=128)
        if krange is None:
            kcnt = src.shape[1]
            src = src[:, :, m0:m0 + mw]
        else:
            kcnt = krange[1] - krange[0]
            src = src[:, krange[0]:krange[1], m0:m0 + mw]
        sem = sWp[par]
        kb.wait("p", sem, sem.n)
        kb.dma("p", wbuf[par][:, :kcnt, :mw], src, sem)
        wready[id(wbuf[par])] = (sem, sem.n)

    def wwait(e, wbuf, par):
        s, v = wready[id(wbuf[par])]
        kb.wait(e, s, v)

    def wdone(wbuf, par):
        wfree[id(wbuf[par])] = (sT, sT.n)

    def emit_attn_a():
        stk = ExitStack()
        state = kb.sb("state", [128, NH * HD], F32, stack=stk)
        state_bf = kb.sb("state_bf", [128, NH * HD], BF16, stack=stk)
        dt_pre = kb.sb("dt_pre", [64, T], F32, stack=stk)
        expT = kb.sb("expT", [128, NCHUNK, 2, 64], F32, stack=stk)
        endtT = kb.sb("endtT", [128, NCHUNK, 2, 64], F32, stack=stk)
        eLbc = kb.sb("eLbc", [128, NCHUNK, 64], F32, stack=stk)
        acsLbc = kb.sb("acsLbc", [128, NCHUNK, 64], F32, stack=stk)
        kb.op("p", lambda E: E.memset(state[:, :], 0.0))
        kb.op("p", lambda E: E.memset(state_bf[:, :], 0.0))
        kb.mile("p", sP)
        stk1 = ExitStack()
        wbuf = [kb.sb(f"wa{i}", [128, KCH, 128], BF16, stack=stk1)
                for i in range(2)]
        xcbuf = kb.sb("xcbuf", [128, KCH, SBW], BF16, stack=stk1)
        strA = [kb.sb(f"strA{i}", [128, SBW], BF16, stack=stk1)
                for i in range(2)]
        zt1 = kb.sb("zt1", [128, SBW], F32, stack=stk1)
        xbchalo = kb.sb("xbchalo", [128, 34, HALO], BF16, stack=stk1)

        def evac_zpath(m, ps, bi, c0, n):
            t0 = c0 - HALO
            par = m % 2
            kb.wait("a", sT)
            war_wait("a", strA[par])
            kb.op("a", lambda E, ps=ps, par=par, n=n: E.activation(
                strA[par][:, :n], ps[:128, :n], AF.Silu))
            kb.mile("a", sA)
            B.rel(bi, sA)
            kb.wait("y", sA)
            kb.dma("y", silz_v[:, m, t0:t0 + n], strA[par][:, :n], sSS[par])
            war_set(strA[par], sSS[par])

        def evac_xbc(m, ps, bi, c0, n):
            par = m % 2
            kb.wait("a", sT)
            war_wait("a", strA[par])
            kb.op("a", lambda E, ps=ps, n=n, par=par: E.activation(
                strA[par][:, :n], ps[:128, :n], AF.Copy))
            kb.mile("a", sA)
            B.rel(bi, sA)
            kb.wait("y", sA)
            kb.dma("y", xbc_v[:, m - 32, c0:c0 + n], strA[par][:, :n],
                   sSS[par])
            war_set(strA[par], sSS[par])

        def evac_xbc_halo(m, ps, bi):
            kb.wait("a", sT)
            kb.op("a", lambda E, ps=ps, m=m: E.activation(
                xbchalo[:, m - 32, :], ps[:128, :HALO], AF.Copy))
            kb.mile("a", sA)
            B.rel(bi, sA)

        def evac_dt(m, ps, bi, c0, n):
            t0 = c0 - HALO
            kb.wait("a", sT)
            kb.op("a", lambda E, ps=ps, n=n, t0=t0: E.activation(
                dt_pre[:, t0:t0 + n], ps[0:64, :n], AF.Copy))
            kb.mile("a", sA)
            B.rel(bi, sA)

        def passA(c0, n, mlist, outfn):
            kb.wait("t", sV)
            par = 0
            wload(WinT, mlist[0] * 128, 64 if mlist[0] == 66 else 128,
                  wbuf, par)
            for mi, m in enumerate(mlist):
                if mi + 1 < len(mlist):
                    nm = mlist[mi + 1]
                    wload(WinT, nm * 128, 64 if nm == 66 else 128,
                          wbuf, 1 - par)
                mw = 64 if m == 66 else 128
                bi = mi % 2
                ps = B.acq(bi)
                wwait("t", wbuf, par)
                for k in range(KCH):
                    kb.op("t", lambda E, k=k, ps=ps, par=par, mw=mw, n=n:
                          E.matmul(ps[:mw, :n], wbuf[par][:, k, :mw],
                                   xcbuf[:, k, :n],
                                   start=(k == 0), stop=(k == KCH - 1)))
                kb.mile("t", sT)
                wdone(wbuf, par)
                outfn(m, ps, bi)
                par = 1 - par

        emit_ln(0, HALO, xcbuf)
        emit_xc(0, HALO, xcbuf)
        passA(0, HALO, list(range(32, 66)),
              lambda m, ps, bi: evac_xbc_halo(m, ps, bi))
        kb.wait("y", sA)
        kb.dma("y", xbc_v[:, :, 0:HALO], xbchalo.ap(), sST)

        for sbi in range(2):
            c0 = HALO + sbi * SBW
            emit_ln(c0, SBW, xcbuf)
            emit_xc(c0, SBW, xcbuf)
            passA(c0, SBW, list(range(0, 67)),
                  lambda m, ps, bi, c0=c0: (
                      evac_zpath(m, ps, bi, c0, SBW) if m < 32 else
                      (evac_dt(m, ps, bi, c0, SBW) if m == 66 else
                       evac_xbc(m, ps, bi, c0, SBW))))

        tap("dtpre", lambda: dt_pre.ap(), [64, T], F32,
            [(sV, sV.n), (sA, sA.n)])
        tap("xbcscr", lambda: xbc_scr, [34 * 128, TF], BF16,
            [(sST, sST.n)])
        tap("silzscr", lambda: silz_scr, [32 * 128, T], BF16,
            [(sST, sST.n)])
        for _e in ("t", "v", "a", "p"):
            kb.mile(_e, sXC)
        for _e in ("t", "v", "a", "p", "y"):
            kb.wait(_e, sXC)
            kb.wait(_e, sST)
            kb.wait(_e, sSS[0])
            kb.wait(_e, sSS[1])
        stk1.close()
        stk2 = ExitStack()
        y32w = kb.sb("y32w", [128, 32, CHUNK], BF16, stack=stk2)
        co = kb.sb("co", [128, 34, CHUNK], BF16, stack=stk2)
        xbcin = [kb.sb(f"xbcin{i}", [128, 10, CHUNK + HALO], BF16, stack=stk2)
                 for i in range(2)]
        zstr = [kb.sb(f"zstr{i}", [128, CHUNK], BF16, stack=stk2)
                for i in range(2)]
        cvA = kb.sb("cvA", [128, 5, CHUNK], F32, stack=stk2)
        cvB = kb.sb("cvB", [128, 5, CHUNK], F32, stack=stk2)
        Ssb = kb.sb("Ssb", [128, 2, CHUNK], BF16, stack=stk2)
        SmaskT = kb.sb("SmaskT", [128, 2, CHUNK], BF16, stack=stk2)
        btok = kb.sb("btok", [128, 2, 128], BF16, stack=stk2)
        xT = kb.sb("xT", [128, 2, 128], BF16, stack=stk2)
        xd = kb.sb("xd", [128, 2, 128], BF16, stack=stk2)
        u_bf = kb.sb("u_bf", [128, 2, 128], BF16, stack=stk2)
        dt_c = kb.sb("dt_c", [64, CHUNK], F32, stack=stk2)
        dta = kb.sb("dta", [64, CHUNK], F32, stack=stk2)
        dtb = kb.sb("dtb", [64, CHUNK], F32, stack=stk2)
        dtv = kb.sb("dtv", [64, CHUNK], F32, stack=stk2)
        acs = kb.sb("acs", [64, CHUNK], F32, stack=stk2)
        acsT = kb.sb("acsT", [128, 2, 64], F32, stack=stk2)
        dtvT = kb.sb("dtvT", [128, 2, 64], F32, stack=stk2)
        negT = kb.sb("negT", [128, 2, 64], F32, stack=stk2)
        eLrow = kb.sb("eLrow", [1, 64], F32, stack=stk2)
        arow = kb.sb("arow", [1, CHUNK], F32, stack=stk2)
        arow_hi = kb.sb("arow_hi", [1, CHUNK], F32, stack=stk2)
        Larg = kb.sb("Larg", [128, CHUNK], F32, stack=stk2)
        SL = kb.sb("SL", [128, 2, CHUNK], BF16, stack=stk2)
        uw = kb.sb("uw", [128, 2, 128], BF16, stack=stk2)
        xdY = kb.sb("xdY", [128, 2, 128], BF16, stack=stk2)
        onesf_row = kb.sb("onesf_row", [1, 128], F32, stack=stk2)
        xD = kb.sb("xD", [128, CHUNK], F32, stack=stk2)
        yt = kb.sb("yt", [128, CHUNK], F32, stack=stk2)
        y2s = kb.sb("y2s", [128, CHUNK], BF16, stack=stk2)
        stw = kb.sb("stw", [128, 128], F32, stack=stk2)
        rst2_bfr = kb.sb("rst2_bfr", [1, CHUNK], BF16, stack=stk2)
        rst2_bc = kb.sb("rst2_bc", [128, CHUNK], F32, stack=stk2)
        wbfull = [kb.sb(f"wafull{i}", [128, 32, 128], BF16, stack=stk2)
                  for i in range(2)]
        kb.wait("p", sV, sV.n)
        kb.wait("p", sA, sA.n)

        def rx_state(E):
            with E.register("mh") as mh, E.register("fv") as fv, \
                 E.register("nr") as nr, E.register("pc") as pc, \
                 E.register("nv") as nv:
                E.reg_load(mh, flags_sb[0:1, 0:1])
                E.reg_load(nv, flags_sb[0:1, 1:2])
                with E.If_eq(mh, 1):
                    E.reg_mov(pc, 0)

                    def poll():
                        E.dma_start(out=flagrx[:1, :],
                                    in_=sh_flag[0:1, :]).then_inc(sPL.h, 16)
                        E.reg_add(pc, pc, 16)
                        E.wait_ge(sPL.h, pc)
                        E.reg_load(fv, flagrx[0:1, 0:1])
                        E.reg_sub(nr, fv, nv)

                    poll()
                    with E.While(nr):
                        poll()
                    E.dma_start(out=state[:, :],
                                in_=sh_state[0:128, :]).then_inc(sPL.h, 16)
                    E.reg_add(pc, pc, 16)
                    E.wait_ge(sPL.h, pc)
            E.tensor_copy(state_bf[:, :], state[:, :])
            return E.drain()

        def rx_state_sim(E):
            E.tensor_copy(state_bf[:, :], state[:, :])
            return E.drain()

        kb.op("p", rx_state_sim if sim_local else rx_state, inc=sEX)
        for e in ("t", "v"):
            kb.wait(e, sEX)

        wbufL = wbuf
        sbf_ready = {}
        for c in range(NCHUNK):
            tok0 = c * CHUNK
            kb.wait("a", sT, sT.n)
            kb.wait("a", sV, sV.n)
            kb.wait("v", sT, sT.n)
            kb.wait("v", sA, sA.n)
            kb.wait("t", sA, sA.n)
            kb.wait("p", sT, sT.n)
            kb.wait("p", sV, sV.n)
            kb.wait("p", sA, sA.n)
            groups = [(0, 10), (10, 10), (20, 10), (30, 4)]

            def issue_group(gi):
                g0, gn = groups[gi]
                par = gi % 2
                war_wait("y", xbcin[par])
                kb.dma("y", xbcin[par][:, :gn, :],
                       xbc_v[:, g0:g0 + gn, tok0:tok0 + CHUNK + HALO],
                       sXB[par])
                return (g0, gn, par, sXB[par].n)

            gq = [issue_group(0), issue_group(1)]
            for gi in range(len(groups)):
                (g0, gn, par, ldv) = gq[gi]
                if gi + 2 < len(groups):
                    pass
                kb.wait("p", sXB[par], ldv)
                for sub in range(0, gn, 5):
                    sn = min(5, gn - sub)
                    kb.wait("p", sA, sA.n)

                    def wbc(col):
                        return convw_sb[:, col:col + 1].broadcast_to(
                            [128, CHUNK])

                    for j in range(sn):
                        kb.op("p", lambda E, par=par, sub=sub, j=j, g0=g0:
                              E.tensor_mul(
                                  cvA[:, j, :],
                                  xbcin[par][:, sub + j, 0:CHUNK],
                                  wbc((g0 + sub + j) * 4)))
                    for i in range(1, 4):
                        kb.wait("p", sP, sP.n)
                        for j in range(sn):
                            kb.op("p", lambda E, par=par, sub=sub, j=j, g0=g0,
                                  i=i: E.tensor_mul(
                                      cvB[:, j, :],
                                      xbcin[par][:, sub + j, i:i + CHUNK],
                                      wbc((g0 + sub + j) * 4 + i)))
                        kb.wait("p", sP, sP.n)
                        for j in range(sn):
                            kb.op("p", lambda E, j=j: E.tensor_add(
                                cvA[:, j, :], cvA[:, j, :], cvB[:, j, :]))
                    kb.mile("p", sP)
                    kb.wait("a", sP)
                    for j in range(sn):
                        kb.op("a", lambda E, g0=g0, sub=sub, j=j: E.activation(
                            co[:, g0 + sub + j, :], cvA[:, j, :], AF.Silu,
                            bias=convb_sb[:, g0 + sub + j:g0 + sub + j + 1]))
                    kb.mile("a", sA)
                war_set(xbcin[par], sP)
                if gi + 2 < len(groups):
                    gq.append(issue_group(gi + 2))
            co_ready = sA.n

            kb.wait("a", sV, sV.n)
            kb.wait("a", sA, sA.n)
            kb.op("a", lambda E, tok0=tok0: E.activation(
                dta[:, :], dt_pre[0:64, tok0:tok0 + CHUNK], AF.Abs,
                bias=dtbias_sb[:, 0:1]))
            kb.op("a", lambda E, tok0=tok0: E.activation(
                dt_c[:, :], dt_pre[0:64, tok0:tok0 + CHUNK], AF.Identity,
                bias=dtbias_sb[:, 0:1]))
            kb.mile("a", sA)
            kb.wait("a", sA)
            kb.op("a", lambda E: E.activation(dtb[:, :], dta[:, :], AF.Exp,
                                              scale=-1.0))
            kb.mile("a", sA)
            kb.wait("a", sA)
            kb.op("a", lambda E: E.activation(dtb[:, :], dtb[:, :], AF.Ln,
                                              bias=1.0))
            kb.mile("a", sA)
            kb.wait("v", sA)
            kb.op("v", lambda E: E.tensor_scalar(dta[:, :], dt_c[:, :], 0.0,
                                                 None, ALU.max))
            kb.mile("v", sV)
            kb.wait("v", sV)
            kb.op("v", lambda E: E.tensor_add(dtv[:, :], dta[:, :], dtb[:, :]))
            kb.mile("v", sV)
            kb.wait("v", sV)
            kb.op("v", lambda E: E.tensor_scalar(dta[:, :], dtv[:, :],
                                                 Acol[:, 0:1], None, ALU.mult))
            kb.mile("v", sV)
            kb.wait("v", sV)
            kb.op("v", lambda E: E.tensor_tensor_scan(
                acs[:, :], dta[:, :], zeros64[:, :], 0.0, ALU.add, ALU.add))
            kb.mile("v", sV)
            pm = B.acq(5)
            kb.wait("t", sV)
            for lt in range(2):
                kb.op("t", lambda E, lt=lt, pm=pm: E.transpose(
                    pm[:, lt * 64:lt * 64 + 64],
                    acs[0:64, lt * 128:lt * 128 + 128], identf[0:64, 0:64]))
                kb.op("t", lambda E, lt=lt, pm=pm: E.transpose(
                    pm[:, 128 + lt * 64:128 + lt * 64 + 64],
                    dtv[0:64, lt * 128:lt * 128 + 128], identf[0:64, 0:64]))
            kb.mile("t", sT)
            kb.wait("v", sT)
            for lt in range(2):
                kb.op("v", lambda E, lt=lt, pm=pm: E.tensor_copy(
                    acsT[:, lt, :], pm[:, lt * 64:lt * 64 + 64]))
                kb.op("v", lambda E, lt=lt, pm=pm: E.tensor_copy(
                    dtvT[:, lt, :], pm[:, 128 + lt * 64:128 + lt * 64 + 64]))
            kb.mile("v", sV)
            B.rel(5, sV)
            kb.wait("a", sV)
            for lt in range(2):
                kb.op("a", lambda E, lt=lt, c=c: E.activation(
                    expT[:, c, lt, :], acsT[:, lt, :], AF.Exp))
            kb.mile("a", sA)
            pm = B.acq(5)
            kb.wait("t", sV)
            kb.op("t", lambda E, pm=pm: E.transpose(
                pm[0:1, 0:64], acs[0:64, 255:256], identf[0:64, 0:64]))
            kb.mile("t", sT)
            kb.wait("a", sT)
            kb.op("a", lambda E, pm=pm: E.activation(
                eLrow[:1, :], pm[0:1, 0:64], AF.Copy))
            kb.mile("a", sA)
            kb.wait("p", sA)
            kb.op("p", lambda E: E.memset(onesf_row[:1, :], 1.0))
            kb.mile("p", sP)
            kb.wait("t", sA)
            kb.wait("t", sP)
            kb.op("t", lambda E, pm=pm: E.matmul(
                pm[:, 64:128], onesf_row[:1, :], eLrow[:1, :],
                start=True, stop=True))
            kb.mile("t", sT)
            kb.wait("v", sT)
            kb.op("v", lambda E, pm=pm, c=c: E.tensor_copy(
                acsLbc[:, c, :], pm[:, 64:128]))
            kb.mile("v", sV)
            B.rel(5, sV)
            kb.wait("a", sV)
            kb.op("a", lambda E, c=c: E.activation(
                eLbc[:, c, :], acsLbc[:, c, :], AF.Exp))
            kb.mile("a", sA)
            kb.wait("v", sA)
            for lt in range(2):
                kb.op("v", lambda E, lt=lt, c=c: E.tensor_sub(
                    negT[:, lt, :], acsLbc[:, c, :], acsT[:, lt, :]))
            kb.mile("v", sV)
            kb.wait("a", sV)
            for lt in range(2):
                kb.op("a", lambda E, lt=lt: E.activation(
                    negT[:, lt, :], negT[:, lt, :], AF.Exp))
            kb.mile("a", sA)
            kb.wait("v", sA)
            for lt in range(2):
                kb.op("v", lambda E, lt=lt, c=c: E.tensor_mul(
                    endtT[:, c, lt, :], negT[:, lt, :], dtvT[:, lt, :]))
            kb.mile("v", sV)

            ps_s = B.acq(5)
            kb.wait("t", sA, co_ready)
            for lt in range(2):
                kb.op("t", lambda E, lt=lt, ps=ps_s: E.matmul(
                    ps[:, lt * 256:lt * 256 + 256],
                    co[:, 33, lt * 128:lt * 128 + 128], co[:, 32, :],
                    start=True, stop=True))
            kb.mile("t", sT)
            kb.wait("a", sT)
            for lt in range(2):
                kb.op("a", lambda E, lt=lt, ps=ps_s: E.activation(
                    Ssb[:, lt, :], ps[:, lt * 256:lt * 256 + 256], AF.Copy))
            kb.mile("a", sA)
            B.rel(5, sA)
            ps_t = B.acq(5)
            pt_bf = ps_t.ap().bitcast(BF16)
            kb.wait("t", sA)
            for st in range(2):
                for lt in range(2):
                    kb.op("t", lambda E, st=st, lt=lt, pt=pt_bf: E.transpose(
                        pt[:, st * 512 + lt * 128:st * 512 + lt * 128 + 128],
                        Ssb[:, lt, st * 128:st * 128 + 128], ident[:, :]))
            kb.mile("t", sT)
            kb.wait("v", sT)
            kb.wait("v", sP, sP.n)
            for st in range(2):
                kb.op("v", lambda E, st=st, pt=pt_bf: E.tensor_copy(
                    SmaskT[:, st, :], pt[:, st * 512:st * 512 + 256]))
            kb.mile("v", sV)
            B.rel(5, sV)
            kb.wait("p", sV)
            for st in range(2):
                kb.op("p", lambda E, st=st: E.affine_select(
                    SmaskT[:, st, :], SmaskT[:, st, :], [[1, CHUNK]],
                    ALU.is_ge, 0.0, base=-st * 128, channel_multiplier=-1))
            kb.mile("p", sP)
            ps_b = B.acq(5)
            pb_bf = ps_b.ap().bitcast(BF16)
            kb.wait("t", sP)
            for lt in range(2):
                kb.op("t", lambda E, lt=lt, pb=pb_bf: E.transpose(
                    pb[:, lt * 128:lt * 128 + 128],
                    co[:, 32, lt * 128:lt * 128 + 128], ident[:, :]))
            kb.mile("t", sT)
            kb.wait("v", sT)
            for lt in range(2):
                kb.op("v", lambda E, lt=lt, pb=pb_bf: E.tensor_copy(
                    btok[:, lt, :], pb[:, lt * 128:lt * 128 + 128]))
            kb.mile("v", sV)
            B.rel(5, sV)
            pre_v = sV.n

            tap(f"co{c}", lambda: co.ap(), [128, 34, CHUNK], BF16,
                [(sA, co_ready)])
            tap(f"acs{c}", lambda: acs.ap(), [64, CHUNK], F32, [(sV, sV.n)])

            kb.dma("y", zstr[0][:, :], silz_v[:, 0, tok0:tok0 + CHUNK],
                   sZS[0])
            zq = [(sZS[0], sZS[0].n)]

            for hp in range(32):
                h0 = 2 * hp
                if hp + 1 < 32:
                    par = (hp + 1) % 2
                    war_wait("y", zstr[par])
                    kb.dma("y", zstr[par][:, :],
                           silz_v[:, hp + 1, tok0:tok0 + CHUNK], sZS[par])
                    zq.append((sZS[par], sZS[par].n))
                ps_x = B.acq(6)
                px_bf = ps_x.ap().bitcast(BF16)
                kb.wait("t", sA, co_ready)
                for lt in range(2):
                    kb.op("t", lambda E, lt=lt, hp=hp, px=px_bf: E.transpose(
                        px[:, lt * 128:lt * 128 + 128],
                        co[:, hp, lt * 128:lt * 128 + 128], ident[:, :]))
                kb.mile("t", sT)
                kb.wait("v", sT)
                kb.wait("v", sP, sP.n)
                for lt in range(2):
                    kb.op("v", lambda E, lt=lt, px=px_bf: E.tensor_copy(
                        xT[:, lt, :], px[:, lt * 128:lt * 128 + 128]))
                kb.mile("v", sV)
                B.rel(6, sV)
                kb.wait("p", sV)
                kb.wait("p", sT, sT.n)
                for lt in range(2):
                    kb.op("p", lambda E, lt=lt, hp=hp, c=c, h0=h0:
                          E.tensor_tensor(
                              xdY[:, lt, :].rearrange("p (h q) -> p h q", h=2),
                              xT[:, lt, :].rearrange("p (h q) -> p h q", h=2),
                              dtvT[:, lt, h0:h0 + 2].unsqueeze(2)
                              .broadcast_to([128, 2, 64]),
                              ALU.mult))
                    kb.op("p", lambda E, lt=lt, hp=hp, c=c, h0=h0:
                          E.tensor_tensor(
                              xd[:, lt, :].rearrange("p (h q) -> p h q", h=2),
                              xT[:, lt, :].rearrange("p (h q) -> p h q", h=2),
                              endtT[:, c, lt, h0:h0 + 2].unsqueeze(2)
                              .broadcast_to([128, 2, 64]),
                              ALU.mult))
                kb.mile("p", sP)
                ps_y = B.acq(7)
                for hh in range(2):
                    h = h0 + hh
                    pm5 = B.acq(5)
                    kb.wait("t", sV, pre_v)
                    for lt in range(2):
                        kb.op("t", lambda E, lt=lt, h=h, pm=pm5: E.transpose(
                            pm[0:1, lt * 128:lt * 128 + 128],
                            acsT[:, lt, h:h + 1], identf[:, :]))
                    kb.mile("t", sT)
                    kb.wait("a", sT)
                    kb.op("a", lambda E, pm=pm5: E.activation(
                        arow[:1, :], pm[0:1, 0:CHUNK], AF.Copy))
                    kb.mile("a", sA)
                    kb.wait("t", sA)
                    kb.op("t", lambda E, pm=pm5: E.matmul(
                        pm[:, 256:512], onesf_row[:1, :], arow[:1, :],
                        start=True, stop=True))
                    kb.mile("t", sT)
                    kb.wait("v", sT)
                    war_wait("v", Larg)
                    for st in range(2):
                        kb.op("v", lambda E, st=st, h=h, pm=pm5:
                              E.tensor_scalar(
                                  Larg[:, :], pm[:, 256:512],
                                  acsT[:, st, h:h + 1], 0.0,
                                  ALU.subtract, ALU.min))
                        kb.mile("v", sV)
                        kb.wait("a", sV)
                        kb.op("a", lambda E, st=st: E.activation(
                            SL[:, st, :], Larg[:, :], AF.Exp))
                        kb.mile("a", sA)
                        kb.wait("v", sA)
                    war_set(Larg, sA, sA.n)
                    B.rel(5, sV)
                    kb.wait("p", sA)
                    kb.wait("p", sT, sT.n)
                    for st in range(2):
                        kb.op("p", lambda E, st=st: E.tensor_mul(
                            SL[:, st, :], SL[:, st, :], SmaskT[:, st, :]))
                    kb.mile("p", sP)
                    kb.wait("t", sP)
                    for lt in range(2):
                        for st in range(2):
                            kb.op("t", lambda E, lt=lt, st=st, hh=hh,
                                  ps=ps_y: E.matmul(
                                      ps[:, lt * 128 + hh * 64:
                                         lt * 128 + hh * 64 + 64],
                                      SL[:, st, lt * 128:lt * 128 + 128],
                                      xdY[:, st, hh * 64:hh * 64 + 64],
                                      start=(st == 0), stop=(st == 1)))
                    kb.mile("t", sT)
                if c > 0:
                    kb.wait("t", sV, sbf_ready[hp])
                for lt in range(2):
                    kb.op("t", lambda E, lt=lt, hp=hp, ps=ps_y: E.matmul(
                        ps[:, 256 + lt * 128:256 + lt * 128 + 128],
                        co[:, 33, lt * 128:lt * 128 + 128],
                        state_bf[:, hp * 128:hp * 128 + 128],
                        start=True, stop=True))
                kb.mile("t", sT)
                kb.wait("v", sT)
                for lt in range(2):
                    for hh in range(2):
                        kb.op("v", lambda E, lt=lt, hh=hh, c=c, h0=h0,
                              ps=ps_y: E.tensor_scalar(
                                  uw[:, lt, hh * 64:hh * 64 + 64],
                                  ps[:, 256 + lt * 128 + hh * 64:
                                     256 + lt * 128 + hh * 64 + 64],
                                  expT[:, c, lt, h0 + hh:h0 + hh + 1],
                                  None, ALU.mult))
                kb.mile("v", sV)
                kb.wait("v", sV)
                for lt in range(2):
                    kb.op("v", lambda E, lt=lt, ps=ps_y: E.tensor_add(
                        u_bf[:, lt, :], ps[:, lt * 128:lt * 128 + 128],
                        uw[:, lt, :]))
                kb.mile("v", sV)
                B.rel(7, sV)
                ps_c = B.acq(3)
                kb.wait("t", sV, pre_v)
                for lt in range(2):
                    kb.op("t", lambda E, lt=lt, ps=ps_c: E.matmul(
                        ps[:, 0:128], btok[:, lt, :], xd[:, lt, :],
                        start=(lt == 0), stop=(lt == 1)))
                kb.mile("t", sT)
                kb.wait("v", sT)
                for hh in range(2):
                    kb.op("v", lambda E, hh=hh, hp=hp, c=c, h0=h0:
                          E.tensor_scalar(
                              stw[:, hh * 64:hh * 64 + 64],
                              state[:, hp * 128 + hh * 64:
                                    hp * 128 + hh * 64 + 64],
                              eLbc[:, c, h0 + hh:h0 + hh + 1], None,
                              ALU.mult))
                kb.mile("v", sV)
                kb.wait("v", sV)
                kb.op("v", lambda E, hp=hp, ps=ps_c: E.tensor_add(
                    state[:, hp * 128:hp * 128 + 128], stw[:, :],
                    ps[:, 0:128]))
                kb.mile("v", sV)
                B.rel(3, sV)
                kb.wait("v", sV)
                kb.op("v", lambda E, hp=hp: E.tensor_copy(
                    state_bf[:, hp * 128:hp * 128 + 128],
                    state[:, hp * 128:hp * 128 + 128]))
                kb.mile("v", sV)
                sbf_ready[hp] = sV.n
                ps_f = B.acq(4)
                pf_bf = ps_f.ap().bitcast(BF16)
                kb.wait("t", sV)
                for lt in range(2):
                    kb.op("t", lambda E, lt=lt, pf=pf_bf: E.transpose(
                        pf[:, lt * 128:lt * 128 + 128], u_bf[:, lt, :],
                        ident[:, :]))
                kb.mile("t", sT)
                kb.wait("v", sT)
                kb.wait("v", sA, sA.n)
                kb.wait("v", zq[hp][0], zq[hp][1])
                kb.op("v", lambda E, hp=hp: E.tensor_scalar(
                    xD[:, :], co[:, hp, :], Dbc_sb[:, hp:hp + 1], None,
                    ALU.mult))
                kb.mile("v", sV)
                kb.wait("v", sV)
                kb.op("v", lambda E, pf=pf_bf: E.tensor_add(
                    yt[:, :], pf[:, 0:256], xD[:, :]))
                kb.mile("v", sV)
                B.rel(4, sV)
                kb.wait("v", sV)
                kb.op("v", lambda E, hp=hp: E.tensor_mul(
                    yt[:, :], yt[:, :], zstr[hp % 2][:, :]))
                kb.mile("v", sV)
                war_set(zstr[hp % 2], sV)
                kb.wait("a", sV)
                kb.op("a", lambda E: E.activation(y2s[:, :], yt[:, :],
                                                  AF.Square))
                kb.mile("a", sA)
                ps_v = B.acq(2) if hp == 0 else B.b[2]
                kb.wait("t", sA)
                kb.op("t", lambda E, hp=hp, ps=ps_v: E.matmul(
                    ps[0:1, 0:CHUNK], ones_col[:, :], y2s[:, :],
                    start=(hp == 0), stop=(hp == 31)), inc=sT)
                kb.wait("v", sV)
                kb.op("v", lambda E, hp=hp: E.tensor_scalar(
                    y32w[:, hp, :], yt[:, :], normw_sb[:, hp:hp + 1],
                    None, ALU.mult))
                kb.mile("v", sV)

            tap(f"y32w{c}", lambda: y32w.ap(), [128, 32, CHUNK], BF16,
                [(sV, sV.n)])
            tap(f"state{c}", lambda: state.ap(), [128, NH * HD], F32,
                [(sV, sV.n)])

            kb.wait("a", sT)
            kb.op("a", lambda E: E.activation(
                row_a[:1, 0:CHUNK], B.b[2][0:1, 0:CHUNK], AF.Copy,
                scale=1.0 / DIN))
            kb.mile("a", sA)
            B.rel(2, sA)
            kb.wait("a", sA)
            kb.op("a", lambda E: E.activation(row_a[:1, 0:CHUNK],
                                              row_a[:1, 0:CHUNK], AF.Sqrt,
                                              bias=EPS))
            kb.mile("a", sA)
            kb.wait("v", sA)
            kb.op("v", lambda E: E.reciprocal(row_b[:1, 0:CHUNK],
                                              row_a[:1, 0:CHUNK]))
            kb.mile("v", sV)
            kb.wait("v", sV)
            kb.op("v", lambda E: E.tensor_copy(rst2_bfr[:1, :],
                                               row_b[:1, 0:CHUNK]))
            kb.mile("v", sV)
            pm2 = B.acq(2)
            kb.wait("t", sV)
            kb.op("t", lambda E, pm=pm2: E.matmul(
                pm[:, 0:CHUNK], ones_row[:1, :], rst2_bfr[:1, :],
                start=True, stop=True))
            kb.mile("t", sT)
            kb.wait("v", sT)
            kb.op("v", lambda E, pm=pm2: E.tensor_copy(rst2_bc[:, :],
                                                       pm[:, 0:CHUNK]))
            kb.mile("v", sV)
            B.rel(2, sV)
            kb.wait("v", sV)

            col0 = c * CHUNK + HALO
            parL = 0
            wload(WoaT, 0, 128, wbfull, 0)
            for mi in range(16):
                if mi + 1 < 16:
                    wload(WoaT, (mi + 1) * 128, 128, wbfull, 1 - parL)
                bi = mi % 2
                ps = B.acq(bi)
                wwait("t", wbfull, parL)
                for k in range(32):
                    kb.op("t", lambda E, k=k, ps=ps, par=parL: E.matmul(
                        ps[:, 0:CHUNK], wbfull[par][:, k, :],
                        y32w[:, k, :], start=(k == 0), stop=(k == 31)))
                kb.mile("t", sT)
                wdone(wbfull, parL)
                parL = 1 - parL
                kb.wait("v", sT)
                war_wait("v", yt)
                kb.op("v", lambda E, ps=ps: E.tensor_mul(
                    yt[:, :], ps[:, 0:CHUNK], rst2_bc[:, :]))
                kb.mile("v", sV)
                B.rel(bi, sV)
                war_set(yt, sV, sV.n + 1)
                kb.wait("v", sV)
                kb.op("v", lambda E, mi=mi, col0=col0: E.tensor_add(
                    x_fm[:, mi, col0:col0 + CHUNK],
                    x_fm[:, mi, col0:col0 + CHUNK], yt[:, :]))
                kb.mile("v", sV)

        kb.wait("p", sV, sV.n)

        def tx_state(E):
            with E.register("txmh") as mh, E.register("txpc") as pc:
                E.reg_load(mh, flags_sb[0:1, 0:1])
                E.reg_mov(pc, 0)
                with E.If_eq(mh, 0):
                    E.dma_start(out=sh_state[0:128, :],
                                in_=state[:, :]).then_inc(sPL2.h, 16)
                    E.reg_add(pc, pc, 16)
                    E.wait_ge(sPL2.h, pc)
                    E.dma_start(out=sh_flag[0:1, :],
                                in_=nonce_sb[:1, :]).then_inc(sPL2.h, 16)
                with E.Else():
                    E.dma_start(out=sh_state[128:256, :],
                                in_=state[:, :]).then_inc(sPL2.h, 16)
                    E.reg_add(pc, pc, 16)
                    E.wait_ge(sPL2.h, pc)
                    E.dma_start(out=sh_flag[1:2, :],
                                in_=nonce_sb[:1, :]).then_inc(sPL2.h, 16)
            return E.drain()

        if sim_local:
            kb.op("p", lambda E: E.drain(), inc=sEX)
        else:
            kb.op("p", tx_state, inc=sEX)
        stk2.close()
        stk.close()

    emit_attn_a()

    def emit_ffn(tag):
        stkf = ExitStack()
        wb = [kb.sb(f"wf{tag}{i}", [128, KCH, 128], BF16, stack=stkf)
              for i in range(2)]
        wbL = [kb.sb(f"wfl{tag}{i}", [128, 32, 128], BF16, stack=stkf)
               for i in range(2)]
        xcf = kb.sb(f"xcf{tag}", [128, KCH, SBW], BF16, stack=stkf)
        h1sb = kb.sb(f"h1{tag}", [128, 32, SBW], BF16, stack=stkf)
        h2sb = kb.sb(f"h2{tag}", [128, 32, SBW], BF16, stack=stkf)
        t1 = kb.sb(f"t1{tag}", [128, SBW], F32, stack=stkf)

        for sbi in range(2):
            c0 = HALO + sbi * SBW
            emit_ln(c0, SBW, xcf)
            emit_xc(c0, SBW, xcf)
            kb.wait("t", sV)
            par = 0
            wload(W1T, 0, 128, wb, 0)
            for mi in range(32):
                if mi + 1 < 32:
                    wload(W1T, (mi + 1) * 128, 128, wb, 1 - par)
                bi = mi % 2
                ps = B.acq(bi)
                wwait("t", wb, par)
                for k in range(KCH):
                    kb.op("t", lambda E, k=k, ps=ps, par=par: E.matmul(
                        ps[:, :], wb[par][:, k, :], xcf[:, k, :],
                        start=(k == 0), stop=(k == KCH - 1)))
                kb.mile("t", sT)
                wdone(wb, par)
                kb.wait("a", sT)
                kb.op("a", lambda E, mi=mi, ps=ps: E.activation(
                    h1sb[:, mi, :], ps[:, :], AF.Relu,
                    bias=b1_sb[:, mi:mi + 1]))
                kb.mile("a", sA)
                B.rel(bi, sA)
                par = 1 - par
            kb.wait("t", sA)
            par = 0
            wload(W2T, 0, 128, wbL, 0)
            for mi in range(32):
                if mi + 1 < 32:
                    wload(W2T, (mi + 1) * 128, 128, wbL, 1 - par)
                bi = mi % 2
                ps = B.acq(bi)
                wwait("t", wbL, par)
                for k in range(32):
                    kb.op("t", lambda E, k=k, ps=ps, par=par: E.matmul(
                        ps[:, :], wbL[par][:, k, :], h1sb[:, k, :],
                        start=(k == 0), stop=(k == 31)))
                kb.mile("t", sT)
                wdone(wbL, par)
                kb.wait("a", sT)
                kb.op("a", lambda E, mi=mi, ps=ps: E.activation(
                    h2sb[:, mi, :], ps[:, :], AF.Relu,
                    bias=b2_sb[:, mi:mi + 1]))
                kb.mile("a", sA)
                B.rel(bi, sA)
                par = 1 - par
            kb.wait("t", sA)
            par = 0
            wload(W3T, 0, 128, wbL, 0)
            for mi in range(16):
                if mi + 1 < 16:
                    wload(W3T, (mi + 1) * 128, 128, wbL, 1 - par)
                bi = mi % 2
                ps = B.acq(bi)
                wwait("t", wbL, par)
                for k in range(32):
                    kb.op("t", lambda E, k=k, ps=ps, par=par: E.matmul(
                        ps[:, :], wbL[par][:, k, :], h2sb[:, k, :],
                        start=(k == 0), stop=(k == 31)))
                kb.mile("t", sT)
                wdone(wbL, par)
                kb.wait("a", sT)
                war_wait("a", t1)
                kb.op("a", lambda E, mi=mi, ps=ps: E.activation(
                    t1[:, :], ps[:, :], AF.Identity,
                    bias=b3_sb[:, mi:mi + 1]))
                kb.mile("a", sA)
                B.rel(bi, sA)
                kb.wait("v", sA)
                war_set(t1, sV, sV.n + 1)
                kb.op("v", lambda E, mi=mi, c0=c0: E.tensor_add(
                    x_fm[:, mi, c0:c0 + SBW], x_fm[:, mi, c0:c0 + SBW],
                    t1[:, :]))
                kb.mile("v", sV)
                par = 1 - par
        stkf.close()

    if build_upto >= 1:
        emit_ffn("A")

    def emit_attn_b():
        stkb = ExitStack()
        wb = [kb.sb(f"wb{i}", [128, KCH, 128], BF16, stack=stkb)
              for i in range(2)]
        k_full = kb.sb("k_full", [128, KVH, 2 * T], BF16, stack=stkb)
        vtok = kb.sb("vtok", [128, KVH, 16, 128], BF16, stack=stkb)
        q_bf = kb.sb("q_bf", [128, HB, T], BF16, stack=stkb)
        qsc_bc = kb.sb("qsc_bc", [128, SBW], F32, stack=stkb)
        qrow = kb.sb("qrow", [1, SBW], F32, stack=stkb)
        qrow_bf = kb.sb("qrow_bf", [1, SBW], BF16, stack=stkb)
        negmax = kb.sb("negmax", [128, 1], F32, stack=stkb)
        rowsum = kb.sb("rowsum", [128, 1], F32, stack=stkb)
        rsum_i = kb.sb("rsum_i", [128, 1], F32, stack=stkb)
        stkv = ExitStack()
        v_fm = kb.sb("v_fm", [128, KVH, 2 * T], BF16, stack=stkv)
        xcb = kb.sb("xcb", [128, KCH, SBW], BF16, stack=stkv)
        astr = [kb.sb(f"astr{i}", [128, SBW], BF16, stack=stkv)
                for i in range(2)]

        myt0 = None

        for sbi in range(2):
            c0 = HALO + sbi * SBW
            t0 = sbi * SBW
            emit_ln(c0, SBW, xcb)
            emit_xc(c0, SBW, xcb)
            ps_cw = B.acq(3)
            for k in range(KCH):
                par = k % 2
                kb.wait("a", sV)
                war_wait("a", astr[par])
                kb.op("a", lambda E, k=k, par=par: E.activation(
                    astr[par][:, :], xcb[:, k, :], AF.Abs))
                kb.mile("a", sA)
                kb.wait("t", sA)
                kb.op("t", lambda E, k=k, par=par, ps=ps_cw: E.matmul(
                    ps[0:1, :], ones_col[:, :], astr[par][:, :],
                    start=(k == 0), stop=(k == KCH - 1)))
                kb.mile("t", sT)
                war_set(astr[par], sT)
            kb.wait("a", sT)
            kb.op("a", lambda E, ps=ps_cw: E.activation(
                qrow[:1, :], ps[0:1, :], AF.Sigmoid))
            kb.mile("a", sA)
            B.rel(3, sA)
            kb.wait("v", sA)
            kb.op("v", lambda E: E.tensor_scalar(
                qrow[:1, :], qrow[:1, :], float(ASCALE), None, ALU.mult))
            kb.mile("v", sV)
            kb.wait("v", sV)
            kb.op("v", lambda E: E.tensor_copy(qrow_bf[:1, :], qrow[:1, :]))
            kb.mile("v", sV)
            pb = B.acq(3)
            kb.wait("t", sV)
            kb.op("t", lambda E, pb=pb: E.matmul(
                pb[:, :], ones_row[:1, :], qrow_bf[:1, :],
                start=True, stop=True))
            kb.mile("t", sT)
            kb.wait("v", sT)
            kb.op("v", lambda E, pb=pb: E.tensor_copy(qsc_bc[:, :], pb[:, :]))
            kb.mile("v", sV)
            B.rel(3, sV)

            kb.wait("t", sV)
            par = 0
            wload(WkvT, 0, 128, wb, 0)
            for mi in range(8):
                if mi + 1 < 8:
                    wload(WkvT, (mi + 1) * 128, 128, wb, 1 - par)
                bi = mi % 2
                ps = B.acq(bi)
                wwait("t", wb, par)
                for k in range(KCH):
                    kb.op("t", lambda E, k=k, ps=ps, par=par: E.matmul(
                        ps[:, :], wb[par][:, k, :], xcb[:, k, :],
                        start=(k == 0), stop=(k == KCH - 1)))
                kb.mile("t", sT)
                wdone(wb, par)
                kb.wait("v", sT)

                def evkv(E, mi=mi, ps=ps, t0=t0, rn=f"kvh{sbi}_{mi}"):
                    with E.register(rn) as mh:
                        E.reg_load(mh, flags_sb[0:1, 0:1])
                        dst = k_full if mi < 4 else v_fm
                        mm = mi % 4
                        with E.If_eq(mh, 0):
                            E.tensor_copy(dst[:, mm, t0:t0 + SBW], ps[:, :])
                        with E.Else():
                            E.tensor_copy(dst[:, mm, T + t0:T + t0 + SBW],
                                          ps[:, :])
                    return E.drain()
                kb.op("v", evkv, inc=sV)
                B.rel(bi, sV)
                par = 1 - par

            kb.wait("t", sV)
            par = 0
            wload(WqT, 0, 128, wb, 0)
            for mi in range(16):
                if mi + 1 < 16:
                    wload(WqT, (mi + 1) * 128, 128, wb, 1 - par)
                bi = mi % 2
                ps = B.acq(bi)
                wwait("t", wb, par)
                for k in range(KCH):
                    kb.op("t", lambda E, k=k, ps=ps, par=par: E.matmul(
                        ps[:, :], wb[par][:, k, :], xcb[:, k, :],
                        start=(k == 0), stop=(k == KCH - 1)))
                kb.mile("t", sT)
                wdone(wb, par)
                kb.wait("v", sT)
                kb.op("v", lambda E, mi=mi, ps=ps, t0=t0: E.tensor_mul(
                    q_bf[:, mi, t0:t0 + SBW], ps[:, :], qsc_bc[:, :]))
                kb.mile("v", sV)
                B.rel(bi, sV)
                par = 1 - par

        shkv_v = sh_kv.rearrange("(o m p) t -> p o m t", p=128, o=2)
        kb.wait("p", sV, sV.n)

        def kv_xch(E):
            with E.register("xmh") as mh, E.register("xfv") as fv, \
                 E.register("xnr") as nr, E.register("xpc") as pc, \
                 E.register("xnv") as nv:
                E.reg_load(mh, flags_sb[0:1, 0:1])
                E.reg_load(nv, flags_sb[0:1, 1:2])
                E.reg_mov(pc, 0)
                with E.If_eq(mh, 0):
                    E.dma_start(out=shkv_v[:, 0, 0:4, :],
                                in_=k_full[:, :, 0:T]).then_inc(sPL3.h, 16)
                    E.dma_start(out=shkv_v[:, 0, 4:8, :],
                                in_=v_fm[:, :, 0:T]).then_inc(sPL3.h, 16)
                    E.reg_add(pc, pc, 32)
                    E.wait_ge(sPL3.h, pc)
                    E.dma_start(out=sh_flag[2:3, :],
                                in_=nonce_sb[:1, :]).then_inc(sPL3.h, 16)
                    E.reg_add(pc, pc, 16)

                    def poll0():
                        E.dma_start(out=flagrx[:1, :],
                                    in_=sh_flag[3:4, :]).then_inc(sPL3.h, 16)
                        E.reg_add(pc, pc, 16)
                        E.wait_ge(sPL3.h, pc)
                        E.reg_load(fv, flagrx[0:1, 0:1])
                        E.reg_sub(nr, fv, nv)
                    poll0()
                    with E.While(nr):
                        poll0()
                    E.dma_start(out=k_full[:, :, T:2 * T],
                                in_=shkv_v[:, 1, 0:4, :]).then_inc(sPL3.h, 16)
                    E.dma_start(out=v_fm[:, :, T:2 * T],
                                in_=shkv_v[:, 1, 4:8, :]).then_inc(sPL3.h, 16)
                    E.reg_add(pc, pc, 32)
                    E.wait_ge(sPL3.h, pc)
                with E.Else():
                    E.dma_start(out=shkv_v[:, 1, 0:4, :],
                                in_=k_full[:, :, T:2 * T]).then_inc(sPL3.h, 16)
                    E.dma_start(out=shkv_v[:, 1, 4:8, :],
                                in_=v_fm[:, :, T:2 * T]).then_inc(sPL3.h, 16)
                    E.reg_add(pc, pc, 32)
                    E.wait_ge(sPL3.h, pc)
                    E.dma_start(out=sh_flag[3:4, :],
                                in_=nonce_sb[:1, :]).then_inc(sPL3.h, 16)
                    E.reg_add(pc, pc, 16)

                    def poll1():
                        E.dma_start(out=flagrx[:1, :],
                                    in_=sh_flag[2:3, :]).then_inc(sPL3.h, 16)
                        E.reg_add(pc, pc, 16)
                        E.wait_ge(sPL3.h, pc)
                        E.reg_load(fv, flagrx[0:1, 0:1])
                        E.reg_sub(nr, fv, nv)
                    poll1()
                    with E.While(nr):
                        poll1()
                    E.dma_start(out=k_full[:, :, 0:T],
                                in_=shkv_v[:, 0, 0:4, :]).then_inc(sPL3.h, 16)
                    E.dma_start(out=v_fm[:, :, 0:T],
                                in_=shkv_v[:, 0, 4:8, :]).then_inc(sPL3.h, 16)
                    E.reg_add(pc, pc, 32)
                    E.wait_ge(sPL3.h, pc)
            return E.drain()

        def kv_xch_sim(E):
            with E.register("xmh_s") as mh:
                E.reg_load(mh, flags_sb[0:1, 0:1])
                with E.If_eq(mh, 0):
                    E.memset(k_full[:, :, T:2 * T], 0.0)
                    E.memset(v_fm[:, :, T:2 * T], 0.0)
                with E.Else():
                    E.memset(k_full[:, :, 0:T], 0.0)
                    E.memset(v_fm[:, :, 0:T], 0.0)
            return E.drain()

        kb.op("p", kv_xch_sim if sim_local else kv_xch, inc=sEX)
        for e in ("t", "v", "a"):
            kb.wait(e, sEX)

        for kv in range(KVH):
            for g in range(4):
                ps_t = B.acq(4 + (g % 2))
                pt_bf = ps_t.ap().bitcast(BF16)
                for jt4 in range(4):
                    jt = g * 4 + jt4
                    kb.op("t", lambda E, kv=kv, jt=jt, jt4=jt4, pt=pt_bf:
                          E.transpose(pt[:, jt4 * 128:jt4 * 128 + 128],
                                      v_fm[:, kv, jt * 128:jt * 128 + 128],
                                      ident[:, :]))
                kb.mile("t", sT)
                kb.wait("v", sT)
                for jt4 in range(4):
                    jt = g * 4 + jt4
                    kb.op("v", lambda E, kv=kv, jt=jt, jt4=jt4, pt=pt_bf:
                          E.tensor_copy(vtok[:, kv, jt, :],
                                        pt[:, jt4 * 128:jt4 * 128 + 128]))
                kb.mile("v", sV)
                B.rel(4 + (g % 2), sV)
        stkv.close()

        stkh = ExitStack()
        P_bf = kb.sb("P_bf", [128, 2, 2 * T], BF16, stack=stkh)
        PT_bf = kb.sb("PT_bf", [128, 16, 256], BF16, stack=stkh)
        S_sb = kb.sb("S_sb", [128, 2 * T], F32, stack=stkh)
        dbase = kb.sb("dbase", [128, 2, 2 * T], F32, stack=stkh)
        band01 = kb.sb("band01", [128, 2 * T], BF16, stack=stkh)
        aostr = kb.sb("aostr", [128, 256], BF16, stack=stkh)

        for itp in range(4):
            for sl in range(2):
                it = 2 * itp + sl

                def mk_dbase(E, it=it, sl=sl, rn=f"dmh{it}"):
                    with E.register(rn) as mh:
                        E.reg_load(mh, flags_sb[0:1, 0:1])
                        with E.If_eq(mh, 0):
                            E.iota(dbase[:, sl, :], [[1, 2 * T]],
                                   base=-(it * 128), channel_multiplier=-1,
                                   allow_small_or_imprecise_dtypes=True)
                        with E.Else():
                            E.iota(dbase[:, sl, :], [[1, 2 * T]],
                                   base=-(T + it * 128),
                                   channel_multiplier=-1,
                                   allow_small_or_imprecise_dtypes=True)
                        E.memset(band01[:, :], 1.0)
                    return E.drain()
                kb.wait("p", sA, sA.n)
                kb.wait("p", sV, sV.n)
                kb.op("p", mk_dbase, inc=sP)
                kb.wait("a", sP)
                kb.op("a", lambda E, sl=sl: E.activation(
                    dbase[:, sl, :], dbase[:, sl, :], AF.Abs))
                kb.mile("a", sA)

                def mk_band(E, it=it, rn=f"bmh{it}"):
                    with E.register(rn) as mh:
                        E.reg_load(mh, flags_sb[0:1, 0:1])
                        for half in range(2):
                            base_i = half * T + it * 128
                            with (E.If_eq(mh, 0) if half == 0 else E.Else()):
                                E.affine_select(band01[:, :], band01[:, :],
                                                [[1, 2 * T]], ALU.is_ge, 0.0,
                                                base=-base_i + 256,
                                                channel_multiplier=-1)
                                E.drain()
                                E.affine_select(band01[:, :], band01[:, :],
                                                [[-1, 2 * T]], ALU.is_ge, 0.0,
                                                base=base_i + 256,
                                                channel_multiplier=1)
                    return E.drain()
                kb.op("p", mk_band, inc=sP)
                kb.wait("v", sP)
                kb.wait("v", sA)
                kb.op("v", lambda E, sl=sl: E.scalar_tensor_tensor(
                    dbase[:, sl, :], band01[:, :], 4.0e9, dbase[:, sl, :],
                    ALU.mult, ALU.add))
                kb.mile("v", sV)
                kb.wait("v", sV)
                kb.op("v", lambda E, sl=sl: E.memset(dbase[:, sl, 0:64],
                                                     4.0e9))
                kb.mile("v", sV)
            dbase_ready = sV.n

            for h in range(HB):
                kv = h // 4
                for sl in range(2):
                    it = 2 * itp + sl
                    kb.wait("v", sA, sA.n)
                    for js in range(4):
                        bi = js % 2
                        ps = B.acq(bi)
                        kb.wait("t", sEX)
                        kb.wait("t", sV, dbase_ready)
                        kb.op("t", lambda E, h=h, it=it, js=js, kv=kv, ps=ps:
                              E.matmul(ps[:, :],
                                       q_bf[:, h, it * 128:it * 128 + 128],
                                       k_full[:, kv, js * 512:js * 512 + 512],
                                       start=True, stop=True))
                        kb.mile("t", sT)
                        kb.wait("v", sT)
                        kb.op("v", lambda E, h=h, js=js, sl=sl, ps=ps:
                              E.scalar_tensor_tensor(
                                  S_sb[:, js * 512:js * 512 + 512],
                                  dbase[:, sl, js * 512:js * 512 + 512],
                                  slopes_sb[:, h:h + 1], ps[:, :],
                                  ALU.mult, ALU.add))
                        kb.mile("v", sV)
                        B.rel(bi, sV)
                    kb.wait("v", sV)
                    kb.op("v", lambda E: E.reduce_max(
                        negmax[:, :], S_sb[:, :], axis=mybir.AxisListType.X,
                        negate=True))
                    kb.mile("v", sV)
                    kb.wait("a", sV)
                    kb.op("a", lambda E, sl=sl: E.activation(
                        P_bf[:, sl, :], S_sb[:, :], AF.Exp,
                        bias=negmax[:, 0:1], accum_out=rowsum[:, 0:1]))
                    kb.mile("a", sA)
                    kb.wait("v", sA)
                    if it == 0:
                        def fix_glob(E, sl=sl, rn=f"gmh{h}"):
                            with E.register(rn) as mh:
                                E.reg_load(mh, flags_sb[0:1, 0:1])
                                with E.If_eq(mh, 0):
                                    E.memset(P_bf[0:64, sl, :], 1.0)
                                    E.memset(rowsum[0:64, 0:1], 2048.0)
                            return E.drain()
                        kb.op("v", fix_glob, inc=sV)
                        kb.wait("v", sV)
                    kb.op("v", lambda E: E.reciprocal(rsum_i[:, :],
                                                      rowsum[:, :]))
                    kb.mile("v", sV)
                    kb.wait("v", sV)
                    kb.op("v", lambda E, sl=sl: E.tensor_scalar(
                        P_bf[:, sl, :], P_bf[:, sl, :], rsum_i[:, 0:1], None,
                        ALU.mult))
                    kb.mile("v", sV)

                for jt in range(16):
                    ps_t = B.acq(4 + (jt % 2))
                    pt_bf = ps_t.ap().bitcast(BF16)
                    kb.wait("t", sV, sV.n)
                    for i4 in range(2):
                        kb.op("t", lambda E, jt=jt, i4=i4, pt=pt_bf:
                              E.transpose(
                                  pt[:, i4 * 128:i4 * 128 + 128],
                                  P_bf[:, i4, jt * 128:jt * 128 + 128],
                                  ident[:, :]))
                    kb.mile("t", sT)
                    kb.wait("v", sT)
                    kb.op("v", lambda E, jt=jt, pt=pt_bf: E.tensor_copy(
                        PT_bf[:, jt, :], pt[:, 0:256]))
                    kb.mile("v", sV)
                    B.rel(4 + (jt % 2), sV)
                ps_av = B.acq(6)
                kb.wait("t", sV)
                for jt in range(16):
                    kb.op("t", lambda E, jt=jt, kv=kv, ps=ps_av: E.matmul(
                        ps[:, 0:256], vtok[:, kv, jt, :], PT_bf[:, jt, :],
                        start=(jt == 0), stop=(jt == 15)))
                kb.mile("t", sT)
                kb.wait("v", sT)
                war_wait("v", aostr)
                kb.op("v", lambda E, ps=ps_av: E.tensor_copy(
                    aostr[:, :], ps[:, 0:256]))
                kb.mile("v", sV)
                B.rel(6, sV)
                kb.wait("y", sV)
                kb.dma("y", ao_v[:, h, itp * 256:itp * 256 + 256],
                       aostr[:, :], sAO)
                war_set(aostr, sAO)
        stkh.close()

        tap("aoscr", lambda: ao_scr, [HB * HDB, T], BF16,
            [(sST, sST.n), (sAO, sAO.n)])
        tap("kfull", lambda: k_full.ap(), [128, KVH, 2 * T], BF16,
            [(sEX, sEX.n)])
        tap("qbf", lambda: q_bf.ap(), [128, HB, T], BF16, [(sV, sV.n)])
        stko = ExitStack()
        aosb = kb.sb("aosb", [128, HB, T], BF16, stack=stko)
        t1b = kb.sb("t1b", [128, SBW], F32, stack=stko)
        kb.wait("y", sST)
        kb.wait("y", sAO)
        kb.dma("y", aosb.ap(), ao_v, sLD)
        for sbi in range(2):
            t0 = sbi * SBW
            c0 = HALO + t0
            wload(WobT, 0, 128, wb, 0)
            for mi in range(16):
                if mi + 1 < 16:
                    wload(WobT, (mi + 1) * 128, 128, wb, 1 - (mi % 2))
                bi = mi % 2
                ps = B.acq(bi)
                wwait("t", wb, mi % 2)
                kb.wait("t", sLD)
                for k in range(HB):
                    kb.op("t", lambda E, k=k, ps=ps, par=mi % 2, t0=t0:
                          E.matmul(ps[:, :], wb[par][:, k, :],
                                   aosb[:, k, t0:t0 + SBW],
                                   start=(k == 0), stop=(k == HB - 1)))
                kb.mile("t", sT)
                wdone(wb, mi % 2)
                kb.wait("a", sT)
                war_wait("a", t1b)
                kb.op("a", lambda E, mi=mi, ps=ps: E.activation(
                    t1b[:, :], ps[:, :], AF.Identity,
                    bias=bo_sb[:, mi:mi + 1]))
                kb.mile("a", sA)
                B.rel(bi, sA)
                kb.wait("v", sA)
                war_set(t1b, sV, sV.n + 1)
                kb.op("v", lambda E, mi=mi, c0=c0: E.tensor_add(
                    x_fm[:, mi, c0:c0 + SBW], x_fm[:, mi, c0:c0 + SBW],
                    t1b[:, :]))
                kb.mile("v", sV)
        stko.close()
        stkb.close()

    if build_upto >= 2:
        emit_attn_b()
    if build_upto >= 3:
        emit_ffn("B")


    for e in ("v", "a", "p", "t"):
        kb.mile(e, sXC)
    kb.wait("y", sXC)
    kb.dma("y", out_fm.rearrange("(k p) t -> p k t", p=128),
           x_fm.ap()[:, :, HALO:], sST)
    kb.wait("y", sST)

    with nc.Block() as block:
        @block.tensor
        def _(E):
            for f in kb.cl["t"]:
                f(E)

        @block.vector
        def _(E):
            for f in kb.cl["v"]:
                f(E)

        @block.scalar
        def _(E):
            for f in kb.cl["a"]:
                f(E)

        @block.gpsimd
        def _(E):
            for f in kb.cl["p"]:
                f(E)

        @block.sync
        def _(E):
            for f in kb.cl["y"]:
                f(E)

    kb.es.close()
    return nc, tap_names



def _prep_inputs(inputs):
    x = np.ascontiguousarray(np.asarray(inputs["L0_attn_a_hidden_states"],
                                        np.float32))
    f32 = np.float32

    def c(a):
        return np.ascontiguousarray(np.asarray(a, f32))

    Dp = np.asarray(inputs["L0_attn_a_D"], f32)
    shared = {
        "WinT": c(np.asarray(inputs["L0_attn_a_in_proj_w"]).T),
        "convw": c(np.asarray(inputs["L0_attn_a_conv_w"])
                   .reshape(34, 128, 4).transpose(1, 0, 2).reshape(128, 136)),
        "convb": c(np.asarray(inputs["L0_attn_a_conv_b"]).reshape(34, 128).T),
        "dtbias": c(np.asarray(inputs["L0_attn_a_dt_bias"]).reshape(64, 1)),
        "A_log": c(np.asarray(inputs["L0_attn_a_A_log"]).reshape(64, 1)),
        "D_bc": c(Dp.repeat(64).reshape(32, 128).T),
        "normw": c(np.asarray(inputs["L0_attn_a_norm_w"]).reshape(32, 128).T),
        "WoaT": c(np.asarray(inputs["L0_attn_a_out_proj_w"]).T),
        "W1T": c(np.asarray(inputs["L0_ffn_W1"]).T),
        "W2T": c(np.asarray(inputs["L0_ffn_W2"]).T),
        "W3T": c(np.asarray(inputs["L0_ffn_W3"]).T),
        "b1c": c(np.asarray(inputs["L0_ffn_b1"]).reshape(32, 128).T),
        "b2c": c(np.asarray(inputs["L0_ffn_b2"]).reshape(32, 128).T),
        "b3c": c(np.asarray(inputs["L0_ffn_b3"]).reshape(16, 128).T),
        "WqT": c(inputs["L0_attn_b_Wq"]),
        "WkvT": c(inputs["L0_attn_b_Wkv"]),
        "WobT": c(inputs["L0_attn_b_Wo"]),
        "boc": c(np.asarray(inputs["L0_attn_b_bo"]).reshape(16, 128).T),
    }
    cp = 2.0 ** np.floor(np.log2(HB))
    base = 2.0 ** (-(2.0 ** (-(np.log2(cp) - 3.0))))
    slopes = (base ** np.arange(1, HB + 1, dtype=np.float64)).astype(f32)
    shared["slopes_n"] = c(np.broadcast_to(-slopes[None, :], (128, HB)))

    rng = np.random.default_rng()
    nonce = np.uint32(rng.integers(1, 2**32 - 1, dtype=np.uint64))

    in_maps = []
    for cid in range(8):
        b, h = cid // 2, cid % 2
        xi = x[b]
        t0 = h * T
        halo = np.zeros((HALO, D), f32) if h == 0 else xi[t0 - HALO:t0]
        xslab = np.concatenate([halo, xi[t0:t0 + T]], axis=0)
        m = dict(shared)
        m["x_fm"] = c(xslab.T)
        m["flags"] = np.array([[h, nonce, 0, 0, 0, 0, 0, 0]], np.uint32)
        m["halfsc"] = np.full((128, 1), float(h), f32)
        in_maps.append(m)
    return in_maps


def kernel(**inputs):
    in_maps = _prep_inputs(inputs)
    nc, _ = build()
    res = run_bass_kernel_spmd(nc, in_maps, core_ids=list(range(8)))
    out = np.empty((4, 2048, 2048), np.float32)
    for cid in range(8):
        b, h = cid // 2, cid % 2
        out[b, h * T:(h + 1) * T, :] = res.results[cid]["out_fm"].T
    return out
